# version 07
# baseline (speedup 1.0000x reference)
import math
from contextlib import ExitStack

import numpy as np
import concourse.bass as bass
import concourse.mybir as mybir

F32 = mybir.dt.float32
BF16 = mybir.dt.bfloat16
F16 = mybir.dt.float16
AF = mybir.ActivationFunctionType
ALU = mybir.AluOpType

D = 2048
TOWN = 2048
TCTX = 1024
TIN = TOWN + TCTX
HD = 128
INVSQ_HD = 1.0 / math.sqrt(128.0)
FFN_H = 5632


USED_SEM_NUMS = set()


class Sem:
    def __init__(self, stack, nc, name):
        self.h = stack.enter_context(nc.semaphore(name))
        self.n = 0

    def inc(self, instr, k=1):
        instr.then_inc(self.h, k)
        self.n += k
        return instr


def edge(eng, sem, instr):
    sem.inc(instr)
    eng.wait_ge(sem.h, sem.n)


class Phase:
    def __init__(self, nc, stack, block, tag):
        self.nc, self.stack, self.block, self.tag = nc, stack, block, tag

    def sem(self, name):
        s = Sem(self.stack, self.nc, f"{self.tag}_{name}")
        USED_SEM_NUMS.add(s.h.num)
        return s

    def sbuf(self, name, shape, dtype, stack=None):
        return (stack or self.stack).enter_context(
            self.nc.sbuf_tensor(f"{self.tag}_{name}", shape, dtype))

    def psum(self, name, shape, dtype):
        return self.stack.enter_context(
            self.nc.psum_tensor(f"{self.tag}_{name}", shape, dtype))


def emit_ln_xhat(ph, *, src_hbm, n_tc, dst_tiles, ident, eps, psbf, name,
                 bufstack=None, skip_transpose=False):
    vseq = ph.sem(f"{name}_vq")
    sseq = ph.sem(f"{name}_sq")
    NS = 4
    xb = ph.sbuf(f"{name}_xb", [128, NS * D], F32, bufstack)
    xh = ph.sbuf(f"{name}_xh", [128, NS * D], BF16, bufstack)
    bn6 = ph.sbuf(f"{name}_bn6", [128, NS * 24], F32, bufstack)
    mv = ph.sbuf(f"{name}_mv", [128, NS * 2], F32, bufstack)
    rstd = ph.sbuf(f"{name}_rstd", [128, NS], F32, bufstack)
    nmr = ph.sbuf(f"{name}_nmr", [128, NS], F32, bufstack)

    s_ld = [ph.sem(f"{name}_ld{j}") for j in range(NS)]
    s_bn = ph.sem(f"{name}_bn")
    s_mv = ph.sem(f"{name}_mv")
    s_rs = ph.sem(f"{name}_rs")
    s_nm = ph.sem(f"{name}_nm")
    s_ap = ph.sem(f"{name}_ap")
    s_tp = ph.sem(f"{name}_tp")
    s_ev = ph.sem(f"{name}_ev")

    @ph.block.sync
    def _(sync):
        for tc in range(n_tc):
            sl = tc % NS
            if tc >= NS:
                sync.wait_ge(s_ap.h, tc - NS + 1)
            s_ld[sl].inc(sync.dma_start(
                out=xb[:, sl * D:(sl + 1) * D],
                in_=src_hbm[tc * 128:(tc + 1) * 128, :]), 16)

    @ph.block.vector
    def _(v):
        def emit_evict_group(G):
            tc, g = G // 4, G % 4
            v.wait_ge(s_tp.h, G + 1)
            if skip_transpose:
                sl = tc % NS
                ii = None
                for j in range(4):
                    ci = g * 4 + j
                    ii = v.tensor_copy(dst_tiles(ci, tc),
                                       xh[:, sl * D + ci * 128:sl * D + (ci + 1) * 128])
                s_ev.inc(ii)
                return
            ii = None
            for j in range(4):
                ci = g * 4 + j
                ii = v.tensor_copy(dst_tiles(ci, tc),
                                   psbf[:, (G % 2) * 1024 + j * 128:
                                        (G % 2) * 1024 + (j + 1) * 128])
            s_ev.inc(ii)

        for tc in range(n_tc):
            sl = tc % NS
            if tc >= 1:
                for g in range(4):
                    emit_evict_group((tc - 1) * 4 + g)
            v.wait_ge(s_ld[sl].h, (tc // NS + 1) * 16)
            ii = None
            for j in range(4):
                ii = v.bn_stats(bn6[:, sl * 24 + j * 6: sl * 24 + (j + 1) * 6],
                                xb[:, sl * D + j * 512: sl * D + (j + 1) * 512])
            s_bn.inc(ii)
            v.wait_ge(s_bn.h, tc + 1)
            s_mv.inc(v.bn_aggr(mv[:, sl * 2:(sl + 1) * 2],
                               bn6[:, sl * 24:(sl + 1) * 24]))
            v.wait_ge(s_rs.h, tc + 1)
            v.wait_ge(s_mv.h, tc + 1)
            s_nm.inc(v.tensor_scalar(nmr[:, sl:sl + 1], mv[:, sl * 2:sl * 2 + 1],
                                     rstd[:, sl:sl + 1], -1.0, ALU.mult, ALU.mult))
        for g in range(4):
            emit_evict_group((n_tc - 1) * 4 + g)

    @ph.block.scalar
    def _(s):
        for tc in range(n_tc):
            sl = tc % NS
            s.wait_ge(s_mv.h, tc + 1)
            edge(s, sseq, s.activation(rstd[:, sl:sl + 1], mv[:, sl * 2 + 1:sl * 2 + 2],
                                       AF.Ln, bias=eps[:, :]))
            s_rs.inc(s.activation(rstd[:, sl:sl + 1], rstd[:, sl:sl + 1],
                                  AF.Exp, scale=-0.5))
            s.wait_ge(s_rs.h, tc + 1)
            s.wait_ge(s_nm.h, tc + 1)
            if tc >= NS:
                s.wait_ge(s_tp.h, 4 * (tc - NS + 1))
            s_ap.inc(s.activation(xh[:, sl * D:(sl + 1) * D],
                                  xb[:, sl * D:(sl + 1) * D],
                                  AF.Identity, scale=rstd[:, sl:sl + 1],
                                  bias=nmr[:, sl:sl + 1]))

    @ph.block.tensor
    def _(t):
        for tc in range(n_tc):
            sl = tc % NS
            t.wait_ge(s_ap.h, tc + 1)
            for g in range(4):
                G = tc * 4 + g
                if G >= 2:
                    t.wait_ge(s_ev.h, G - 1)
                if skip_transpose:
                    s_tp.inc(t.nop())
                    continue
                ii = None
                for j in range(4):
                    ci = g * 4 + j
                    ii = t.transpose(psbf[:, (G % 2) * 1024 + j * 128:
                                          (G % 2) * 1024 + (j + 1) * 128],
                                     xh[:, sl * D + ci * 128:sl * D + (ci + 1) * 128],
                                     ident[:, :])
                s_tp.inc(ii)

    return {"ev": s_ev, "ap": s_ap}


def emit_rope(v, pk, cosap, sinap, scr, out_bf, W, vseq):
    sa = scr[:, 0:W]
    sb = scr[:, W:2 * W]
    vseq.inc(v.tensor_tensor(sa[0:64, :], pk[64:128, :], sinap[0:64, :], ALU.mult))
    vseq.inc(v.tensor_tensor(sa[64:128, :], pk[0:64, :], sinap[64:128, :], ALU.mult))
    vseq.inc(v.tensor_tensor(sb[0:64, :], pk[0:64, :], cosap[0:64, :], ALU.mult))
    vseq.inc(v.tensor_tensor(sb[64:128, :], pk[64:128, :], cosap[64:128, :], ALU.mult))
    v.wait_ge(vseq.h, vseq.n)
    vseq.inc(v.tensor_tensor(out_bf[0:64, :], sb[0:64, :], sa[0:64, :], ALU.subtract))
    vseq.inc(v.tensor_tensor(out_bf[64:128, :], sb[64:128, :], sa[64:128, :], ALU.add))
    v.wait_ge(vseq.h, vseq.n)


def build_phase_a(nc, io, tap=None, stages=4):
    with ExitStack() as stack:
        block = stack.enter_context(nc.Block())
        ph = Phase(nc, stack, block, "A")

        ident = ph.sbuf("ident", [128, 128], BF16)
        eps = ph.sbuf("eps", [128, 1], F32)
        ones_bf = ph.sbuf("onesb", [128, 1], BF16)
        ones_row = ph.sbuf("onesr", [1, 128], BF16)
        psbf = ph.psum("psbf", [128, 2048], BF16)
        PS = ph.psum("ps", [128, 3072], F32)
        st12 = ExitStack()
        st01 = ExitStack()
        stLN = ExitStack()
        kbar = ph.sbuf("kbar", [128, 4 * TIN], BF16, st12)
        xhat_ctx = ph.sbuf("xhc", [128, 16 * 1024], BF16, st01)
        xhat_own = ph.sbuf("xho", [128, 16 * 2048], BF16, st01)

        s_pre = ph.sem("pre")
        vseq = ph.sem("vseq")
        sseq = ph.sem("sseq")

        @ph.block.sync
        def _(sync):
            s_pre.inc(sync.dma_start(out=ident[:, :], in_=io["ident"][:, :]), 16)

        @ph.block.vector
        def _(v):
            v.memset(eps[:, :], 1e-5)
            v.memset(ones_bf[:, :], 1.0)
            v.memset(ones_row[:, :], 1.0)
            v.wait_ge(s_pre.h, 16)

        def dst(ci, tc):
            if tc < 8:
                return xhat_ctx[:, ci * 1024 + tc * 128: ci * 1024 + (tc + 1) * 128]
            u = tc - 8
            return xhat_own[:, ci * 2048 + u * 128: ci * 2048 + (u + 1) * 128]

        lns = emit_ln_xhat(ph, src_hbm=io["xin"], n_tc=24, dst_tiles=dst,
                           ident=ident, eps=eps, psbf=psbf, name="ln0",
                           bufstack=stLN)
        s_ev = lns["ev"]
        EV_ALL = s_ev.n
        stLN.close()

        if stages < 1:
            st01.close(); st12.close()
            return nc
        wcol = ph.sbuf("wcol", [128, 2 * 2048], BF16, st01)
        wvbf = ph.sbuf("wvbf", [128, 16 * 512], BF16, st01)
        cfull = ph.sbuf("cfull", [128, TIN], F32, st01)
        sfull = ph.sbuf("sfull", [128, TIN], F32, st01)
        rsc = ph.sbuf("rsc", [128, 2 * 1024], F32, st01)
        vev = ph.sbuf("vev", [128, 2 * 512], BF16, st01)
        qev = ph.sbuf("qev", [128, 2 * 2048], BF16, st01)

        s_wld = [ph.sem("wld0"), ph.sem("wld1")]
        s_wv = ph.sem("wv")
        s_qld = [ph.sem("qld0"), ph.sem("qld1")]
        s_cs = ph.sem("cs")
        s_kmm = ph.sem("kmm")
        s_krp = ph.sem("krp")
        s_vmm = ph.sem("vmm")
        s_vev = ph.sem("vev")
        s_vst = [ph.sem("vst0"), ph.sem("vst1")]
        s_qmm = ph.sem("qmm")
        s_qrp = ph.sem("qrp")
        s_qst = [ph.sem("qst0"), ph.sem("qst1")]

        def xcol512(ci, t0):
            if t0 < 1024:
                return xhat_ctx[:, ci * 1024 + t0: ci * 1024 + t0 + 512]
            u = t0 - 1024
            return xhat_own[:, ci * 2048 + u: ci * 2048 + u + 512]

        @ph.block.gpsimd
        def _(g):
            g.wait_ge(s_ev.h, EV_ALL)
            for ut in range(4):
                sl = ut % 2
                if ut >= 2:
                    g.wait_ge(s_kmm.h, (ut - 2) * 3 + 3)
                s_wld[sl].inc(g.dma_start(
                    out=wcol[:, sl * 2048: sl * 2048 + 2048]
                    .rearrange("p (ci u) -> p ci u", u=128),
                    in_=io["wk"].rearrange("(ci p) u -> p ci u", p=128)
                    [:, :, ut * 128:(ut + 1) * 128]), 16)
            s_wv.inc(g.dma_start(
                out=wvbf[:, :].rearrange("p (ci u) -> p ci u", u=512),
                in_=io["wv"].rearrange("(ci p) u -> p ci u", p=128)), 16)
            for ut in range(16):
                sl = ut % 2
                if ut < 2:
                    g.wait_ge(s_kmm.h, 12)
                else:
                    g.wait_ge(s_qmm.h, (ut - 2) * 2 + 2)
                s_qld[sl].inc(g.dma_start(
                    out=wcol[:, sl * 2048: sl * 2048 + 2048]
                    .rearrange("p (ci u) -> p ci u", u=128),
                    in_=io["wq"].rearrange("(ci p) u -> p ci u", p=128)
                    [:, :, ut * 128:(ut + 1) * 128]), 16)

        @ph.block.sync
        def _(sync):
            sync.wait_ge(s_ev.h, EV_ALL)
            s_cs.inc(sync.dma_start(out=cfull[:, :], in_=io["cost"][:, :]), 16)
            s_cs.inc(sync.dma_start(out=sfull[:, :], in_=io["sint"][:, :]), 16)
            for tt in range(24):
                sync.wait_ge(s_vev.h, tt + 1)
                s_vst[tt % 2].inc(sync.dma_start(
                    out=io["vsc"][tt * 128:(tt + 1) * 128, :],
                    in_=vev[:, (tt % 2) * 512:(tt % 2 + 1) * 512]), 16)
            for ut in range(16):
                sync.wait_ge(s_qrp.h, 2 * (ut + 1))
                s_qst[ut % 2].inc(sync.dma_start(
                    out=io["qsc"][ut * 128:(ut + 1) * 128, :],
                    in_=qev[:, (ut % 2) * 2048:(ut % 2 + 1) * 2048]), 16)

        @ph.block.tensor
        def _(t):
            t.wait_ge(s_ev.h, EV_ALL)
            for ut in range(4):
                sl = ut % 2
                t.wait_ge(s_wld[ut % 2].h, (ut // 2 + 1) * 16)
                for th in range(3):
                    G = ut * 3 + th
                    if G >= 2:
                        t.wait_ge(s_krp.h, G - 1)
                    ii = None
                    for ci in range(16):
                        for mo in range(2):
                            t0 = th * 1024 + mo * 512
                            ii = t.matmul(PS[:, (G % 2) * 1024 + mo * 512:
                                             (G % 2) * 1024 + (mo + 1) * 512],
                                          wcol[:, sl * 2048 + ci * 128:
                                               sl * 2048 + (ci + 1) * 128],
                                          xcol512(ci, t0),
                                          start=(ci == 0), stop=(ci == 15))
                    s_kmm.inc(ii)
            t.wait_ge(s_wv.h, 16)
            for tt in range(24):
                if tt >= 2:
                    t.wait_ge(s_vev.h, tt - 1)
                ii = None
                for ci in range(16):
                    ii = t.matmul(PS[:, 2048 + (tt % 2) * 512: 2048 + (tt % 2 + 1) * 512],
                                  dst(ci, tt), wvbf[:, ci * 512:(ci + 1) * 512],
                                  start=(ci == 0), stop=(ci == 15))
                s_vmm.inc(ii)
            t.wait_ge(s_krp.h, 12)
            for ut in range(16):
                sl = ut % 2
                t.wait_ge(s_qld[ut % 2].h, (ut // 2 + 1) * 16)
                for th in range(2):
                    G = ut * 2 + th
                    if G >= 2:
                        t.wait_ge(s_qrp.h, G - 1)
                    ii = None
                    for ci in range(16):
                        for mo in range(2):
                            u0 = th * 1024 + mo * 512
                            ii = t.matmul(PS[:, (G % 2) * 1024 + mo * 512:
                                             (G % 2) * 1024 + (mo + 1) * 512],
                                          wcol[:, sl * 2048 + ci * 128:
                                               sl * 2048 + (ci + 1) * 128],
                                          xhat_own[:, ci * 2048 + u0: ci * 2048 + u0 + 512],
                                          start=(ci == 0), stop=(ci == 15))
                    s_qmm.inc(ii)

        @ph.block.vector
        def _(v):
            v.wait_ge(s_cs.h, 32)
            for ut in range(4):
                for th in range(3):
                    G = ut * 3 + th
                    v.wait_ge(s_kmm.h, G + 1)
                    emit_rope(
                        v, PS[:, (G % 2) * 1024:(G % 2) * 1024 + 1024],
                        cfull[:, th * 1024:(th + 1) * 1024],
                        sfull[:, th * 1024:(th + 1) * 1024],
                        rsc,
                        kbar[:, ut * TIN + th * 1024: ut * TIN + (th + 1) * 1024],
                        1024, vseq)
                    v.sem_inc(s_krp.h, 1)
                    s_krp.n += 1
            for tt in range(24):
                v.wait_ge(s_vmm.h, tt + 1)
                if tt >= 2:
                    v.wait_ge(s_vst[tt % 2].h, (tt // 2) * 16)
                s_vev.inc(v.tensor_copy(vev[:, (tt % 2) * 512:(tt % 2 + 1) * 512],
                                        PS[:, 2048 + (tt % 2) * 512:
                                           2048 + (tt % 2 + 1) * 512]))
            for ut in range(16):
                for th in range(2):
                    G = ut * 2 + th
                    v.wait_ge(s_qmm.h, G + 1)
                    if ut >= 2 and th == 0:
                        v.wait_ge(s_qst[ut % 2].h, (ut // 2) * 16)
                    emit_rope(
                        v, PS[:, (G % 2) * 1024:(G % 2) * 1024 + 1024],
                        cfull[:, 1024 + th * 1024: 1024 + (th + 1) * 1024],
                        sfull[:, 1024 + th * 1024: 1024 + (th + 1) * 1024],
                        rsc,
                        qev[:, (ut % 2) * 2048 + th * 1024:
                            (ut % 2) * 2048 + (th + 1) * 1024],
                        1024, vseq)
                    v.sem_inc(s_qrp.h, 1)
                    s_qrp.n += 1

        if stages < 2:
            st01.close(); st12.close()
            return nc
        st01.close()
        st2 = ExitStack()
        msk = ph.sbuf("msk", [128, 2 * 12 * 512], BF16, st2)
        vwin = ph.sbuf("vwin", [128, 2 * 12 * 512], BF16, st2)
        qh = ph.sbuf("qh", [128, 2 * 512], BF16, st2)
        pbuf = ph.sbuf("pbuf", [128, 2 * 512], BF16, st2)
        pm = ph.sbuf("pm", [128, 2 * 512], BF16, st2)
        dacc = ph.sbuf("dacc", [128, 512], F16, st2)
        dsum = ph.sbuf("dsum", [1, 512], F32, st2)
        rec = ph.sbuf("rec", [1, 512], F32, st2)
        recbf = ph.sbuf("recbf", [1, 512], BF16, st2)
        rsb = ph.sbuf("rsb", [128, 512], F32, st2)
        atb = ph.sbuf("atb", [128, 2 * 512], BF16, st2)

        s_mld = [ph.sem("mld0"), ph.sem("mld1")]
        s_qhld = [ph.sem("qhld0"), ph.sem("qhld1")]
        s_smm = ph.sem("smm")
        s_exp = ph.sem("exp")
        s_msk = ph.sem("mskd")
        s_dac = ph.sem("dac")
        s_omm = ph.sem("omm")
        s_dbf = ph.sem("dbfs")
        s_dmm = ph.sem("dmm")
        s_dsm = ph.sem("dsm")
        s_rc = ph.sem("rc")
        s_rmm = ph.sem("rmm")
        s_rsb = ph.sem("rsbs")
        s_nrm = ph.sem("nrm")
        s_ast = [ph.sem("ast0"), ph.sem("ast1")]

        @ph.block.sync
        def _(sync):
            sync.wait_ge(s_qst[0].h, 8 * 16)
            sync.wait_ge(s_qst[1].h, 8 * 16)
            sync.wait_ge(s_vst[0].h, 12 * 16)
            sync.wait_ge(s_vst[1].h, 12 * 16)
            def emit_at3_store(sync, it2):
                qb2, h2 = it2 // 16, it2 % 16
                sync.wait_ge(s_nrm.h, it2 + 1)
                s_ast[it2 % 2].inc(sync.dma_start(
                    out=io["at3"][h2, qb2 * 4:(qb2 + 1) * 4]
                    .rearrange("j p t -> p j t"),
                    in_=atb[:, (it2 % 2) * 512:(it2 % 2 + 1) * 512]
                    .rearrange("p (j t) -> p j t", j=4)), 16)

            for qb in range(4):
                sq = qb % 2
                if qb >= 2:
                    sync.wait_ge(s_nrm.h, (qb - 1) * 16)
                s_mld[sq].inc(sync.dma_start(
                    out=msk[:, sq * 12 * 512:(sq + 1) * 12 * 512]
                    .rearrange("p (c q) -> p c q", q=512),
                    in_=io["l0mask"][qb].rearrange("c p q -> p c q")), 16)
                s_mld[sq].inc(sync.dma_start(
                    out=vwin[:, sq * 12 * 512:(sq + 1) * 12 * 512]
                    .rearrange("p (c u) -> p c u", u=512),
                    in_=io["vsc"][qb * 512: qb * 512 + 1536, :]
                    .rearrange("(c p) u -> p c u", p=128)), 16)
                for h in range(16):
                    it = qb * 16 + h
                    if it >= 2:
                        sync.wait_ge(s_smm.h, (it - 1) * 12)
                    s_qhld[it % 2].inc(sync.dma_start(
                        out=qh[:, (it % 2) * 512:(it % 2 + 1) * 512],
                        in_=io["qsc"][h * 128:(h + 1) * 128, qb * 512:(qb + 1) * 512]), 16)
                    if it >= 2:
                        emit_at3_store(sync, it - 2)
            emit_at3_store(sync, 62)
            emit_at3_store(sync, 63)

        @ph.block.tensor
        def _(t):
            t.wait_ge(s_qrp.h, 32)
            t.wait_ge(s_vst[0].h, 12 * 16)
            t.wait_ge(s_vst[1].h, 12 * 16)
            t.wait_ge(s_qst[0].h, 8 * 16)
            t.wait_ge(s_qst[1].h, 8 * 16)
            for qb in range(4):
                sq = qb % 2
                for h in range(16):
                    it = qb * 16 + h
                    g = h // 4
                    t.wait_ge(s_qhld[it % 2].h, (it // 2 + 1) * 16)
                    if h == 0:
                        t.wait_ge(s_mld[sq].h, (qb // 2 + 1) * 32)
                    if it >= 2:
                        t.wait_ge(s_nrm.h, it - 1)
                    ii_o = None

                    def emit_score(c):
                        ic = it * 12 + c
                        jc = qb * 4 + c
                        if ic >= 2:
                            t.wait_ge(s_exp.h, ic - 1)
                        ii = t.matmul(PS[:, (ic % 2) * 512:(ic % 2 + 1) * 512],
                                      kbar[:, g * TIN + jc * 128: g * TIN + (jc + 1) * 128],
                                      qh[:, (it % 2) * 512:(it % 2 + 1) * 512],
                                      start=True, stop=True)
                        s_smm.inc(ii)

                    def emit_av(c):
                        nonlocal ii_o
                        ic = it * 12 + c
                        jc = qb * 4 + c
                        t.wait_ge(s_msk.h, ic + 1)
                        ii_o = t.matmul(PS[:, 1024 + (it % 2) * 512:
                                           1024 + (it % 2 + 1) * 512],
                                        vwin[:, sq * 12 * 512 + c * 512 + g * 128:
                                             sq * 12 * 512 + c * 512 + g * 128 + 128],
                                        pm[:, (ic % 2) * 512:(ic % 2 + 1) * 512],
                                        start=(c == 0), stop=(c == 11))

                    emit_score(0)
                    for c in range(1, 12):
                        emit_score(c)
                        emit_av(c - 1)
                    emit_av(11)
                    s_omm.inc(ii_o)
                    t.wait_ge(s_dbf.h, it + 1)
                    if it >= 1:
                        t.wait_ge(s_dsm.h, it)
                    s_dmm.inc(t.matmul(PS[0:1, 2048:2560], ones_bf[:, :],
                                       dacc[:, :], start=True, stop=True))
                    t.wait_ge(s_rc.h, it + 1)
                    if it >= 1:
                        t.wait_ge(s_rsb.h, it)
                    s_rmm.inc(t.matmul(PS[:, 2560:3072], ones_row[:, :],
                                       recbf[:, :], start=True, stop=True))

        @ph.block.scalar
        def _(s):
            for qb in range(4):
                for h in range(16):
                    it = qb * 16 + h
                    for c in range(12):
                        ic = it * 12 + c
                        s.wait_ge(s_smm.h, ic + 1)
                        if ic >= 2:
                            s.wait_ge(s_msk.h, ic - 1)
                        s_exp.inc(s.activation(pbuf[:, (ic % 2) * 512:(ic % 2 + 1) * 512],
                                               PS[:, (ic % 2) * 512:(ic % 2 + 1) * 512],
                                               AF.Exp, scale=INVSQ_HD))
                    s.wait_ge(s_dmm.h, it + 1)
                    s_dsm.inc(s.copy(dsum[:, :], PS[0:1, 2048:2560]))
                    s.wait_ge(s_rmm.h, it + 1)
                    if it >= 1:
                        s.wait_ge(s_nrm.h, it)
                    s_rsb.inc(s.copy(rsb[:, :], PS[:, 2560:3072]))

        @ph.block.vector
        def _(v):
            for qb in range(4):
                sq = qb % 2
                for h in range(16):
                    it = qb * 16 + h
                    for c in range(12):
                        ic = it * 12 + c
                        v.wait_ge(s_exp.h, ic + 1)
                        edge(v, vseq, v.tensor_tensor(
                            pm[:, (ic % 2) * 512:(ic % 2 + 1) * 512],
                            pbuf[:, (ic % 2) * 512:(ic % 2 + 1) * 512],
                            msk[:, sq * 12 * 512 + c * 512: sq * 12 * 512 + (c + 1) * 512],
                            ALU.mult))
                        v.sem_inc(s_msk.h, 1)
                        s_msk.n += 1
                        if c == 0:
                            edge(v, vseq, v.tensor_copy(dacc[:, :],
                                                        pm[:, (ic % 2) * 512:(ic % 2 + 1) * 512]))
                        else:
                            edge(v, vseq, v.tensor_tensor(
                                dacc[:, :], dacc[:, :],
                                pm[:, (ic % 2) * 512:(ic % 2 + 1) * 512], ALU.add))
                    v.sem_inc(s_dbf.h, 1)
                    s_dbf.n += 1
                    v.wait_ge(s_dsm.h, it + 1)
                    if it >= 1:
                        v.wait_ge(s_rmm.h, it)
                    edge(v, vseq, v.reciprocal(rec[:, :], dsum[:, :]))
                    s_rc.inc(v.tensor_copy(recbf[:, :], rec[:, :]))
                    v.wait_ge(s_rsb.h, it + 1)
                    v.wait_ge(s_omm.h, it + 1)
                    if it >= 2:
                        v.wait_ge(s_ast[it % 2].h, (it // 2) * 16)
                    s_nrm.inc(v.tensor_tensor(atb[:, (it % 2) * 512:(it % 2 + 1) * 512],
                                              PS[:, 1024 + (it % 2) * 512:
                                                 1024 + (it % 2 + 1) * 512],
                                              rsb[:, :], ALU.mult))

        if stages < 3:
            st2.close(); st12.close()
            return nc
        st2.close()
        st12.close()
        wobf = ph.sbuf("wobf", [128, 16 * 2048], BF16)
        atc = ph.sbuf("atc", [128, 2 * 2048], BF16)
        fbuf = ph.sbuf("fbuf", [128, 2 * 2048], F32)
        lnf = ph.sbuf("lnf", [128, 2048], F32)
        xo = ph.sbuf("xo", [128, 2 * 2048], F32)
        obuf = ph.sbuf("obuf", [128, 2 * 2048], F32)
        fb6 = ph.sbuf("fb6", [128, 24], F32)
        fmv = ph.sbuf("fmv", [128, 2], F32)
        frs = ph.sbuf("frs", [128, 1], F32)
        fnm = ph.sbuf("fnm", [128, 1], F32)

        s_wo = ph.sem("wo")
        s_atc = [ph.sem("atc0"), ph.sem("atc1")]
        s_xo = [ph.sem("xo0"), ph.sem("xo1")]
        s_fmm = ph.sem("fmm")
        s_fev = ph.sem("fev")
        s_fbn = ph.sem("fbn")
        s_fmv = ph.sem("fmv")
        s_frs = ph.sem("frs")
        s_fnm = ph.sem("fnm")
        s_fap = ph.sem("fap")
        s_res = ph.sem("res")
        s_ost = [ph.sem("ost0"), ph.sem("ost1")]

        @ph.block.gpsimd
        def _(g):
            g.wait_ge(s_ast[0].h, 32 * 16)
            g.wait_ge(s_ast[1].h, 32 * 16)
            s_wo.inc(g.dma_start(
                out=wobf[:, :].rearrange("p (hc d) -> p hc d", d=2048),
                in_=io["wo"].rearrange("(hc p) d -> p hc d", p=128)), 16)

        @ph.block.sync
        def _(sync):
            sync.wait_ge(s_ast[0].h, 32 * 16)
            sync.wait_ge(s_ast[1].h, 32 * 16)
            for tt in range(16):
                sl = tt % 2
                if tt >= 2:
                    sync.wait_ge(s_fmm.h, tt - 1)
                s_atc[sl].inc(sync.dma_start(
                    out=atc[:, sl * 2048:(sl + 1) * 2048]
                    .rearrange("p (hc t) -> p hc t", t=128),
                    in_=io["at3"][:, tt].rearrange("hc p t -> p hc t")), 16)
                if tt >= 2:
                    sync.wait_ge(s_res.h, tt - 1)
                s_xo[sl].inc(sync.dma_start(
                    out=xo[:, sl * 2048:(sl + 1) * 2048],
                    in_=io["xin"][1024 + tt * 128: 1024 + (tt + 1) * 128, :]), 16)
                sync.wait_ge(s_res.h, tt + 1)
                s_ost[sl].inc(sync.dma_start(
                    out=io["x1"][tt * 128:(tt + 1) * 128, :],
                    in_=obuf[:, sl * 2048:(sl + 1) * 2048]), 16)
            sync.wait_ge(s_ost[0].h, 8 * 16)
            sync.wait_ge(s_ost[1].h, 8 * 16)

        @ph.block.tensor
        def _(t):
            t.wait_ge(s_nrm.h, 64)
            t.wait_ge(s_wo.h, 16)
            for tt in range(16):
                sl = tt % 2
                t.wait_ge(s_atc[sl].h, (tt // 2 + 1) * 16)
                if tt >= 1:
                    t.wait_ge(s_fev.h, tt)
                ii = None
                for hd in range(16):
                    for dc in range(4):
                        ii = t.matmul(PS[:, dc * 512:(dc + 1) * 512],
                                      atc[:, sl * 2048 + hd * 128: sl * 2048 + (hd + 1) * 128],
                                      wobf[:, hd * 2048 + dc * 512: hd * 2048 + (dc + 1) * 512],
                                      start=(hd == 0), stop=(hd == 15))
                s_fmm.inc(ii)

        @ph.block.scalar
        def _(s):
            for tt in range(16):
                sl = tt % 2
                s.wait_ge(s_fmm.h, tt + 1)
                if tt >= 2:
                    s.wait_ge(s_fbn.h, tt - 1)
                s_fev.inc(s.copy(fbuf[:, sl * 2048:(sl + 1) * 2048], PS[:, 0:2048]))
                s.wait_ge(s_fmv.h, tt + 1)
                edge(s, sseq, s.activation(frs[:, :], fmv[:, 1:2], AF.Ln, bias=eps[:, :]))
                s_frs.inc(s.activation(frs[:, :], frs[:, :], AF.Exp, scale=-0.5))
                s.wait_ge(s_frs.h, tt + 1)
                s.wait_ge(s_fnm.h, tt + 1)
                if tt >= 1:
                    s.wait_ge(s_res.h, tt)
                s_fap.inc(s.activation(lnf[:, :], fbuf[:, sl * 2048:(sl + 1) * 2048],
                                       AF.Identity, scale=frs[:, :], bias=fnm[:, :]))

        @ph.block.vector
        def _(v):
            for tt in range(16):
                sl = tt % 2
                v.wait_ge(s_fev.h, tt + 1)
                ii = None
                for j in range(4):
                    ii = v.bn_stats(fb6[:, j * 6:(j + 1) * 6],
                                    fbuf[:, sl * 2048 + j * 512: sl * 2048 + (j + 1) * 512])
                s_fbn.inc(ii)
                v.wait_ge(s_fbn.h, tt + 1)
                s_fmv.inc(v.bn_aggr(fmv[:, :], fb6[:, :]))
                v.wait_ge(s_fmv.h, tt + 1)
                v.wait_ge(s_frs.h, tt + 1)
                s_fnm.inc(v.tensor_scalar(fnm[:, :], fmv[:, 0:1], frs[:, :], -1.0,
                                          ALU.mult, ALU.mult))
                v.wait_ge(s_fap.h, tt + 1)
                v.wait_ge(s_xo[sl].h, (tt // 2 + 1) * 16)
                if tt >= 2:
                    v.wait_ge(s_ost[sl].h, (tt // 2) * 16)
                s_res.inc(v.tensor_tensor(obuf[:, sl * 2048:(sl + 1) * 2048],
                                          lnf[:, :], xo[:, sl * 2048:(sl + 1) * 2048],
                                          ALU.add))
    return nc



def rope_tables(positions):
    inv_freq = 1.0 / (10000.0 ** (np.arange(0, HD, 2, dtype=np.float64) / HD))
    freqs = np.asarray(positions, dtype=np.float64)[:, None] * inv_freq[None, :]
    emb = np.concatenate([freqs, freqs], axis=1)
    return (np.cos(emb).T.astype(np.float32).copy(),
            np.sin(emb).T.astype(np.float32).copy())


def l0_masks(half):
    import ml_dtypes
    m = np.zeros((4, 12, 128, 512), dtype=np.float32)
    for qb in range(4):
        for c in range(12):
            j = qb * 512 + c * 128 + np.arange(128)[:, None]
            qo = qb * 512 + np.arange(512)[None, :]
            ok = (j >= qo) & (j <= qo + 1024)
            if half == 0:
                ok &= (j >= 1024)
            m[qb, c] = ok.astype(np.float32)
    return m.astype(ml_dtypes.bfloat16)



def emit_resid_tail_scalar(ph, s, *, tt, sl, fbuf, lnf, eps, frs, fnm, fb6, fmv,
                           sems, sseq):
    s.wait_ge(sems["fmv"].h, tt + 1)
    edge(s, sseq, s.activation(frs[:, :], fmv[:, 1:2], AF.Ln, bias=eps[:, :]))
    sems["frs"].inc(s.activation(frs[:, :], frs[:, :], AF.Exp, scale=-0.5))
    s.wait_ge(sems["frs"].h, tt + 1)
    s.wait_ge(sems["fnm"].h, tt + 1)
    if tt >= 1:
        s.wait_ge(sems["res"].h, tt)
    sems["fap"].inc(s.activation(lnf[:, :], fbuf[:, sl * 2048:(sl + 1) * 2048],
                                 AF.Identity, scale=frs[:, :], bias=fnm[:, :]))


def build_phase_b(nc, io):
    with ExitStack() as stack:
        block = stack.enter_context(nc.Block())
        ph = Phase(nc, stack, block, "B")

        ident = ph.sbuf("ident", [128, 128], BF16)
        eps = ph.sbuf("eps", [128, 1], F32)
        ones_f = ph.sbuf("onesf", [128, 1], F32)
        onesr_f = ph.sbuf("onesrf", [1, 128], F32)
        psbf = ph.psum("psbf", [128, 2048], BF16)
        PS = ph.psum("ps", [128, 3072], F32)
        vseq = ph.sem("vseq")
        sseq = ph.sem("sseq")
        s_pre = ph.sem("pre")

        @ph.block.sync
        def _(sync):
            s_pre.inc(sync.dma_start(out=ident[:, :], in_=io["ident"][:, :]), 16)

        @ph.block.vector
        def _(v):
            v.memset(eps[:, :], 1e-5)
            v.memset(ones_f[:, :], 1.0)
            v.memset(onesr_f[:, :], 1.0)
            v.wait_ge(s_pre.h, 16)

        gcol = ph.sbuf("gcol", [128, 104], F32)
        gsum = ph.sbuf("gsum", [1, 104], F32)
        hcol = ph.sbuf("hcol", [128, 6], F32)

        st0 = ExitStack()
        stLN = ExitStack()
        xhat = ph.sbuf("xhat", [128, 16 * 2048], BF16, st0)

        def dstb(ci, tc):
            return xhat[:, ci * 2048 + tc * 128: ci * 2048 + (tc + 1) * 128]

        lns = emit_ln_xhat(ph, src_hbm=io["x1"], n_tc=16, dst_tiles=dstb,
                           ident=ident, eps=eps, psbf=psbf, name="lnb",
                           bufstack=stLN)
        s_ev = lns["ev"]
        EV_ALL = s_ev.n
        stLN.close()

        stG = ExitStack()
        gbuf = ph.sbuf("gbuf", [128, 2 * 5632], F32, stG)
        gscr = ph.sbuf("gscr", [128, 5632], BF16, stG)

        s_gld = [ph.sem("gld0"), ph.sem("gld1")]
        s_gab = ph.sem("gab")
        s_gmm = ph.sem("gmm")
        s_gsm = ph.sem("gsm")
        s_hred = ph.sem("hred")
        s_hmm = ph.sem("hmm")
        s_hcp = ph.sem("hcp")

        g_chunks = [("w1", i, 2048) for i in range(44)] + \
                   [("w2", i, 2048) for i in range(44)] + \
                   [("w3", i, 5632) for i in range(16)]

        @ph.block.gpsimd
        def _(g):
            g.wait_ge(s_ev.h, EV_ALL)
            for k, (wn, i, w) in enumerate(g_chunks):
                sl = k % 2
                if k >= 2:
                    g.wait_ge(s_gab.h, k - 1)
                s_gld[sl].inc(g.dma_start(
                    out=gbuf[:, sl * 5632: sl * 5632 + w],
                    in_=io[wn][i * 128:(i + 1) * 128, :]), 16)

        @ph.block.scalar
        def _(s):
            for k, (wn, i, w) in enumerate(g_chunks):
                sl = k % 2
                s.wait_ge(s_gld[sl].h, (k // 2 + 1) * 16)
                if k >= 1:
                    s.wait_ge(s_gab.h, k)
                s_gab.inc(s.activation(gscr[:, 0:w], gbuf[:, sl * 5632: sl * 5632 + w],
                                       AF.Abs, accum_out=gcol[:, k:k + 1]))

        @ph.block.tensor
        def _(t):
            t.wait_ge(s_gab.h, 104)
            s_gmm.inc(t.matmul(PS[0:1, 2048:2152], ones_f[:, :], gcol[:, 0:104],
                               start=True, stop=True))

        @ph.block.scalar
        def _(s):
            s.wait_ge(s_gmm.h, 1)
            s_gsm.inc(s.copy(gsum[:, :], PS[0:1, 2048:2152]))

        @ph.block.vector
        def _(v):
            v.wait_ge(s_gsm.h, 1)
            for wi, (lo, hi, numel) in enumerate([(0, 44, 5632 * 2048),
                                                  (44, 88, 5632 * 2048),
                                                  (88, 104, 2048 * 5632)]):
                edge(v, vseq, v.reduce_sum(gsum[0:1, lo:lo + 1] if False else
                                           hcol[0:1, wi:wi + 1],
                                           gsum[0:1, lo:hi], mybir.AxisListType.X))
                edge(v, vseq, v.tensor_scalar(hcol[0:1, wi:wi + 1], hcol[0:1, wi:wi + 1],
                                              0.5 / numel, 0.5e-8, ALU.mult, ALU.add))
            s_hred.inc(v.tensor_scalar(hcol[0:1, 3:6], hcol[0:1, 0:3], -1.0, None,
                                       ALU.mult))

        @ph.block.tensor
        def _(t):
            t.wait_ge(s_hred.h, 1)
            s_hmm.inc(t.matmul(PS[:, 2560:2566], onesr_f[:, :], hcol[0:1, 0:6],
                               start=True, stop=True))

        @ph.block.scalar
        def _(s):
            s.wait_ge(s_hmm.h, 1)
            s_hcp.inc(s.copy(hcol[:, 0:6], PS[:, 2560:2566]))

        stG.close()
        wbuf = ph.sbuf("wbuf", [128, 2 * 2048], F32, st0)
        wbuf2 = ph.sbuf("wbuf2", [128, 2 * 2048], F32, st0)
        p1s = ph.sbuf("p1s", [128, 2048], F32, st0)
        qn = ph.sbuf("qn", [128, 2 * 2048], BF16, st0)
        qn2 = ph.sbuf("qn2", [128, 2 * 2048], BF16, st0)
        q1t = ph.sbuf("q1t", [128, 2 * 2048], BF16, st0)
        q2t = ph.sbuf("q2t", [128, 2 * 2048], BF16, st0)
        sgb = ph.sbuf("sgb", [128, 2 * 512], F32, st0)
        hid = ph.sbuf("hid", [128, 2 * 512], BF16, st0)

        s_w1ld = [ph.sem("w1ld0"), ph.sem("w1ld1")]
        s_w2ld = [ph.sem("w2ld0"), ph.sem("w2ld1")]
        s_qz = ph.sem("qz")
        s_qtp = ph.sem("qtp")
        s_qev = ph.sem("qev")
        s_xmm = ph.sem("xmm")
        s_sg = ph.sem("sg")
        s_hd = ph.sem("hd")
        s_hst = [ph.sem("hst0"), ph.sem("hst1")]

        @ph.block.sync
        def _(sync):
            sync.wait_ge(s_gab.h, 104)
            def emit_h3_store(hcx):
                for tb in range(4):
                    it = hcx * 4 + tb
                    sync.wait_ge(s_hd.h, it + 1)
                    s_hst[it % 2].inc(sync.dma_start(
                        out=io["h3"][hcx, tb * 4:(tb + 1) * 4]
                        .rearrange("j p t -> p j t"),
                        in_=hid[:, (it % 2) * 512:(it % 2 + 1) * 512]
                        .rearrange("p (j t) -> p j t", j=4)), 16)

            for hc in range(44):
                sl = hc % 2
                if hc >= 2:
                    sync.wait_ge(s_qz.h, 2 * (hc - 1))
                s_w1ld[sl].inc(sync.dma_start(
                    out=wbuf[:, sl * 2048:(sl + 1) * 2048],
                    in_=io["w1"][hc * 128:(hc + 1) * 128, :]), 16)
                s_w2ld[sl].inc(sync.dma_start(
                    out=wbuf2[:, sl * 2048:(sl + 1) * 2048],
                    in_=io["w2"][hc * 128:(hc + 1) * 128, :]), 16)
                if hc >= 1:
                    emit_h3_store(hc - 1)
            emit_h3_store(43)

        @ph.block.vector
        def _(v):
            v.wait_ge(s_hcp.h, 1)

            def emit_hidden(hcx):
                for tb in range(4):
                    it = hcx * 4 + tb
                    v.wait_ge(s_sg.h, it + 1)
                    v.wait_ge(s_xmm.h, hcx * 4 + (tb // 2) * 2 + 2)
                    if it >= 2:
                        v.wait_ge(s_hst[it % 2].h, (it // 2) * 16)
                    s_hd.inc(v.tensor_tensor(
                        hid[:, (it % 2) * 512:(it % 2 + 1) * 512],
                        sgb[:, (it % 2) * 512:(it % 2 + 1) * 512],
                        PS[:, 1024 + (it % 2) * 512: 1024 + (it % 2 + 1) * 512],
                        ALU.mult))

            for hc in range(44):
                sl = hc % 2
                v.wait_ge(s_w1ld[sl].h, (hc // 2 + 1) * 16)
                if hc >= 2:
                    v.wait_ge(s_qtp.h, 16 * (hc - 1))
                edge(v, vseq, v.tensor_scalar(p1s[:, :], wbuf[:, sl * 2048:(sl + 1) * 2048],
                                              hcol[:, 0:1], -1.0, ALU.is_ge, ALU.add))
                edge(v, vseq, v.scalar_tensor_tensor(
                    qn[:, sl * 2048:(sl + 1) * 2048], wbuf[:, sl * 2048:(sl + 1) * 2048],
                    hcol[:, 3:4], p1s[:, :], ALU.is_ge, ALU.add))
                v.sem_inc(s_qz.h, 1)
                s_qz.n += 1
                v.wait_ge(s_w2ld[sl].h, (hc // 2 + 1) * 16)
                edge(v, vseq, v.tensor_scalar(p1s[:, :], wbuf2[:, sl * 2048:(sl + 1) * 2048],
                                              hcol[:, 1:2], -1.0, ALU.is_ge, ALU.add))
                edge(v, vseq, v.scalar_tensor_tensor(
                    qn2[:, sl * 2048:(sl + 1) * 2048], wbuf2[:, sl * 2048:(sl + 1) * 2048],
                    hcol[:, 4:5], p1s[:, :], ALU.is_ge, ALU.add))
                v.sem_inc(s_qz.h, 1)
                s_qz.n += 1
                for g8 in range(16):
                    G = hc * 16 + g8
                    v.wait_ge(s_qtp.h, G + 1)
                    if hc >= 2 and g8 == 0:
                        v.wait_ge(s_xmm.h, 4 * (hc - 1))
                    wq, dstq = (qn, q1t) if g8 < 8 else (qn2, q2t)
                    gg = g8 % 8
                    half = G % 2
                    ii = None
                    for j in range(2):
                        ci = gg * 2 + j
                        ii = v.tensor_copy(
                            dstq[:, sl * 2048 + ci * 128: sl * 2048 + (ci + 1) * 128],
                            psbf[:, half * 1024 + j * 128: half * 1024 + (j + 1) * 128])
                    s_qev.inc(ii)
                if hc >= 1:
                    emit_hidden(hc - 1)
            emit_hidden(43)

        @ph.block.tensor
        def _(t):
            t.wait_ge(s_hcp.h, 1)

            def emit_tp(hc):
                sl = hc % 2
                for g8 in range(16):
                    G = hc * 16 + g8
                    t.wait_ge(s_qz.h, 2 * hc + (1 if g8 < 8 else 2))
                    if G >= 2:
                        t.wait_ge(s_qev.h, G - 1)
                    wq = qn if g8 < 8 else qn2
                    gg = g8 % 8
                    half = G % 2
                    ii = None
                    for j in range(2):
                        ci = gg * 2 + j
                        ii = t.transpose(psbf[:, half * 1024 + j * 128:
                                              half * 1024 + (j + 1) * 128],
                                         wq[:, sl * 2048 + ci * 128: sl * 2048 + (ci + 1) * 128],
                                         ident[:, :])
                    s_qtp.inc(ii)

            def emit_mm(hc):
                sl = hc % 2
                t.wait_ge(s_qev.h, 16 * (hc + 1))
                for tp2 in range(2):
                    it0 = hc * 4 + tp2 * 2
                    t.wait_ge(s_sg.h, max(0, it0))
                    t.wait_ge(s_hd.h, max(0, it0))
                    ii = None
                    for ci in range(16):
                        for tb in (tp2 * 2, tp2 * 2 + 1):
                            ii = t.matmul(PS[:, (tb % 2) * 512:(tb % 2 + 1) * 512],
                                          q1t[:, sl * 2048 + ci * 128: sl * 2048 + (ci + 1) * 128],
                                          xhat[:, ci * 2048 + tb * 512: ci * 2048 + (tb + 1) * 512],
                                          start=(ci == 0), stop=(ci == 15))
                    s_xmm.inc(ii)
                    ii = None
                    for ci in range(16):
                        for tb in (tp2 * 2, tp2 * 2 + 1):
                            ii = t.matmul(PS[:, 1024 + (tb % 2) * 512: 1024 + (tb % 2 + 1) * 512],
                                          q2t[:, sl * 2048 + ci * 128: sl * 2048 + (ci + 1) * 128],
                                          xhat[:, ci * 2048 + tb * 512: ci * 2048 + (tb + 1) * 512],
                                          start=(ci == 0), stop=(ci == 15))
                    s_xmm.inc(ii)

            emit_tp(0)
            for hc in range(1, 44):
                emit_tp(hc)
                emit_mm(hc - 1)
            emit_mm(43)

        @ph.block.scalar
        def _(s):
            s.wait_ge(s_hcp.h, 1)
            for hc in range(44):
                for tb in range(4):
                    it = hc * 4 + tb
                    s.wait_ge(s_xmm.h, hc * 4 + (tb // 2) * 2 + 1)
                    if it >= 2:
                        s.wait_ge(s_hd.h, it - 1)
                    s_sg.inc(s.activation(sgb[:, (it % 2) * 512:(it % 2 + 1) * 512],
                                          PS[:, (it % 2) * 512:(it % 2 + 1) * 512],
                                          AF.Sigmoid))

        st0.close()
        st2 = ExitStack()
        w3b = ph.sbuf("w3b", [128, 2 * 5632], F32, st2)
        p3s = ph.sbuf("p3s", [128, 5632], F32, st2)
        q3n = ph.sbuf("q3n", [128, 2 * 5632], BF16, st2)
        q3t = ph.sbuf("q3t", [128, 2 * 44 * 512], BF16, st2)
        hc3 = ph.sbuf("hc3", [128, 2 * 5632], BF16, st2)
        fdb = ph.sbuf("fdb", [128, 2 * 512], F32, st2)

        s_w3ld = [ph.sem("w3ld0"), ph.sem("w3ld1")]
        s_q3z = ph.sem("q3z")
        s_q3tp = ph.sem("q3tp")
        s_q3ev = ph.sem("q3ev")
        s_h3ld = [ph.sem("h3ld0"), ph.sem("h3ld1")]
        s_fmm = ph.sem("fmmB")
        s_fevB = ph.sem("fevB")
        s_fst = [ph.sem("fst0"), ph.sem("fst1")]

        NG3 = 11

        @ph.block.sync
        def _(sync):
            sync.wait_ge(s_hst[0].h, 88 * 16)
            sync.wait_ge(s_hst[1].h, 88 * 16)
            def emit_fio(dbx):
                for tt in range(16):
                    it = dbx * 16 + tt
                    sl = it % 2
                    if it >= 2:
                        sync.wait_ge(s_fmm.h, it - 1)
                    s_h3ld[sl].inc(sync.dma_start(
                        out=hc3[:, sl * 5632: sl * 5632 + 44 * 128]
                        .rearrange("p (hcx t) -> p hcx t", t=128),
                        in_=io["h3"][:, tt].rearrange("hcx p t -> p hcx t")), 16)
                    sync.wait_ge(s_fevB.h, it + 1)
                    s_fst[sl].inc(sync.dma_start(
                        out=io["fsc"][tt * 128:(tt + 1) * 128, dbx * 512:(dbx + 1) * 512],
                        in_=fdb[:, sl * 512:(sl + 1) * 512]), 16)

            for db in range(4):
                for dr in range(4):
                    k = db * 4 + dr
                    sl = k % 2
                    if k >= 2:
                        sync.wait_ge(s_q3z.h, k - 1)
                    s_w3ld[sl].inc(sync.dma_start(
                        out=w3b[:, sl * 5632:(sl + 1) * 5632],
                        in_=io["w3"][k * 128:(k + 1) * 128, :]), 16)
                if db >= 1:
                    emit_fio(db - 1)
            emit_fio(3)

        @ph.block.vector
        def _(v):
            for db in range(4):
                for dr in range(4):
                    k = db * 4 + dr
                    sl = k % 2
                    v.wait_ge(s_w3ld[sl].h, (k // 2 + 1) * 16)
                    if k >= 2:
                        v.wait_ge(s_q3tp.h, NG3 * (k - 1))
                    edge(v, vseq, v.tensor_scalar(p3s[:, :], w3b[:, sl * 5632:(sl + 1) * 5632],
                                                  hcol[:, 2:3], -1.0, ALU.is_ge, ALU.add))
                    edge(v, vseq, v.scalar_tensor_tensor(
                        q3n[:, sl * 5632:(sl + 1) * 5632], w3b[:, sl * 5632:(sl + 1) * 5632],
                        hcol[:, 5:6], p3s[:, :], ALU.is_ge, ALU.add))
                    v.sem_inc(s_q3z.h, 1)
                    s_q3z.n += 1
                    for g in range(NG3):
                        G = k * NG3 + g
                        v.wait_ge(s_q3tp.h, G + 1)
                        if db > 1 and dr == 0 and g == 0:
                            v.wait_ge(s_fmm.h, (db - 1) * 16)
                        qoff = (db % 2) * 44 * 512
                        ii = None
                        for j in range(4):
                            hcx = g * 4 + j
                            if hcx >= 44:
                                break
                            ii = v.tensor_copy(
                                q3t[:, qoff + hcx * 512 + dr * 128:
                                    qoff + hcx * 512 + (dr + 1) * 128],
                                psbf[:, (G % 2) * 1024 + j * 128:
                                     (G % 2) * 1024 + (j + 1) * 128])
                        s_q3ev.inc(ii)

        @ph.block.tensor
        def _(t):
            t.wait_ge(s_hst[0].h, 88 * 16)
            t.wait_ge(s_hst[1].h, 88 * 16)

            def emit_tp3(db):
                for dr in range(4):
                    k = db * 4 + dr
                    sl = k % 2
                    t.wait_ge(s_q3z.h, k + 1)
                    for g in range(NG3):
                        G = k * NG3 + g
                        if G >= 2:
                            t.wait_ge(s_q3ev.h, G - 1)
                        ii = None
                        for j in range(4):
                            hcx = g * 4 + j
                            if hcx >= 44:
                                break
                            ii = t.transpose(psbf[:, (G % 2) * 1024 + j * 128:
                                                  (G % 2) * 1024 + (j + 1) * 128],
                                             q3n[:, sl * 5632 + hcx * 128:
                                                 sl * 5632 + (hcx + 1) * 128],
                                             ident[:, :])
                        s_q3tp.inc(ii)

            def emit_mm3(db):
                qoff = (db % 2) * 44 * 512
                t.wait_ge(s_q3ev.h, NG3 * 4 * (db + 1))
                for tt in range(16):
                    it = db * 16 + tt
                    sl = it % 2
                    t.wait_ge(s_h3ld[sl].h, (it // 2 + 1) * 16)
                    if it >= 2:
                        t.wait_ge(s_fevB.h, it - 1)
                    ii = None
                    for hcx in range(44):
                        ii = t.matmul(PS[:, sl * 512:(sl + 1) * 512],
                                      hc3[:, sl * 5632 + hcx * 128: sl * 5632 + (hcx + 1) * 128],
                                      q3t[:, qoff + hcx * 512: qoff + hcx * 512 + 512],
                                      start=(hcx == 0), stop=(hcx == 43))
                    s_fmm.inc(ii)

            emit_tp3(0)
            for db in range(1, 4):
                emit_tp3(db)
                emit_mm3(db - 1)
            emit_mm3(3)

        @ph.block.scalar
        def _(s):
            for db in range(4):
                for tt in range(16):
                    it = db * 16 + tt
                    sl = it % 2
                    s.wait_ge(s_fmm.h, it + 1)
                    if it >= 2:
                        s.wait_ge(s_fst[sl].h, (it // 2) * 16)
                    s_fevB.inc(s.copy(fdb[:, sl * 512:(sl + 1) * 512],
                                      PS[:, sl * 512:(sl + 1) * 512]))

        st2.close()
        fb = ph.sbuf("fb", [128, 2 * 2048], F32)
        xr = ph.sbuf("xr", [128, 2 * 2048], F32)
        ob = ph.sbuf("ob", [128, 2 * 2048], F32)
        lnf = ph.sbuf("lnfB", [128, 2048], F32)
        fb6 = ph.sbuf("fb6B", [128, 24], F32)
        fmv = ph.sbuf("fmvB", [128, 2], F32)
        frs = ph.sbuf("frsB", [128, 1], F32)
        fnm = ph.sbuf("fnmB", [128, 1], F32)

        s_fld = [ph.sem("fld0"), ph.sem("fld1")]
        s_xld = [ph.sem("xld0"), ph.sem("xld1")]
        sems3 = {n: ph.sem(n + "3") for n in
                 ["fbn", "fmv", "frs", "fnm", "fap", "res"]}
        s_ost = [ph.sem("ostB0"), ph.sem("ostB1")]

        @ph.block.sync
        def _(sync):
            sync.wait_ge(s_fst[0].h, 32 * 16)
            sync.wait_ge(s_fst[1].h, 32 * 16)
            for tc in range(16):
                sl = tc % 2
                if tc >= 2:
                    sync.wait_ge(sems3["fbn"].h, tc - 1)
                s_fld[sl].inc(sync.dma_start(
                    out=fb[:, sl * 2048:(sl + 1) * 2048],
                    in_=io["fsc"][tc * 128:(tc + 1) * 128, :]), 16)
                if tc >= 2:
                    sync.wait_ge(sems3["res"].h, tc - 1)
                s_xld[sl].inc(sync.dma_start(
                    out=xr[:, sl * 2048:(sl + 1) * 2048],
                    in_=io["x1"][tc * 128:(tc + 1) * 128, :]), 16)
                sync.wait_ge(sems3["res"].h, tc + 1)
                s_ost[sl].inc(sync.dma_start(
                    out=io["x2"][tc * 128:(tc + 1) * 128, :],
                    in_=ob[:, sl * 2048:(sl + 1) * 2048]), 16)
            sync.wait_ge(s_ost[0].h, 8 * 16)
            sync.wait_ge(s_ost[1].h, 8 * 16)

        @ph.block.vector
        def _(v):
            for tc in range(16):
                sl = tc % 2
                v.wait_ge(s_fld[sl].h, (tc // 2 + 1) * 16)
                if tc >= 2:
                    v.wait_ge(sems3["fap"].h, tc - 1)
                ii = None
                for j in range(4):
                    ii = v.bn_stats(fb6[:, j * 6:(j + 1) * 6],
                                    fb[:, sl * 2048 + j * 512: sl * 2048 + (j + 1) * 512])
                sems3["fbn"].inc(ii)
                v.wait_ge(sems3["fbn"].h, tc + 1)
                sems3["fmv"].inc(v.bn_aggr(fmv[:, :], fb6[:, :]))
                v.wait_ge(sems3["fmv"].h, tc + 1)
                v.wait_ge(sems3["frs"].h, tc + 1)
                sems3["fnm"].inc(v.tensor_scalar(fnm[:, :], fmv[:, 0:1], frs[:, :],
                                                 -1.0, ALU.mult, ALU.mult))
                v.wait_ge(sems3["fap"].h, tc + 1)
                v.wait_ge(s_xld[sl].h, (tc // 2 + 1) * 16)
                if tc >= 2:
                    v.wait_ge(s_ost[sl].h, (tc // 2) * 16)
                sems3["res"].inc(v.tensor_tensor(ob[:, sl * 2048:(sl + 1) * 2048],
                                                 lnf[:, :], xr[:, sl * 2048:(sl + 1) * 2048],
                                                 ALU.add))

        @ph.block.scalar
        def _(s):
            for tc in range(16):
                sl = tc % 2
                s.wait_ge(sems3["fmv"].h, tc + 1)
                edge(s, sseq, s.activation(frs[:, :], fmv[:, 1:2], AF.Ln, bias=eps[:, :]))
                sems3["frs"].inc(s.activation(frs[:, :], frs[:, :], AF.Exp, scale=-0.5))
                s.wait_ge(sems3["frs"].h, tc + 1)
                s.wait_ge(sems3["fnm"].h, tc + 1)
                if tc >= 1:
                    s.wait_ge(sems3["res"].h, tc)
                sems3["fap"].inc(s.activation(lnf[:, :], fb[:, sl * 2048:(sl + 1) * 2048],
                                              AF.Identity, scale=frs[:, :], bias=fnm[:, :]))
    return nc



def build_phase_c(nc, io, use_cc=True):
    build_phase_c_pre(nc, io)
    if use_cc:
        pairs = [[0, 1], [2, 3], [4, 5], [6, 7]]
        with ExitStack() as stack:
            block = stack.enter_context(nc.Block())
            ph = Phase(nc, stack, block, "Ccc")
            s_cc = ph.sem("cc")

            @ph.block.gpsimd
            def _(g):
                for src, dst in [("kvc1", "kvg1"), ("kvc2", "kvg2"),
                                 ("kvcv", "kvgv")]:
                    s_cc.inc(g.collective_compute(
                        "AllGather", ALU.bypass, replica_groups=pairs,
                        ins=[io[src + "_t"].ap().opt()],
                        outs=[io[dst + "_t"].ap().opt()]))
                g.wait_ge(s_cc.h, 3)
    build_phase_c_post(nc, io)
    return nc


def build_phase_c_pre(nc, io):
    with ExitStack() as stack:
        block = stack.enter_context(nc.Block())
        ph = Phase(nc, stack, block, "C")

        ident = ph.sbuf("ident", [128, 128], BF16)
        eps = ph.sbuf("eps", [128, 1], F32)
        psbf = ph.psum("psbf", [128, 2048], BF16)
        PS = ph.psum("ps", [128, 3072], F32)
        vseq = ph.sem("vseq")
        s_pre = ph.sem("pre")

        @ph.block.sync
        def _(sync):
            s_pre.inc(sync.dma_start(out=ident[:, :], in_=io["ident"][:, :]), 16)

        @ph.block.vector
        def _(v):
            v.memset(eps[:, :], 1e-5)
            v.wait_ge(s_pre.h, 16)

        st0 = ExitStack()
        stLN = ExitStack()
        xhat = ph.sbuf("xhat", [128, 16 * 2048], BF16, st0)

        def dstc(ci, tc):
            return xhat[:, ci * 2048 + tc * 128: ci * 2048 + (tc + 1) * 128]

        lns = emit_ln_xhat(ph, src_hbm=io["x2"], n_tc=16, dst_tiles=dstc,
                           ident=ident, eps=eps, psbf=psbf, name="lnc",
                           bufstack=stLN)
        s_ev = lns["ev"]
        EV_ALL = s_ev.n
        stLN.close()

        wnat = ph.sbuf("wnat", [128, 2 * 2048], BF16, st0)
        wtb = ph.sbuf("wtb", [128, 2 * 2048], BF16, st0)
        pev = ph.sbuf("pev", [128, 2 * 2048], BF16, st0)

        W_LIST = [("k1w", "kvc1"), ("k2w", "kvc2"), ("vw", "kvcv"),
                  ("q1w", "qsc1"), ("q2w", "qsc2")]
        s_wn = [ph.sem("wn0"), ph.sem("wn1")]
        s_wtp = ph.sem("wtp")
        s_wev = ph.sem("wev")
        s_pmm = ph.sem("pmmC")
        s_pevs = ph.sem("pevsC")
        s_pstg = [ph.sem("pstg0"), ph.sem("pstg1")]
        s_pw = ph.sem("pw")
        s_cc = ph.sem("ccC")

        pairs = [[0, 1], [2, 3], [4, 5], [6, 7]]
        AG_LIST = [("kvc1", "kvg1"), ("kvc2", "kvg2"), ("kvcv", "kvgv")]

        @ph.block.gpsimd
        def _(g):
            g.wait_ge(s_ev.h, EV_ALL)
            for u in range(80):
                wi, ut = u // 16, u % 16
                wsrc = W_LIST[wi][0]
                sl = u % 2
                if u >= 2:
                    g.wait_ge(s_wtp.h, 4 * (u - 1))
                s_wn[sl].inc(g.dma_start(
                    out=wnat[:, sl * 2048:(sl + 1) * 2048],
                    in_=io[wsrc][ut * 128:(ut + 1) * 128, :]), 16)
                if u in (17, 33, 49):
                    agw = (u - 17) // 16
                    g.wait_ge(s_pw.h, agw + 1)
                    src, dst = AG_LIST[agw]
                    s_cc.inc(g.collective_compute(
                        "AllGather", ALU.bypass, replica_groups=pairs,
                        ins=[io[src + "_t"].ap().opt()],
                        outs=[io[dst + "_t"].ap().opt()]))
            g.wait_ge(s_cc.h, 3)

        @ph.block.tensor
        def _(t):
            t.wait_ge(s_ev.h, EV_ALL)
            for u in range(80):
                wi, ut = u // 16, u % 16
                sl = u % 2
                t.wait_ge(s_wn[sl].h, (u // 2 + 1) * 16)
                for g4 in range(4):
                    G = u * 4 + g4
                    if G >= 2:
                        t.wait_ge(s_wev.h, G - 1)
                    ii = None
                    for j in range(4):
                        ci = g4 * 4 + j
                        ii = t.transpose(psbf[:, (G % 2) * 1024 + j * 128:
                                              (G % 2) * 1024 + (j + 1) * 128],
                                         wnat[:, sl * 2048 + ci * 128:
                                              sl * 2048 + (ci + 1) * 128],
                                         ident[:, :])
                    s_wtp.inc(ii)
                t.wait_ge(s_wev.h, 4 * (u + 1))
                if u >= 1:
                    t.wait_ge(s_pevs.h, u)
                ii = None
                for ci in range(16):
                    for mo in range(4):
                        ii = t.matmul(PS[:, mo * 512:(mo + 1) * 512],
                                      wtb[:, sl * 2048 + ci * 128: sl * 2048 + (ci + 1) * 128],
                                      xhat[:, ci * 2048 + mo * 512: ci * 2048 + (mo + 1) * 512],
                                      start=(ci == 0), stop=(ci == 15))
                s_pmm.inc(ii)

        @ph.block.vector
        def _(v):
            for u in range(80):
                wi, ut = u // 16, u % 16
                wsrc = W_LIST[wi][0]
                sl = u % 2
                for g4 in range(4):
                    G = u * 4 + g4
                    v.wait_ge(s_wtp.h, G + 1)
                    if u >= 2 and g4 == 0:
                        v.wait_ge(s_pmm.h, u - 1)
                    ii = None
                    for j in range(4):
                        ci = g4 * 4 + j
                        ii = v.tensor_copy(
                            wtb[:, sl * 2048 + ci * 128: sl * 2048 + (ci + 1) * 128],
                            psbf[:, (G % 2) * 1024 + j * 128:
                                 (G % 2) * 1024 + (j + 1) * 128])
                    s_wev.inc(ii)
                v.wait_ge(s_pmm.h, u + 1)
                if u >= 2:
                    v.wait_ge(s_pstg[u % 2].h, (u // 2) * 16)
                s_pevs.inc(v.tensor_copy(pev[:, sl * 2048:(sl + 1) * 2048],
                                         PS[:, 0:2048]))

        @ph.block.sync
        def _(sync):
            for u in range(80):
                wi, ut = u // 16, u % 16
                wsrc, wdst = W_LIST[wi]
                sl = u % 2
                sync.wait_ge(s_pevs.h, u + 1)
                s_pstg[sl].inc(sync.dma_start(
                    out=io[wdst][ut * 128:(ut + 1) * 128, :],
                    in_=pev[:, sl * 2048:(sl + 1) * 2048]), 16)
                if u in (15, 31, 47):
                    k = (u + 1) // 2
                    sync.wait_ge(s_pstg[0].h, k * 16)
                    sync.wait_ge(s_pstg[1].h, k * 16)
                    sync.sem_inc(s_pw.h, 1)
                    s_pw.n += 1
            sync.wait_ge(s_pstg[0].h, 40 * 16)
            sync.wait_ge(s_pstg[1].h, 40 * 16)
        st0.close()
    return nc


def build_phase_c_post(nc, io):
    with ExitStack() as stack:
        block = stack.enter_context(nc.Block())
        ph = Phase(nc, stack, block, "Cp")

        ident = ph.sbuf("ident", [128, 128], BF16)
        eps = ph.sbuf("eps", [128, 1], F32)
        ones_bf = ph.sbuf("onesb", [128, 1], BF16)
        ones_row = ph.sbuf("onesr", [1, 128], BF16)
        lam = ph.sbuf("lam", [128, 16], F32)
        psbf = ph.psum("psbf", [128, 2048], BF16)
        PS = ph.psum("ps", [128, 3072], F32)
        vseq = ph.sem("vseq")
        sseq = ph.sem("sseq")
        s_pre = ph.sem("pre")

        @ph.block.sync
        def _(sync):
            s_pre.inc(sync.dma_start(out=ident[:, :], in_=io["ident"][:, :]), 16)
            s_pre.inc(sync.dma_start(out=lam[:, :], in_=io["lam_b"][:, :]), 16)

        @ph.block.vector
        def _(v):
            v.memset(eps[:, :], 1e-5)
            v.memset(ones_bf[:, :], 1.0)
            v.memset(ones_row[:, :], 1.0)
            v.wait_ge(s_pre.h, 32)

        st3 = ExitStack()
        k1h = ph.sbuf("k1h", [128, 2 * 4096], BF16, st3)
        k2h = ph.sbuf("k2h", [128, 2 * 4096], BF16, st3)
        vth = ph.sbuf("vth", [128, 2 * 4096], BF16, st3)
        vh = ph.sbuf("vh", [128, 2 * 4096], BF16, st3)
        q1h = ph.sbuf("q1h", [128, 2 * 2048], BF16, st3)
        q2h = ph.sbuf("q2h", [128, 2 * 2048], BF16, st3)
        pb = ph.sbuf("pb", [128, 2 * 1024], BF16, st3)
        dacc = ph.sbuf("daccC", [128, 1024], F16, st3)
        dsum = ph.sbuf("dsumC", [1, 1024], F32, st3)
        rec = ph.sbuf("recC", [1, 1024], F32, st3)
        recbf = ph.sbuf("recbfC", [1, 1024], BF16, st3)
        rsb = ph.sbuf("rsbC", [128, 1024], F32, st3)
        out1 = ph.sbuf("out1", [128, 1024], F32, st3)
        t2b = ph.sbuf("t2b", [128, 1024], F32, st3)
        atb = ph.sbuf("atbC", [128, 2 * 1024], BF16, st3)

        s_kld = [ph.sem("kld0"), ph.sem("kld1")]
        s_vtp = ph.sem("vtp")
        s_vev2 = ph.sem("vev2")
        s_smm = ph.sem("smmC")
        s_exp = ph.sem("expC")
        s_dac = ph.sem("dacC")
        s_omm = ph.sem("ommC")
        s_pod = ph.sem("pod")
        s_dbf = ph.sem("dbfsC")
        s_dmm = ph.sem("dmmC")
        s_dsm = ph.sem("dsmC")
        s_rc = ph.sem("rcC")
        s_rmm = ph.sem("rmmC")
        s_rsbs = ph.sem("rsbsC")
        s_rsd = ph.sem("rsd")
        s_nrm = ph.sem("nrmC")
        s_ast = [ph.sem("astC0"), ph.sem("astC1")]

        def its(h, qb, m):
            return (h * 2 + qb) * 2 + m

        @ph.block.sync
        def _(sync):
            for h in range(16):
                sl = h % 2
                if h >= 2:
                    sync.wait_ge(s_smm.h, 128 * (h - 1))
                    sync.wait_ge(s_vtp.h, 8 * (h - 1))
                for name, buf, src in [("k1", k1h, "kvg1"), ("k2", k2h, "kvg2"),
                                       ("vt", vth, "kvgv")]:
                    s_kld[sl].inc(sync.dma_start(
                        out=buf[:, sl * 4096: sl * 4096 + 2048],
                        in_=io[src][h * 128:(h + 1) * 128, :]), 16)
                    s_kld[sl].inc(sync.dma_start(
                        out=buf[:, sl * 4096 + 2048: sl * 4096 + 4096],
                        in_=io[src][2048 + h * 128: 2048 + (h + 1) * 128, :]), 16)
                s_kld[sl].inc(sync.dma_start(
                    out=q1h[:, sl * 2048:(sl + 1) * 2048],
                    in_=io["qsc1"][h * 128:(h + 1) * 128, :]), 16)
                s_kld[sl].inc(sync.dma_start(
                    out=q2h[:, sl * 2048:(sl + 1) * 2048],
                    in_=io["qsc2"][h * 128:(h + 1) * 128, :]), 16)
                if h >= 1:
                    for qb in range(2):
                        itq = (h - 1) * 2 + qb
                        sync.wait_ge(s_nrm.h, itq + 1)
                        s_ast[itq % 2].inc(sync.dma_start(
                            out=io["at3c"][h - 1, qb * 8:(qb + 1) * 8]
                            .rearrange("j p t -> p j t"),
                            in_=atb[:, (itq % 2) * 1024:(itq % 2 + 1) * 1024]
                            .rearrange("p (j t) -> p j t", j=8)), 16)
            for qb in range(2):
                itq = 15 * 2 + qb
                sync.wait_ge(s_nrm.h, itq + 1)
                s_ast[itq % 2].inc(sync.dma_start(
                    out=io["at3c"][15, qb * 8:(qb + 1) * 8]
                    .rearrange("j p t -> p j t"),
                    in_=atb[:, (itq % 2) * 1024:(itq % 2 + 1) * 1024]
                    .rearrange("p (j t) -> p j t", j=8)), 16)

        @ph.block.tensor
        def _(t):
            t.wait_ge(s_pre.h, 32)
            for h in range(16):
                sl = h % 2
                t.wait_ge(s_kld[sl].h, (h // 2 + 1) * 8 * 16)
                for g4 in range(8):
                    G = h * 8 + g4
                    if G >= 2:
                        t.wait_ge(s_vev2.h, G - 1)
                    ii = None
                    for j in range(4):
                        kc = g4 * 4 + j
                        ii = t.transpose(psbf[:, (G % 2) * 1024 + j * 128:
                                              (G % 2) * 1024 + (j + 1) * 128],
                                         vth[:, sl * 4096 + kc * 128: sl * 4096 + (kc + 1) * 128],
                                         ident[:, :])
                    s_vtp.inc(ii)
                t.wait_ge(s_vev2.h, 8 * (h + 1))
                for qb in range(2):
                    for m in range(2):
                        it2 = its(h, qb, m)
                        kmh = k1h if m == 0 else k2h
                        qmh = q1h if m == 0 else q2h
                        if it2 >= 1:
                            t.wait_ge(s_pod.h, it2)
                        ii_o = None

                        def emit_score(kc):
                            ic = it2 * 32 + kc
                            if ic >= 2:
                                t.wait_ge(s_exp.h, ic - 1)
                            if kc == 0 and it2 >= 1:
                                t.wait_ge(s_dsm.h, it2)
                            if kc == 1 and it2 >= 1:
                                t.wait_ge(s_rsbs.h, it2)
                            ii = None
                            for mo in range(2):
                                ii = t.matmul(
                                    PS[:, (ic % 2) * 1024 + mo * 512:
                                       (ic % 2) * 1024 + (mo + 1) * 512],
                                    kmh[:, sl * 4096 + kc * 128: sl * 4096 + (kc + 1) * 128],
                                    qmh[:, sl * 2048 + qb * 1024 + mo * 512:
                                        sl * 2048 + qb * 1024 + (mo + 1) * 512],
                                    start=True, stop=True)
                            s_smm.inc(ii)

                        def emit_av(kc):
                            nonlocal ii_o
                            ic = it2 * 32 + kc
                            t.wait_ge(s_exp.h, ic + 1)
                            for mo in range(2):
                                ii_o = t.matmul(
                                    PS[:, 2048 + mo * 512: 2048 + (mo + 1) * 512],
                                    vh[:, sl * 4096 + kc * 128: sl * 4096 + (kc + 1) * 128],
                                    pb[:, (ic % 2) * 1024 + mo * 512:
                                       (ic % 2) * 1024 + (mo + 1) * 512],
                                    start=(kc == 0), stop=(kc == 31))

                        emit_score(0)
                        for kc in range(1, 32):
                            emit_score(kc)
                            emit_av(kc - 1)
                        emit_av(31)
                        s_omm.inc(ii_o)
                        t.wait_ge(s_dbf.h, it2 + 1)
                        t.matmul(PS[0:1, 0:512], ones_bf[:, :],
                                 dacc[:, 0:512], start=True, stop=True)
                        s_dmm.inc(t.matmul(PS[0:1, 512:1024], ones_bf[:, :],
                                           dacc[:, 512:1024], start=True, stop=True))
                        t.wait_ge(s_rc.h, it2 + 1)
                        t.matmul(PS[:, 1024:1536], ones_row[:, :],
                                 recbf[:, 0:512], start=True, stop=True)
                        s_rmm.inc(t.matmul(PS[:, 1536:2048], ones_row[:, :],
                                           recbf[:, 512:1024], start=True, stop=True))

        @ph.block.scalar
        def _(s):
            for h in range(16):
                for qb in range(2):
                    for m in range(2):
                        it2 = its(h, qb, m)
                        for kc in range(32):
                            ic = it2 * 32 + kc
                            s.wait_ge(s_smm.h, ic + 1)
                            if ic >= 2:
                                s.wait_ge(s_dac.h, ic - 1)
                            s_exp.inc(s.activation(
                                pb[:, (ic % 2) * 1024:(ic % 2 + 1) * 1024],
                                PS[:, (ic % 2) * 1024:(ic % 2 + 1) * 1024],
                                AF.Exp, scale=INVSQ_HD))
                        s.wait_ge(s_dmm.h, it2 + 1)
                        s_dsm.inc(s.copy(dsum[:, :], PS[0:1, 0:1024]))
                        s.wait_ge(s_rmm.h, it2 + 1)
                        if it2 >= 1:
                            s.wait_ge(s_rsd.h, it2)
                        s_rsbs.inc(s.copy(rsb[:, :], PS[:, 1024:2048]))

        @ph.block.vector
        def _(v):
            for h in range(16):
                sl = h % 2
                for g4 in range(8):
                    G = h * 8 + g4
                    v.wait_ge(s_vtp.h, G + 1)
                    if h >= 2 and g4 == 0:
                        v.wait_ge(s_omm.h, (h - 1) * 4)
                    ii = None
                    for j in range(4):
                        kc = g4 * 4 + j
                        ii = v.tensor_copy(
                            vh[:, sl * 4096 + kc * 128: sl * 4096 + (kc + 1) * 128],
                            psbf[:, (G % 2) * 1024 + j * 128:
                                 (G % 2) * 1024 + (j + 1) * 128])
                    s_vev2.inc(ii)
                for qb in range(2):
                    for m in range(2):
                        it2 = its(h, qb, m)
                        for kc in range(32):
                            ic = it2 * 32 + kc
                            v.wait_ge(s_exp.h, ic + 1)
                            if kc == 0:
                                if it2 >= 1:
                                    v.wait_ge(s_dmm.h, it2)
                                edge(v, vseq, v.tensor_copy(
                                    dacc[:, :], pb[:, (ic % 2) * 1024:(ic % 2 + 1) * 1024]))
                            else:
                                edge(v, vseq, v.tensor_tensor(
                                    dacc[:, :], dacc[:, :],
                                    pb[:, (ic % 2) * 1024:(ic % 2 + 1) * 1024], ALU.add))
                            v.sem_inc(s_dac.h, 1)
                            s_dac.n += 1
                        v.sem_inc(s_dbf.h, 1)
                        s_dbf.n += 1
                        v.wait_ge(s_dsm.h, it2 + 1)
                        if it2 >= 1:
                            v.wait_ge(s_rmm.h, it2)
                        edge(v, vseq, v.reciprocal(rec[:, :], dsum[:, :]))
                        s_rc.inc(v.tensor_copy(recbf[:, :], rec[:, :]))
                        v.wait_ge(s_rsbs.h, it2 + 1)
                        v.wait_ge(s_omm.h, it2 + 1)
                        if m == 0:
                            edge(v, vseq, v.tensor_tensor(out1[:, :], PS[:, 2048:3072],
                                                          rsb[:, :], ALU.mult))
                            v.sem_inc(s_pod.h, 1)
                            s_pod.n += 1
                            v.sem_inc(s_rsd.h, 1)
                            s_rsd.n += 1
                        else:
                            itq = h * 2 + qb
                            edge(v, vseq, v.tensor_tensor(t2b[:, :], PS[:, 2048:3072],
                                                          rsb[:, :], ALU.mult))
                            v.sem_inc(s_pod.h, 1)
                            s_pod.n += 1
                            v.sem_inc(s_rsd.h, 1)
                            s_rsd.n += 1
                            if itq >= 2:
                                v.wait_ge(s_ast[itq % 2].h, (itq // 2) * 16)
                            edge(v, vseq, v.scalar_tensor_tensor(
                                atb[:, (itq % 2) * 1024:(itq % 2 + 1) * 1024],
                                t2b[:, :], lam[:, h:h + 1], out1[:, :],
                                ALU.mult, ALU.add))
                            v.sem_inc(s_nrm.h, 1)
                            s_nrm.n += 1

        st3.close()
        owt = ph.sbuf("owt", [128, 16 * 2048], BF16)
        ownat = ph.sbuf("ownat", [128, 2 * 2048], BF16)
        atc = ph.sbuf("atcC", [128, 2 * 2048], BF16)
        fbuf = ph.sbuf("fbufC", [128, 2 * 2048], F32)
        lnf = ph.sbuf("lnfC", [128, 2048], F32)
        xo = ph.sbuf("xoC", [128, 2 * 2048], F32)
        obuf = ph.sbuf("obufC", [128, 2 * 2048], F32)
        fb6 = ph.sbuf("fb6C", [128, 24], F32)
        fmv = ph.sbuf("fmvC", [128, 2], F32)
        frs = ph.sbuf("frsC", [128, 1], F32)
        fnm = ph.sbuf("fnmC", [128, 1], F32)

        s_ownl = [ph.sem("ownl0"), ph.sem("ownl1")]
        s_owtp = ph.sem("owtp")
        s_owev = ph.sem("owev")
        s_atc = [ph.sem("atcC0"), ph.sem("atcC1")]
        s_xoC = [ph.sem("xoC0"), ph.sem("xoC1")]
        s_fmm = ph.sem("fmmC")
        s_fev = ph.sem("fevC")
        s_fbn = ph.sem("fbnC")
        s_fmv = ph.sem("fmvsC")
        s_frs = ph.sem("frssC")
        s_fnm = ph.sem("fnmsC")
        s_fap = ph.sem("fapC")
        s_res = ph.sem("resC")
        s_ost = [ph.sem("ostC0"), ph.sem("ostC1")]

        @ph.block.gpsimd
        def _(g):
            g.wait_ge(s_ast[0].h, 16 * 16)
            g.wait_ge(s_ast[1].h, 16 * 16)
            for dr in range(16):
                sl = dr % 2
                if dr >= 2:
                    g.wait_ge(s_owtp.h, 4 * (dr - 1))
                s_ownl[sl].inc(g.dma_start(
                    out=ownat[:, sl * 2048:(sl + 1) * 2048],
                    in_=io["ow"][dr * 128:(dr + 1) * 128, :]), 16)

        @ph.block.tensor
        def _(t):
            for dr in range(16):
                sl = dr % 2
                t.wait_ge(s_ownl[sl].h, (dr // 2 + 1) * 16)
                for g4 in range(4):
                    G = dr * 4 + g4
                    if G >= 2:
                        t.wait_ge(s_owev.h, G - 1)
                    ii = None
                    for j in range(4):
                        hc = g4 * 4 + j
                        ii = t.transpose(psbf[:, (G % 2) * 1024 + j * 128:
                                              (G % 2) * 1024 + (j + 1) * 128],
                                         ownat[:, sl * 2048 + hc * 128:
                                               sl * 2048 + (hc + 1) * 128],
                                         ident[:, :])
                    s_owtp.inc(ii)
            for tt in range(16):
                slt = tt % 2
                t.wait_ge(s_atc[slt].h, (tt // 2 + 1) * 16)
                if tt == 0:
                    t.wait_ge(s_owev.h, 64)
                if tt >= 1:
                    t.wait_ge(s_fev.h, tt)
                ii = None
                for hc in range(16):
                    for dc in range(4):
                        ii = t.matmul(PS[:, dc * 512:(dc + 1) * 512],
                                      atc[:, slt * 2048 + hc * 128: slt * 2048 + (hc + 1) * 128],
                                      owt[:, hc * 2048 + dc * 512: hc * 2048 + (dc + 1) * 512],
                                      start=(hc == 0), stop=(hc == 15))
                s_fmm.inc(ii)

        @ph.block.vector
        def _(v):
            for dr in range(16):
                sl = dr % 2
                for g4 in range(4):
                    G = dr * 4 + g4
                    v.wait_ge(s_owtp.h, G + 1)
                    ii = None
                    for j in range(4):
                        hc = g4 * 4 + j
                        ii = v.tensor_copy(
                            owt[:, hc * 2048 + dr * 128: hc * 2048 + (dr + 1) * 128],
                            psbf[:, (G % 2) * 1024 + j * 128:
                                 (G % 2) * 1024 + (j + 1) * 128])
                    s_owev.inc(ii)
            for tt in range(16):
                slt = tt % 2
                v.wait_ge(s_fev.h, tt + 1)
                ii = None
                for j in range(4):
                    ii = v.bn_stats(fb6[:, j * 6:(j + 1) * 6],
                                    fbuf[:, slt * 2048 + j * 512: slt * 2048 + (j + 1) * 512])
                s_fbn.inc(ii)
                v.wait_ge(s_fbn.h, tt + 1)
                s_fmv.inc(v.bn_aggr(fmv[:, :], fb6[:, :]))
                v.wait_ge(s_fmv.h, tt + 1)
                v.wait_ge(s_frs.h, tt + 1)
                s_fnm.inc(v.tensor_scalar(fnm[:, :], fmv[:, 0:1], frs[:, :], -1.0,
                                          ALU.mult, ALU.mult))
                v.wait_ge(s_fap.h, tt + 1)
                v.wait_ge(s_xoC[slt].h, (tt // 2 + 1) * 16)
                if tt >= 2:
                    v.wait_ge(s_ost[slt].h, (tt // 2) * 16)
                s_res.inc(v.tensor_tensor(obuf[:, slt * 2048:(slt + 1) * 2048],
                                          lnf[:, :], xo[:, slt * 2048:(slt + 1) * 2048],
                                          ALU.add))

        @ph.block.scalar
        def _(s):
            for tt in range(16):
                slt = tt % 2
                s.wait_ge(s_fmm.h, tt + 1)
                if tt >= 2:
                    s.wait_ge(s_fbn.h, tt - 1)
                    s.wait_ge(s_fap.h, tt - 1)
                s_fev.inc(s.copy(fbuf[:, slt * 2048:(slt + 1) * 2048], PS[:, 0:2048]))
                s.wait_ge(s_fmv.h, tt + 1)
                edge(s, sseq, s.activation(frs[:, :], fmv[:, 1:2], AF.Ln, bias=eps[:, :]))
                s_frs.inc(s.activation(frs[:, :], frs[:, :], AF.Exp, scale=-0.5))
                s.wait_ge(s_frs.h, tt + 1)
                s.wait_ge(s_fnm.h, tt + 1)
                if tt >= 1:
                    s.wait_ge(s_res.h, tt)
                s_fap.inc(s.activation(lnf[:, :], fbuf[:, slt * 2048:(slt + 1) * 2048],
                                       AF.Identity, scale=frs[:, :], bias=fnm[:, :]))

        @ph.block.sync
        def _(sync):
            sync.wait_ge(s_ast[0].h, 16 * 16)
            sync.wait_ge(s_ast[1].h, 16 * 16)
            for tt in range(16):
                slt = tt % 2
                if tt >= 2:
                    sync.wait_ge(s_fmm.h, tt - 1)
                s_atc[slt].inc(sync.dma_start(
                    out=atc[:, slt * 2048:(slt + 1) * 2048]
                    .rearrange("p (hc t) -> p hc t", t=128),
                    in_=io["at3c"][:, tt].rearrange("hc p t -> p hc t")), 16)
                if tt >= 2:
                    sync.wait_ge(s_res.h, tt - 1)
                s_xoC[slt].inc(sync.dma_start(
                    out=xo[:, slt * 2048:(slt + 1) * 2048],
                    in_=io["x2"][tt * 128:(tt + 1) * 128, :]), 16)
                sync.wait_ge(s_res.h, tt + 1)
                s_ost[slt].inc(sync.dma_start(
                    out=io["x3"][tt * 128:(tt + 1) * 128, :],
                    in_=obuf[:, slt * 2048:(slt + 1) * 2048]), 16)
            sync.wait_ge(s_ost[0].h, 8 * 16)
            sync.wait_ge(s_ost[1].h, 8 * 16)
    return nc



def build_phase_d(nc, io, use_cc=True):
    pairs = [[0, 1], [2, 3], [4, 5], [6, 7]]
    with ExitStack() as stack:
        block = stack.enter_context(nc.Block())
        ph = Phase(nc, stack, block, "Dp")

        ident = ph.sbuf("ident", [128, 128], BF16)
        eps = ph.sbuf("eps", [128, 1], F32)
        ones_f = ph.sbuf("onesf", [128, 1], F32)
        onesr_f = ph.sbuf("onesrf", [1, 128], F32)
        psbf = ph.psum("psbf", [128, 2048], BF16)
        PS = ph.psum("ps", [128, 3072], F32)
        vseq = ph.sem("vseq")
        sseq = ph.sem("sseq")
        s_pre = ph.sem("pre")

        @ph.block.sync
        def _(sync):
            s_pre.inc(sync.dma_start(out=ident[:, :], in_=io["ident"][:, :]), 16)

        @ph.block.vector
        def _(v):
            v.memset(eps[:, :], 1e-5)
            v.memset(ones_f[:, :], 1.0)
            v.memset(onesr_f[:, :], 1.0)
            v.wait_ge(s_pre.h, 16)

        sr = ph.sbuf("sr", [128, 16], F32)
        s2 = ph.sbuf("s2", [128, 1], F32)
        gsc = ph.sbuf("gsc", [128, 512], BF16)
        simb = ph.sbuf("simb", [128, 16 * 32], F32)
        sqs = ph.sbuf("sqs", [128, 16 * 32], F32)
        smallf = ph.sbuf("smallf", [1, 64], F32)
        small2 = ph.sbuf("small2", [1, 40], F32)
        lgrow = ph.sbuf("lgrow", [1, 32], F32)
        lgc = ph.sbuf("lgc", [32, 1], F32)
        lrow = ph.sbuf("lrow", [32, 32], F32)
        m1 = ph.sbuf("m1", [32, 32], F32)
        e2 = ph.sbuf("e2", [32, 32], F32)
        ltm = ph.sbuf("ltm", [32, 32], F32)
        rankc = ph.sbuf("rankc", [32, 1], F32)
        fbc = ph.sbuf("fbc", [32, 1], F32)
        maskc = ph.sbuf("maskc", [32, 1], F32)
        gatedc = ph.sbuf("gatedc", [32, 1], F32)
        infb = ph.sbuf("infb", [32, 1], F32)
        zc = ph.sbuf("zc", [32, 1], F32)
        ezc = ph.sbuf("ezc", [32, 1], F32)
        rzb = ph.sbuf("rzb", [32, 1], F32)
        routc = ph.sbuf("routc", [32, 1], F32)
        gtsb = ph.sbuf("gtsb", [1, 32], F32)
        sgt = ph.sbuf("sgt", [1, 32], F32)
        rexp = ph.sbuf("rexp", [128, 16], F32)
        ir2 = ph.sbuf("ir2", [128, 64], BF16)
        pat = ph.sbuf("pat", [128, 16 * 64], BF16)
        selb = ph.sbuf("selb", [32, 2048], F32)

        st0 = ExitStack()
        stLN = ExitStack()
        xhat = ph.sbuf("xhatD", [128, 16 * 2048], BF16, st0)

        def dstd(ci, tc):
            return xhat[:, ci * 2048 + tc * 128: ci * 2048 + (tc + 1) * 128]

        lns = emit_ln_xhat(ph, src_hbm=io["x3"], n_tc=16, dst_tiles=dstd,
                           ident=ident, eps=eps, psbf=psbf, name="lnd",
                           bufstack=stLN)
        s_ev = lns["ev"]
        EV_ALL = s_ev.n
        stLN.close()

        s_srp = ph.sem("srp")
        s_sst = ph.sem("sst")
        s_cc = ph.sem("cc")
        s_srl = ph.sem("srl")

        @ph.block.vector
        def _(v):
            v.wait_ge(s_ev.h, EV_ALL)
            for ci in range(16):
                edge(v, vseq, v.reduce_sum(sr[:, ci:ci + 1],
                                           xhat[:, ci * 2048:(ci + 1) * 2048],
                                           mybir.AxisListType.X))
            v.sem_inc(s_srp.h, 1)
            s_srp.n += 1

        @ph.block.sync
        def _(sync):
            sync.wait_ge(s_srp.h, 1)
            s_sst.inc(sync.dma_start(out=io["srp"][:, :], in_=sr[:, :]), 16)

        @ph.block.gpsimd
        def _(g):
            g.wait_ge(s_sst.h, 16)
            if use_cc:
                s_cc.inc(g.collective_compute(
                    "AllReduce", ALU.add, replica_groups=pairs,
                    ins=[io["srp_t"].ap().opt()], outs=[io["srg_t"].ap().opt()]))
            else:
                g.sem_inc(s_cc.h, 1)
                s_cc.n += 1

        @ph.block.sync
        def _(sync):
            sync.wait_ge(s_cc.h, 1)
            s_srl.inc(sync.dma_start(out=sr[:, :], in_=io["srg"][:, :]), 16)
            s_srl.inc(sync.dma_start(
                out=simb[:, :].rearrange("p (ci e) -> p ci e", e=32),
                in_=io["sim"].rearrange("(ci p) e -> p ci e", p=128)), 16)
            s_srl.inc(sync.dma_start(out=gtsb[:, :], in_=io["gates"][:, :]), 16)
            s_srl.inc(sync.dma_start(out=ltm[:, :], in_=io["ltm"][:, :]), 16)
            s_srl.inc(sync.dma_start(out=ir2[:, :], in_=io["ir2"][:, :]), 16)
            s_srl.inc(sync.dma_start(out=selb[:, :], in_=io["selm"][:, :]), 16)

        wnat = ph.sbuf("wnatD", [128, 2 * 2048], BF16, st0)
        wtb = ph.sbuf("wtbD", [128, 2 * 2048], BF16, st0)
        vat = ph.sbuf("vat", [128, 16 * 2048], BF16, st0)

        s_wn = [ph.sem("wnD0"), ph.sem("wnD1")]
        s_wtp = ph.sem("wtpD")
        s_wev = ph.sem("wevD")
        s_vmm = ph.sem("vmmD")
        s_vev = ph.sem("vevD")

        @ph.block.gpsimd
        def _(g):
            g.wait_ge(s_ev.h, EV_ALL)
            for ut in range(16):
                sl = ut % 2
                if ut >= 2:
                    g.wait_ge(s_wtp.h, 4 * (ut - 1))
                s_wn[sl].inc(g.dma_start(
                    out=wnat[:, sl * 2048:(sl + 1) * 2048],
                    in_=io["wv"][ut * 128:(ut + 1) * 128, :]), 16)

        @ph.block.tensor
        def _(t):
            t.wait_ge(s_ev.h, EV_ALL)
            for ut in range(16):
                sl = ut % 2
                t.wait_ge(s_wn[sl].h, (ut // 2 + 1) * 16)
                for g4 in range(4):
                    G = ut * 4 + g4
                    if G >= 2:
                        t.wait_ge(s_wev.h, G - 1)
                    ii = None
                    for j in range(4):
                        ci = g4 * 4 + j
                        ii = t.transpose(psbf[:, (G % 2) * 1024 + j * 128:
                                              (G % 2) * 1024 + (j + 1) * 128],
                                         wnat[:, sl * 2048 + ci * 128:
                                              sl * 2048 + (ci + 1) * 128],
                                         ident[:, :])
                    s_wtp.inc(ii)
                t.wait_ge(s_wev.h, 4 * (ut + 1))
                if ut >= 1:
                    t.wait_ge(s_vev.h, ut)
                ii = None
                for ci in range(16):
                    for mo in range(4):
                        ii = t.matmul(PS[:, mo * 512:(mo + 1) * 512],
                                      wtb[:, sl * 2048 + ci * 128: sl * 2048 + (ci + 1) * 128],
                                      xhat[:, ci * 2048 + mo * 512: ci * 2048 + (mo + 1) * 512],
                                      start=(ci == 0), stop=(ci == 15))
                s_vmm.inc(ii)

        @ph.block.vector
        def _(v):
            for ut in range(16):
                sl = ut % 2
                for g4 in range(4):
                    G = ut * 4 + g4
                    v.wait_ge(s_wtp.h, G + 1)
                    if ut >= 2 and g4 == 0:
                        v.wait_ge(s_vmm.h, ut - 1)
                    ii = None
                    for j in range(4):
                        ci = g4 * 4 + j
                        ii = v.tensor_copy(
                            wtb[:, sl * 2048 + ci * 128: sl * 2048 + (ci + 1) * 128],
                            psbf[:, (G % 2) * 1024 + j * 128:
                                 (G % 2) * 1024 + (j + 1) * 128])
                    s_wev.inc(ii)
                v.wait_ge(s_vmm.h, ut + 1)
                s_vev.inc(v.tensor_copy(vat[:, ut * 2048:(ut + 1) * 2048], PS[:, 0:2048]))

        s_g1 = ph.sem("g1")
        s_g2 = ph.sem("g2")
        s_g3 = ph.sem("g3")
        s_g4 = ph.sem("g4")
        s_lgst = ph.sem("lgst")
        s_lgld = ph.sem("lgld")
        s_g5 = ph.sem("g5")
        s_g6 = ph.sem("g6")
        s_g7 = ph.sem("g7")
        s_g8 = ph.sem("g8")
        s_rex = ph.sem("rex")
        s_pat = ph.sem("pat")

        @ph.block.scalar
        def _(s):
            s.wait_ge(s_srl.h, 6 * 16)
            edge(s, sseq, s.activation(gsc[:, 0:16], sr[:, :], AF.Square,
                                       accum_out=s2[:, :]))
            s_g1.inc(s.activation(sqs[:, :], simb[:, :], AF.Square))

        @ph.block.tensor
        def _(t):
            t.wait_ge(s_g1.h, 1)
            t.matmul(PS[0:1, 2048:2049], ones_f[:, :], s2[:, :], start=True, stop=True)
            for ci in range(16):
                t.matmul(PS[0:1, 2049:2081], sr[:, ci:ci + 1],
                         simb[:, ci * 32:(ci + 1) * 32],
                         start=(ci == 0), stop=(ci == 15))
            ii = None
            for ci in range(16):
                ii = t.matmul(PS[0:1, 2081:2113], ones_f[:, :],
                              sqs[:, ci * 32:(ci + 1) * 32],
                              start=(ci == 0), stop=(ci == 15))
            s_g2.inc(ii)

        @ph.block.scalar
        def _(s):
            s.wait_ge(s_g2.h, 1)
            edge(s, sseq, s.copy(small2[0:1, 0:1], PS[0:1, 2048:2049]))
            edge(s, sseq, s.copy(smallf[0:1, 0:32], PS[0:1, 2049:2081]))
            edge(s, sseq, s.copy(smallf[0:1, 32:64], PS[0:1, 2081:2113]))
            edge(s, sseq, s.activation(small2[0:1, 1:2], small2[0:1, 0:1], AF.Ln))
            edge(s, sseq, s.activation(small2[0:1, 1:2], small2[0:1, 1:2], AF.Exp, scale=-0.5))
            edge(s, sseq, s.activation(small2[0:1, 2:34], smallf[0:1, 32:64], AF.Ln))
            edge(s, sseq, s.activation(small2[0:1, 2:34], small2[0:1, 2:34], AF.Exp, scale=-0.5))
            s_g3.inc(s.activation(sgt[0:1, :], gtsb[0:1, :], AF.Sigmoid))

        @ph.block.vector
        def _(v):
            v.wait_ge(s_g3.h, 1)
            edge(v, vseq, v.tensor_scalar(lgrow[0:1, :], smallf[0:1, 0:32],
                                          small2[0:1, 1:2], None, ALU.mult))
            edge(v, vseq, v.tensor_tensor(lgrow[0:1, :], lgrow[0:1, :],
                                          small2[0:1, 2:34], ALU.mult))
            edge(v, vseq, v.tensor_tensor(lgrow[0:1, :], lgrow[0:1, :],
                                          sgt[0:1, :], ALU.subtract))
            v.sem_inc(s_g4.h, 1)
            s_g4.n += 1

        @ph.block.sync
        def _(sync):
            sync.wait_ge(s_g4.h, 1)
            s_lgst.inc(sync.dma_start(out=io["lgd"][:, :], in_=lgrow[0:1, :]), 16)
            sync.wait_ge(s_lgst.h, 16)
            s_lgld.inc(sync.dma_start(out=lgc[:, :],
                                      in_=io["lgd"].rearrange("one e -> e one")), 16)

        @ph.block.tensor
        def _(t):
            t.wait_ge(s_g4.h, 1)
            s_g5.inc(t.matmul(PS[0:32, 2048:2080], onesr_f[0:1, 0:32],
                              lgrow[0:1, :], start=True, stop=True))

        @ph.block.scalar
        def _(s):
            s.wait_ge(s_g5.h, 1)
            s_g6.inc(s.copy(lrow[:, :], PS[0:32, 2048:2080]))

        @ph.block.vector
        def _(v):
            v.wait_ge(s_g6.h, 1)
            v.wait_ge(s_lgld.h, 16)
            edge(v, vseq, v.tensor_scalar(m1[:, :], lrow[:, :], lgc[:, :], None, ALU.is_gt))
            edge(v, vseq, v.tensor_scalar(e2[:, :], lrow[:, :], lgc[:, :], None, ALU.is_equal))
            edge(v, vseq, v.tensor_tensor(e2[:, :], e2[:, :], ltm[:, :], ALU.mult))
            edge(v, vseq, v.tensor_tensor(m1[:, :], m1[:, :], e2[:, :], ALU.add))
            edge(v, vseq, v.reduce_sum(rankc[:, :], m1[:, :], mybir.AxisListType.X))
            edge(v, vseq, v.tensor_scalar(fbc[:, :], rankc[:, :], 16.0, None, ALU.is_lt))
            edge(v, vseq, v.tensor_scalar(gatedc[:, :], lgc[:, :], 0.0, None, ALU.max))
            edge(v, vseq, v.tensor_scalar(maskc[:, :], lgc[:, :], 0.0, None, ALU.is_gt))
            v.sem_inc(s_g7.h, 1)
            s_g7.n += 1

        @ph.block.tensor
        def _(t):
            t.wait_ge(s_g7.h, 1)
            s_g8.inc(t.matmul(PS[0:1, 2080:2081], ones_f[0:32, :], maskc[:, :],
                              start=True, stop=True))

        @ph.block.scalar
        def _(s):
            s.wait_ge(s_g8.h, 1)
            edge(s, sseq, s.copy(small2[0:1, 34:35], PS[0:1, 2080:2081]))
            s.sem_inc(s_g8.h, 1)
            s_g8.n += 1

        @ph.block.vector
        def _(v):
            v.wait_ge(s_g8.h, 2)
            edge(v, vseq, v.tensor_scalar(small2[0:1, 35:36], small2[0:1, 34:35],
                                          0.0, None, ALU.is_equal))
            v.sem_inc(s_g8.h, 1)
            s_g8.n += 1

        @ph.block.tensor
        def _(t):
            t.wait_ge(s_g8.h, 3)
            s_g8.inc(t.matmul(PS[0:32, 2081:2082], onesr_f[0:1, 0:32],
                              small2[0:1, 35:36], start=True, stop=True))

        @ph.block.scalar
        def _(s):
            s.wait_ge(s_g8.h, 4)
            s_g8.inc(s.copy(infb[:, :], PS[0:32, 2081:2082]))

        @ph.block.vector
        def _(v):
            v.wait_ge(s_g8.h, 5)
            edge(v, vseq, v.tensor_tensor(zc[:, :], infb[:, :], fbc[:, :], ALU.mult))
            edge(v, vseq, v.tensor_tensor(zc[:, :], maskc[:, :], zc[:, :], ALU.add))
            edge(v, vseq, v.tensor_scalar(zc[:, :], zc[:, :], 1.0, None, ALU.min))
            edge(v, vseq, v.tensor_tensor(ezc[:, :], gatedc[:, :], zc[:, :], ALU.mult))
            edge(v, vseq, v.tensor_scalar(zc[:, :], zc[:, :], 3.4e38, -3.4e38,
                                          ALU.mult, ALU.add))
            edge(v, vseq, v.tensor_tensor(zc[:, :], ezc[:, :], zc[:, :], ALU.add))
            v.sem_inc(s_g8.h, 1)
            s_g8.n += 1

        @ph.block.scalar
        def _(s):
            s.wait_ge(s_g8.h, 6)
            s_g8.inc(s.activation(ezc[:, :], zc[:, :], AF.Exp))

        @ph.block.tensor
        def _(t):
            t.wait_ge(s_g8.h, 7)
            s_g8.inc(t.matmul(PS[0:1, 2082:2083], ones_f[0:32, :], ezc[:, :],
                              start=True, stop=True))

        @ph.block.scalar
        def _(s):
            s.wait_ge(s_g8.h, 8)
            s_g8.inc(s.copy(small2[0:1, 36:37], PS[0:1, 2082:2083]))

        @ph.block.vector
        def _(v):
            v.wait_ge(s_g8.h, 9)
            edge(v, vseq, v.reciprocal(small2[0:1, 37:38], small2[0:1, 36:37]))
            v.sem_inc(s_g8.h, 1)
            s_g8.n += 1

        @ph.block.tensor
        def _(t):
            t.wait_ge(s_g8.h, 10)
            s_g8.inc(t.matmul(PS[0:32, 2083:2084], onesr_f[0:1, 0:32],
                              small2[0:1, 37:38], start=True, stop=True))

        @ph.block.scalar
        def _(s):
            s.wait_ge(s_g8.h, 11)
            s_g8.inc(s.copy(rzb[:, :], PS[0:32, 2083:2084]))

        @ph.block.vector
        def _(v):
            v.wait_ge(s_g8.h, 12)
            edge(v, vseq, v.tensor_tensor(routc[:, :], ezc[:, :], rzb[:, :], ALU.mult))
            v.sem_inc(s_g8.h, 1)
            s_g8.n += 1

        @ph.block.tensor
        def _(t):
            t.wait_ge(s_g8.h, 13)
            ii = None
            for ci in range(16):
                ii = t.matmul(PS[:, 2084 + ci:2084 + ci + 1], selb[:, ci * 128:(ci + 1) * 128],
                              routc[:, :], start=True, stop=True)
            s_rex.inc(ii)

        @ph.block.scalar
        def _(s):
            s.wait_ge(s_rex.h, 1)
            s_rex.inc(s.copy(rexp[:, :], PS[:, 2084:2100]))

        @ph.block.vector
        def _(v):
            v.wait_ge(s_rex.h, 2)
            ii = None
            for ci in range(16):
                ii = v.tensor_scalar(pat[:, ci * 64:(ci + 1) * 64], ir2[:, :],
                                     rexp[:, ci:ci + 1], None, ALU.mult)
            s_pat.inc(ii)

        st0.close()
        owb = ph.sbuf("owbD", [128, 2 * 2048], BF16)
        wod = ph.sbuf("wod", [64, 2048], BF16)
        comb = ph.sbuf("comb", [64, 2048], BF16)
        fbuf = ph.sbuf("fbufD", [128, 2 * 2048], F32)
        lnf = ph.sbuf("lnfD", [128, 2048], F32)
        xo = ph.sbuf("xoD", [128, 2 * 2048], F32)
        obuf = ph.sbuf("obufD", [128, 2 * 2048], F32)
        fb6 = ph.sbuf("fb6D", [128, 24], F32)
        fmv = ph.sbuf("fmvD", [128, 2], F32)
        frs = ph.sbuf("frsD", [128, 1], F32)
        fnm = ph.sbuf("fnmD", [128, 1], F32)

        s_owl = [ph.sem("owl0"), ph.sem("owl1")]
        s_wodm = ph.sem("wodm")
        s_wode = ph.sem("wode")
        s_cbm = ph.sem("cbm")
        s_cbe = ph.sem("cbe")
        s_fmm = ph.sem("fmmD")
        s_fev = ph.sem("fevD")
        s_fbn = ph.sem("fbnD")
        s_fmv = ph.sem("fmvsD")
        s_frs = ph.sem("frssD")
        s_fnm = ph.sem("fnmsD")
        s_fap = ph.sem("fapD")
        s_res = ph.sem("resD")
        s_ost = [ph.sem("ostD0"), ph.sem("ostD1")]
        s_xoD = [ph.sem("xoD0"), ph.sem("xoD1")]

        @ph.block.gpsimd
        def _(g):
            g.wait_ge(s_vmm.h, 16)
            for ci in range(16):
                sl = ci % 2
                if ci >= 2:
                    g.wait_ge(s_wodm.h, ci - 1)
                s_owl[sl].inc(g.dma_start(
                    out=owb[:, sl * 2048:(sl + 1) * 2048],
                    in_=io["ow"][ci * 128:(ci + 1) * 128, :]), 16)

        @ph.block.tensor
        def _(t):
            t.wait_ge(s_pat.h, 1)
            t.wait_ge(s_vev.h, 16)
            for ci in range(16):
                sl = ci % 2
                t.wait_ge(s_owl[sl].h, (ci // 2 + 1) * 16)
                ii = None
                for mo in range(4):
                    ii = t.matmul(PS[0:64, mo * 512:(mo + 1) * 512],
                                  pat[:, ci * 64:(ci + 1) * 64],
                                  owb[:, sl * 2048 + mo * 512: sl * 2048 + (mo + 1) * 512],
                                  start=(ci == 0), stop=(ci == 15))
                s_wodm.inc(ii)
            t.wait_ge(s_vev.h, 16)
            t.wait_ge(s_wode.h, 1)
            ii = None
            for ci in range(16):
                for mo in range(4):
                    ii = t.matmul(PS[0:64, mo * 512:(mo + 1) * 512],
                                  pat[:, ci * 64:(ci + 1) * 64],
                                  vat[:, ci * 2048 + mo * 512: ci * 2048 + (mo + 1) * 512],
                                  start=(ci == 0), stop=(ci == 15))
            s_cbm.inc(ii)
            t.wait_ge(s_cbe.h, 1)
            for tt in range(16):
                if tt >= 1:
                    t.wait_ge(s_fev.h, tt)
                ii = None
                for dc in range(4):
                    ii = t.matmul(PS[:, dc * 512:(dc + 1) * 512],
                                  comb[0:64, tt * 128:(tt + 1) * 128],
                                  wod[0:64, dc * 512:(dc + 1) * 512],
                                  start=True, stop=True)
                s_fmm.inc(ii)

        @ph.block.vector
        def _(v):
            v.wait_ge(s_wodm.h, 16)
            s_wode.inc(v.tensor_copy(wod[:, :], PS[0:64, 0:2048]))
            v.wait_ge(s_cbm.h, 1)
            s_cbe.inc(v.tensor_copy(comb[:, :], PS[0:64, 0:2048]))
            for tt in range(16):
                slt = tt % 2
                v.wait_ge(s_fev.h, tt + 1)
                ii = None
                for j in range(4):
                    ii = v.bn_stats(fb6[:, j * 6:(j + 1) * 6],
                                    fbuf[:, slt * 2048 + j * 512: slt * 2048 + (j + 1) * 512])
                s_fbn.inc(ii)
                v.wait_ge(s_fbn.h, tt + 1)
                s_fmv.inc(v.bn_aggr(fmv[:, :], fb6[:, :]))
                v.wait_ge(s_fmv.h, tt + 1)
                v.wait_ge(s_frs.h, tt + 1)
                s_fnm.inc(v.tensor_scalar(fnm[:, :], fmv[:, 0:1], frs[:, :], -1.0,
                                          ALU.mult, ALU.mult))
                v.wait_ge(s_fap.h, tt + 1)
                v.wait_ge(s_xoD[slt].h, (tt // 2 + 1) * 16)
                if tt >= 2:
                    v.wait_ge(s_ost[slt].h, (tt // 2) * 16)
                s_res.inc(v.tensor_tensor(obuf[:, slt * 2048:(slt + 1) * 2048],
                                          lnf[:, :], xo[:, slt * 2048:(slt + 1) * 2048],
                                          ALU.add))

        @ph.block.scalar
        def _(s):
            for tt in range(16):
                slt = tt % 2
                s.wait_ge(s_fmm.h, tt + 1)
                if tt >= 2:
                    s.wait_ge(s_fbn.h, tt - 1)
                    s.wait_ge(s_fap.h, tt - 1)
                s_fev.inc(s.copy(fbuf[:, slt * 2048:(slt + 1) * 2048], PS[:, 0:2048]))
                s.wait_ge(s_fmv.h, tt + 1)
                edge(s, sseq, s.activation(frs[:, :], fmv[:, 1:2], AF.Ln, bias=eps[:, :]))
                s_frs.inc(s.activation(frs[:, :], frs[:, :], AF.Exp, scale=-0.5))
                s.wait_ge(s_frs.h, tt + 1)
                s.wait_ge(s_fnm.h, tt + 1)
                if tt >= 1:
                    s.wait_ge(s_res.h, tt)
                s_fap.inc(s.activation(lnf[:, :], fbuf[:, slt * 2048:(slt + 1) * 2048],
                                       AF.Identity, scale=frs[:, :], bias=fnm[:, :]))

        @ph.block.sync
        def _(sync):
            sync.wait_ge(s_vev.h, 16)
            for tt in range(16):
                slt = tt % 2
                if tt >= 2:
                    sync.wait_ge(s_res.h, tt - 1)
                s_xoD[slt].inc(sync.dma_start(
                    out=xo[:, slt * 2048:(slt + 1) * 2048],
                    in_=io["x3"][tt * 128:(tt + 1) * 128, :]), 16)
                sync.wait_ge(s_res.h, tt + 1)
                s_ost[slt].inc(sync.dma_start(
                    out=io["out"][tt * 128:(tt + 1) * 128, :],
                    in_=obuf[:, slt * 2048:(slt + 1) * 2048]), 16)
            sync.wait_ge(s_ost[0].h, 8 * 16)
            sync.wait_ge(s_ost[1].h, 8 * 16)
    return nc



def emit_sem_clear_block(nc):
    with ExitStack() as es:
        handles = []
        for i in range(128):
            try:
                handles.append(es.enter_context(nc.semaphore(f"clr{i}")))
            except Exception:
                break
        with nc.Block() as blk:
            @blk.gpsimd
            def _(g):
                g.dma_reset()
                for h in handles:
                    if h.num in USED_SEM_NUMS:
                        g.sem_clear(h)
    USED_SEM_NUMS.clear()


def build_mega(use_cc=True):
    nc = bass.Bass("TRN2", target_bir_lowering=False)
    F, B = F32, BF16
    d = {}

    def inp(name, shape, dt):
        d[name] = nc.dram_tensor(name, shape, dt, kind="ExternalInput").ap()

    def scr(name, shape, dt):
        t = nc.dram_tensor(name, shape, dt)
        d[name] = t.ap()
        d[name + "_t"] = t

    inp("xin", [3072, 2048], F)
    inp("l0wq", [2048, 2048], F); inp("l0wk", [2048, 512], F)
    inp("l0wv", [2048, 512], F); inp("l0wo", [2048, 2048], F)
    inp("cost", [128, 3072], F); inp("sint", [128, 3072], F)
    inp("l0mask", [4, 12, 128, 512], B)
    inp("ident", [128, 128], B)
    inp("w1", [5632, 2048], F); inp("w2", [5632, 2048], F); inp("w3", [2048, 5632], F)
    for n in ["q1w", "k1w", "q2w", "k2w", "cvw", "cow"]:
        inp(n, [2048, 2048], F)
    inp("lam_b", [128, 16], F)
    inp("dwv", [2048, 2048], F); inp("dow", [2048, 2048], F)
    inp("sim", [2048, 32], F); inp("gates", [1, 32], F)
    inp("selm", [32, 2048], F); inp("ir2", [128, 64], B); inp("ltm", [32, 32], F)
    d["out"] = nc.dram_tensor("out", [2048, 2048], F, kind="ExternalOutput").ap()

    scr("x1", [2048, 2048], F); scr("x2", [2048, 2048], F); scr("x3", [2048, 2048], F)
    scr("qsc", [2048, 2048], B); scr("vsc", [3072, 512], B)
    scr("at3", [16, 16, 128, 128], B)
    scr("h3", [44, 16, 128, 128], B); scr("fsc", [2048, 2048], F)
    for n in ["kvc1", "kvc2", "kvcv", "qsc1", "qsc2"]:
        scr(n, [2048, 2048], B)
    for n in ["kvg1", "kvg2", "kvgv"]:
        scr(n, [4096, 2048], B)
    scr("at3c", [16, 16, 128, 128], B)
    scr("srp", [128, 16], F); scr("srg", [128, 16], F); scr("lgd", [1, 32], F)

    ioA = {"xin": d["xin"], "wq": d["l0wq"], "wk": d["l0wk"], "wv": d["l0wv"],
           "wo": d["l0wo"], "cost": d["cost"], "sint": d["sint"],
           "l0mask": d["l0mask"], "ident": d["ident"], "x1": d["x1"],
           "qsc": d["qsc"], "vsc": d["vsc"], "at3": d["at3"]}
    build_phase_a(nc, ioA)
    emit_sem_clear_block(nc)

    ioB = {"x1": d["x1"], "w1": d["w1"], "w2": d["w2"], "w3": d["w3"],
           "ident": d["ident"], "x2": d["x2"], "h3": d["h3"], "fsc": d["fsc"]}
    build_phase_b(nc, ioB)
    emit_sem_clear_block(nc)

    ioC = {"x2": d["x2"], "q1w": d["q1w"], "k1w": d["k1w"], "q2w": d["q2w"],
           "k2w": d["k2w"], "vw": d["cvw"], "ow": d["cow"], "lam_b": d["lam_b"],
           "ident": d["ident"], "x3": d["x3"], "at3c": d["at3c"]}
    for n in ["kvc1", "kvc2", "kvcv", "kvg1", "kvg2", "kvgv", "qsc1", "qsc2"]:
        ioC[n] = d[n]
        ioC[n + "_t"] = d[n + "_t"]
    build_phase_c_pre(nc, ioC)
    emit_sem_clear_block(nc)
    build_phase_c_post(nc, ioC)
    emit_sem_clear_block(nc)

    ioD = {"x3": d["x3"], "wv": d["dwv"], "ow": d["dow"], "sim": d["sim"],
           "gates": d["gates"], "selm": d["selm"], "ir2": d["ir2"],
           "ltm": d["ltm"], "ident": d["ident"], "out": d["out"],
           "srp": d["srp"], "srp_t": d["srp_t"], "srg": d["srg"],
           "srg_t": d["srg_t"], "lgd": d["lgd"]}
    build_phase_d(nc, ioD, use_cc=use_cc)
    return nc



_CONSTS = None


def host_consts():
    global _CONSTS
    import ml_dtypes
    if _CONSTS is None:
        selm = np.zeros((32, 2048), dtype=np.float32)
        for ci in range(16):
            for p in range(128):
                selm[2 * ci + p // 64, ci * 128 + p] = 1.0
        ir2 = np.concatenate([np.eye(64), np.eye(64)], axis=0).astype(ml_dtypes.bfloat16)
        ltm = (np.arange(32)[None, :] < np.arange(32)[:, None]).astype(np.float32)
        ident = np.eye(128, dtype=ml_dtypes.bfloat16)
        _CONSTS = (selm, ir2, ltm, ident)
    return _CONSTS


def host_core_inputs(inputs, core):
    selm, ir2, ltm, ident = host_consts()
    b, half = core // 2, core % 2
    x = np.asarray(inputs["L0_attn_hidden_states"])
    xin = np.zeros((3072, 2048), dtype=np.float32)
    if half == 0:
        xin[1024:] = x[b, 0:2048]
        pos = np.arange(-1024, 2048)
    else:
        xin[:] = x[b, 1024:4096]
        pos = np.arange(1024, 4096)
    cost, sint = rope_tables(np.maximum(pos, 0))
    f32 = np.float32
    return {
        "xin": xin,
        "l0wq": np.ascontiguousarray(inputs["L0_attn_w_q"], dtype=f32),
        "l0wk": np.ascontiguousarray(inputs["L0_attn_w_k"], dtype=f32),
        "l0wv": np.ascontiguousarray(inputs["L0_attn_w_v"], dtype=f32),
        "l0wo": np.ascontiguousarray(inputs["L0_attn_w_o"], dtype=f32),
        "cost": cost, "sint": sint,
        "l0mask": np.asarray(l0_masks(half)),
        "ident": ident,
        "w1": np.ascontiguousarray(inputs["L0_ffn_w1"], dtype=f32),
        "w2": np.ascontiguousarray(inputs["L0_ffn_w2"], dtype=f32),
        "w3": np.ascontiguousarray(inputs["L0_ffn_w3"], dtype=f32),
        "q1w": np.ascontiguousarray(inputs["L1_attn_q1_w"], dtype=f32),
        "k1w": np.ascontiguousarray(inputs["L1_attn_k1_w"], dtype=f32),
        "q2w": np.ascontiguousarray(inputs["L1_attn_q2_w"], dtype=f32),
        "k2w": np.ascontiguousarray(inputs["L1_attn_k2_w"], dtype=f32),
        "cvw": np.ascontiguousarray(inputs["L1_attn_v_w"], dtype=f32),
        "cow": np.ascontiguousarray(inputs["L1_attn_o_w"], dtype=f32),
        "lam_b": np.ascontiguousarray(np.broadcast_to(
            np.asarray(inputs["L1_attn_lambda_param"], dtype=f32)[None, :], (128, 16))),
        "dwv": np.ascontiguousarray(
            np.asarray(inputs["L1_ffn_w_v"], dtype=f32).reshape(2048, 2048)),
        "dow": np.ascontiguousarray(
            np.asarray(inputs["L1_ffn_o_weights"], dtype=f32).reshape(2048, 2048)),
        "sim": np.ascontiguousarray(inputs["L1_ffn_sim_matrix"], dtype=f32),
        "gates": np.ascontiguousarray(
            np.asarray(inputs["L1_ffn_gates"], dtype=f32).reshape(1, 32)),
        "selm": selm, "ir2": ir2, "ltm": ltm,
    }


def kernel(**inputs):
    from concourse.bass_utils import run_bass_kernel_spmd
    assert inputs["L1_ffn_w_q"].shape == (32, 64, 2048)
    assert inputs["L1_ffn_w_k"].shape == (32, 64, 2048)
    nc = build_mega(use_cc=True)
    in_maps = [host_core_inputs(inputs, c) for c in range(8)]
    res = run_bass_kernel_spmd(nc, in_maps, core_ids=list(range(8)))
    out = np.zeros((4, 4096, 2048), dtype=np.float32)
    for c in range(8):
        b, half = c // 2, c % 2
        out[b, half * 2048:(half + 1) * 2048] = res.results[c]["out"]
    return out


# version 08
# speedup vs baseline: 1.0177x; 1.0177x over previous
import math
from contextlib import ExitStack

import numpy as np
import concourse.bass as bass
import concourse.mybir as mybir

F32 = mybir.dt.float32
BF16 = mybir.dt.bfloat16
F16 = mybir.dt.float16
AF = mybir.ActivationFunctionType
ALU = mybir.AluOpType

D = 2048
TOWN = 2048
TCTX = 1024
TIN = TOWN + TCTX
HD = 128
INVSQ_HD = 1.0 / math.sqrt(128.0)
FFN_H = 5632


USED_SEM_NUMS = set()


class Sem:
    def __init__(self, stack, nc, name):
        self.h = stack.enter_context(nc.semaphore(name))
        self.n = 0

    def inc(self, instr, k=1):
        instr.then_inc(self.h, k)
        self.n += k
        return instr


def make_warm(ph, PS, ones_col, rhs_ap, lo):
    wsem = ph.sem("warm")

    def warm(t):
        t.wait_ge(wsem.h, wsem.n)
        wsem.inc(t.matmul(PS[0:1, lo:lo + 64], ones_col[:, :], rhs_ap,
                          start=True, stop=True))
    return warm


def edge(eng, sem, instr):
    sem.inc(instr)
    eng.wait_ge(sem.h, sem.n)


class Phase:
    def __init__(self, nc, stack, block, tag):
        self.nc, self.stack, self.block, self.tag = nc, stack, block, tag

    def sem(self, name):
        s = Sem(self.stack, self.nc, f"{self.tag}_{name}")
        USED_SEM_NUMS.add(s.h.num)
        return s

    def sbuf(self, name, shape, dtype, stack=None):
        return (stack or self.stack).enter_context(
            self.nc.sbuf_tensor(f"{self.tag}_{name}", shape, dtype))

    def psum(self, name, shape, dtype):
        return self.stack.enter_context(
            self.nc.psum_tensor(f"{self.tag}_{name}", shape, dtype))


def emit_ln_xhat(ph, *, src_hbm, n_tc, dst_tiles, ident, eps, psbf, name,
                 bufstack=None, skip_transpose=False, dst_group=None, warm=None):
    vseq = ph.sem(f"{name}_vq")
    sseq = ph.sem(f"{name}_sq")
    NS = 4
    xb = ph.sbuf(f"{name}_xb", [128, NS * D], F32, bufstack)
    xh = ph.sbuf(f"{name}_xh", [128, NS * D], BF16, bufstack)
    bn6 = ph.sbuf(f"{name}_bn6", [128, NS * 24], F32, bufstack)
    mv = ph.sbuf(f"{name}_mv", [128, NS * 2], F32, bufstack)
    rstd = ph.sbuf(f"{name}_rstd", [128, NS], F32, bufstack)
    nmr = ph.sbuf(f"{name}_nmr", [128, NS], F32, bufstack)

    s_ld = [ph.sem(f"{name}_ld{j}") for j in range(NS)]
    s_bn = ph.sem(f"{name}_bn")
    s_mv = ph.sem(f"{name}_mv")
    s_rs = ph.sem(f"{name}_rs")
    s_nm = ph.sem(f"{name}_nm")
    s_ap = ph.sem(f"{name}_ap")
    s_tp = ph.sem(f"{name}_tp")
    s_ev = ph.sem(f"{name}_ev")

    @ph.block.sync
    def _(sync):
        for tc in range(n_tc):
            sl = tc % NS
            if tc >= NS:
                sync.wait_ge(s_ap.h, tc - NS + 1)
            s_ld[sl].inc(sync.dma_start(
                out=xb[:, sl * D:(sl + 1) * D],
                in_=src_hbm[tc * 128:(tc + 1) * 128, :]), 16)

    @ph.block.vector
    def _(v):
        def emit_evict_group(G):
            tc, g = G // 4, G % 4
            v.wait_ge(s_tp.h, G + 1)
            if dst_group is not None:
                s_ev.inc(v.tensor_copy(
                    dst_group(g, tc),
                    psbf[:, (G % 2) * 1024:(G % 2) * 1024 + 512]
                    .rearrange("p (j t) -> p j t", t=128)))
                return
            ii = None
            for j in range(4):
                ci = g * 4 + j
                ii = v.tensor_copy(dst_tiles(ci, tc),
                                   psbf[:, (G % 2) * 1024 + j * 128:
                                        (G % 2) * 1024 + (j + 1) * 128])
            s_ev.inc(ii)

        for tc in range(n_tc):
            sl = tc % NS
            if tc >= 1:
                for g in range(4):
                    emit_evict_group((tc - 1) * 4 + g)
            v.wait_ge(s_ld[sl].h, (tc // NS + 1) * 16)
            ii = None
            for j in range(4):
                ii = v.bn_stats(bn6[:, sl * 24 + j * 6: sl * 24 + (j + 1) * 6],
                                xb[:, sl * D + j * 512: sl * D + (j + 1) * 512])
            s_bn.inc(ii)
            v.wait_ge(s_bn.h, tc + 1)
            s_mv.inc(v.bn_aggr(mv[:, sl * 2:(sl + 1) * 2],
                               bn6[:, sl * 24:(sl + 1) * 24]))
            v.wait_ge(s_rs.h, tc + 1)
            v.wait_ge(s_mv.h, tc + 1)
            s_nm.inc(v.tensor_scalar(nmr[:, sl:sl + 1], mv[:, sl * 2:sl * 2 + 1],
                                     rstd[:, sl:sl + 1], -1.0, ALU.mult, ALU.mult))
        for g in range(4):
            emit_evict_group((n_tc - 1) * 4 + g)

    @ph.block.scalar
    def _(s):
        for tc in range(n_tc):
            sl = tc % NS
            s.wait_ge(s_mv.h, tc + 1)
            edge(s, sseq, s.activation(rstd[:, sl:sl + 1], mv[:, sl * 2 + 1:sl * 2 + 2],
                                       AF.Ln, bias=eps[:, :]))
            s_rs.inc(s.activation(rstd[:, sl:sl + 1], rstd[:, sl:sl + 1],
                                  AF.Exp, scale=-0.5))
            s.wait_ge(s_rs.h, tc + 1)
            s.wait_ge(s_nm.h, tc + 1)
            if tc >= NS:
                s.wait_ge(s_tp.h, 4 * (tc - NS + 1))
            s_ap.inc(s.activation(xh[:, sl * D:(sl + 1) * D],
                                  xb[:, sl * D:(sl + 1) * D],
                                  AF.Identity, scale=rstd[:, sl:sl + 1],
                                  bias=nmr[:, sl:sl + 1]))

    @ph.block.tensor
    def _(t):
        for tc in range(n_tc):
            sl = tc % NS
            t.wait_ge(s_ap.h, tc + 1)
            for g in range(4):
                G = tc * 4 + g
                if G >= 2:
                    t.wait_ge(s_ev.h, G - 1)
                if warm is not None:
                    warm(t)
                ii = None
                for j in range(4):
                    ci = g * 4 + j
                    ii = t.transpose(psbf[:, (G % 2) * 1024 + j * 128:
                                          (G % 2) * 1024 + (j + 1) * 128],
                                     xh[:, sl * D + ci * 128:sl * D + (ci + 1) * 128],
                                     ident[:, :])
                s_tp.inc(ii)

    return {"ev": s_ev, "ap": s_ap}


def emit_rope(v, pk, cosap, sinap, scr, out_bf, W, vseq):
    sa = scr[:, 0:W]
    sb = scr[:, W:2 * W]
    vseq.inc(v.tensor_tensor(sa[0:64, :], pk[64:128, :], sinap[0:64, :], ALU.mult))
    vseq.inc(v.tensor_tensor(sa[64:128, :], pk[0:64, :], sinap[64:128, :], ALU.mult))
    vseq.inc(v.tensor_tensor(sb[0:64, :], pk[0:64, :], cosap[0:64, :], ALU.mult))
    vseq.inc(v.tensor_tensor(sb[64:128, :], pk[64:128, :], cosap[64:128, :], ALU.mult))
    v.wait_ge(vseq.h, vseq.n)
    vseq.inc(v.tensor_tensor(out_bf[0:64, :], sb[0:64, :], sa[0:64, :], ALU.subtract))
    vseq.inc(v.tensor_tensor(out_bf[64:128, :], sb[64:128, :], sa[64:128, :], ALU.add))
    v.wait_ge(vseq.h, vseq.n)


def build_phase_a(nc, io, tap=None, stages=4):
    with ExitStack() as stack:
        block = stack.enter_context(nc.Block())
        ph = Phase(nc, stack, block, "A")

        ident = ph.sbuf("ident", [128, 128], BF16)
        eps = ph.sbuf("eps", [128, 1], F32)
        ones_bf = ph.sbuf("onesb", [128, 1], BF16)
        ones_row = ph.sbuf("onesr", [1, 128], BF16)
        psbf = ph.psum("psbf", [128, 2048], BF16)
        PS = ph.psum("ps", [128, 3072], F32)
        st12 = ExitStack()
        st01 = ExitStack()
        stLN = ExitStack()
        kbar = ph.sbuf("kbar", [128, 4 * TIN], BF16, st12)
        xhat_ctx = ph.sbuf("xhc", [128, 16 * 1024], BF16, st01)
        xhat_own = ph.sbuf("xho", [128, 16 * 2048], BF16, st01)

        s_pre = ph.sem("pre")
        vseq = ph.sem("vseq")
        sseq = ph.sem("sseq")

        @ph.block.sync
        def _(sync):
            s_pre.inc(sync.dma_start(out=ident[:, :], in_=io["ident"][:, :]), 16)

        @ph.block.vector
        def _(v):
            v.memset(eps[:, :], 1e-5)
            v.memset(ones_bf[:, :], 1.0)
            v.memset(ones_row[:, :], 1.0)
            v.wait_ge(s_pre.h, 16)

        def dst(ci, tc):
            if tc < 8:
                return xhat_ctx[:, ci * 1024 + tc * 128: ci * 1024 + (tc + 1) * 128]
            u = tc - 8
            return xhat_own[:, ci * 2048 + u * 128: ci * 2048 + (u + 1) * 128]

        def dstg(g, tc):
            if tc < 8:
                return xhat_ctx[:, :].rearrange("p (ci t) -> p ci t", t=1024)[
                    :, g * 4:(g + 1) * 4, tc * 128:(tc + 1) * 128]
            u = tc - 8
            return xhat_own[:, :].rearrange("p (ci t) -> p ci t", t=2048)[
                :, g * 4:(g + 1) * 4, u * 128:(u + 1) * 128]

        warmA = make_warm(ph, PS, ones_bf, ident[:, 0:64], 2944)
        lns = emit_ln_xhat(ph, src_hbm=io["xin"], n_tc=24, dst_tiles=dst,
                           ident=ident, eps=eps, psbf=psbf, name="ln0",
                           bufstack=stLN, dst_group=dstg, warm=warmA)
        s_ev = lns["ev"]
        EV_ALL = s_ev.n
        stLN.close()

        if stages < 1:
            st01.close(); st12.close()
            return nc
        wcol = ph.sbuf("wcol", [128, 2 * 2048], BF16, st01)
        wvbf = ph.sbuf("wvbf", [128, 16 * 512], BF16, st01)
        cfull = ph.sbuf("cfull", [128, TIN], F32, st01)
        sfull = ph.sbuf("sfull", [128, TIN], F32, st01)
        rsc = ph.sbuf("rsc", [128, 2 * 1024], F32, st01)
        vev = ph.sbuf("vev", [128, 2 * 512], BF16, st01)
        qev = ph.sbuf("qev", [128, 2 * 2048], BF16, st01)

        s_wld = [ph.sem("wld0"), ph.sem("wld1")]
        s_wv = ph.sem("wv")
        s_qld = [ph.sem("qld0"), ph.sem("qld1")]
        s_cs = ph.sem("cs")
        s_kmm = ph.sem("kmm")
        s_krp = ph.sem("krp")
        s_vmm = ph.sem("vmm")
        s_vev = ph.sem("vev")
        s_vst = [ph.sem("vst0"), ph.sem("vst1")]
        s_qmm = ph.sem("qmm")
        s_qrp = ph.sem("qrp")
        s_qst = [ph.sem("qst0"), ph.sem("qst1")]

        def xcol512(ci, t0):
            if t0 < 1024:
                return xhat_ctx[:, ci * 1024 + t0: ci * 1024 + t0 + 512]
            u = t0 - 1024
            return xhat_own[:, ci * 2048 + u: ci * 2048 + u + 512]

        @ph.block.gpsimd
        def _(g):
            g.wait_ge(s_ev.h, EV_ALL)
            for ut in range(4):
                sl = ut % 2
                if ut >= 2:
                    g.wait_ge(s_kmm.h, (ut - 2) * 3 + 3)
                s_wld[sl].inc(g.dma_start(
                    out=wcol[:, sl * 2048: sl * 2048 + 2048]
                    .rearrange("p (ci u) -> p ci u", u=128),
                    in_=io["wk"].rearrange("(ci p) u -> p ci u", p=128)
                    [:, :, ut * 128:(ut + 1) * 128]), 16)
            s_wv.inc(g.dma_start(
                out=wvbf[:, :].rearrange("p (ci u) -> p ci u", u=512),
                in_=io["wv"].rearrange("(ci p) u -> p ci u", p=128)), 16)
            for ut in range(16):
                sl = ut % 2
                if ut < 2:
                    g.wait_ge(s_kmm.h, 12)
                else:
                    g.wait_ge(s_qmm.h, (ut - 2) * 2 + 2)
                s_qld[sl].inc(g.dma_start(
                    out=wcol[:, sl * 2048: sl * 2048 + 2048]
                    .rearrange("p (ci u) -> p ci u", u=128),
                    in_=io["wq"].rearrange("(ci p) u -> p ci u", p=128)
                    [:, :, ut * 128:(ut + 1) * 128]), 16)

        @ph.block.sync
        def _(sync):
            sync.wait_ge(s_ev.h, EV_ALL)
            s_cs.inc(sync.dma_start(out=cfull[:, :], in_=io["cost"][:, :]), 16)
            s_cs.inc(sync.dma_start(out=sfull[:, :], in_=io["sint"][:, :]), 16)
            for tt in range(24):
                sync.wait_ge(s_vev.h, tt + 1)
                s_vst[tt % 2].inc(sync.dma_start(
                    out=io["vsc"][tt * 128:(tt + 1) * 128, :],
                    in_=vev[:, (tt % 2) * 512:(tt % 2 + 1) * 512]), 16)
            for ut in range(16):
                sync.wait_ge(s_qrp.h, 2 * (ut + 1))
                s_qst[ut % 2].inc(sync.dma_start(
                    out=io["qsc"][ut * 128:(ut + 1) * 128, :],
                    in_=qev[:, (ut % 2) * 2048:(ut % 2 + 1) * 2048]), 16)

        @ph.block.tensor
        def _(t):
            t.wait_ge(s_ev.h, EV_ALL)
            for ut in range(4):
                sl = ut % 2
                t.wait_ge(s_wld[ut % 2].h, (ut // 2 + 1) * 16)
                for th in range(3):
                    G = ut * 3 + th
                    if G >= 2:
                        t.wait_ge(s_krp.h, G - 1)
                    ii = None
                    for ci in range(16):
                        for mo in range(2):
                            t0 = th * 1024 + mo * 512
                            ii = t.matmul(PS[:, (G % 2) * 1024 + mo * 512:
                                             (G % 2) * 1024 + (mo + 1) * 512],
                                          wcol[:, sl * 2048 + ci * 128:
                                               sl * 2048 + (ci + 1) * 128],
                                          xcol512(ci, t0),
                                          start=(ci == 0), stop=(ci == 15))
                    s_kmm.inc(ii)
            t.wait_ge(s_wv.h, 16)
            for tt in range(24):
                if tt >= 2:
                    t.wait_ge(s_vev.h, tt - 1)
                ii = None
                for ci in range(16):
                    ii = t.matmul(PS[:, 2048 + (tt % 2) * 512: 2048 + (tt % 2 + 1) * 512],
                                  dst(ci, tt), wvbf[:, ci * 512:(ci + 1) * 512],
                                  start=(ci == 0), stop=(ci == 15))
                s_vmm.inc(ii)
            t.wait_ge(s_krp.h, 12)
            for ut in range(16):
                sl = ut % 2
                t.wait_ge(s_qld[ut % 2].h, (ut // 2 + 1) * 16)
                for th in range(2):
                    G = ut * 2 + th
                    if G >= 2:
                        t.wait_ge(s_qrp.h, G - 1)
                    ii = None
                    for ci in range(16):
                        for mo in range(2):
                            u0 = th * 1024 + mo * 512
                            ii = t.matmul(PS[:, (G % 2) * 1024 + mo * 512:
                                             (G % 2) * 1024 + (mo + 1) * 512],
                                          wcol[:, sl * 2048 + ci * 128:
                                               sl * 2048 + (ci + 1) * 128],
                                          xhat_own[:, ci * 2048 + u0: ci * 2048 + u0 + 512],
                                          start=(ci == 0), stop=(ci == 15))
                    s_qmm.inc(ii)

        @ph.block.vector
        def _(v):
            v.wait_ge(s_cs.h, 32)
            for ut in range(4):
                for th in range(3):
                    G = ut * 3 + th
                    v.wait_ge(s_kmm.h, G + 1)
                    emit_rope(
                        v, PS[:, (G % 2) * 1024:(G % 2) * 1024 + 1024],
                        cfull[:, th * 1024:(th + 1) * 1024],
                        sfull[:, th * 1024:(th + 1) * 1024],
                        rsc,
                        kbar[:, ut * TIN + th * 1024: ut * TIN + (th + 1) * 1024],
                        1024, vseq)
                    v.sem_inc(s_krp.h, 1)
                    s_krp.n += 1
            for tt in range(24):
                v.wait_ge(s_vmm.h, tt + 1)
                if tt >= 2:
                    v.wait_ge(s_vst[tt % 2].h, (tt // 2) * 16)
                s_vev.inc(v.tensor_copy(vev[:, (tt % 2) * 512:(tt % 2 + 1) * 512],
                                        PS[:, 2048 + (tt % 2) * 512:
                                           2048 + (tt % 2 + 1) * 512]))
            for ut in range(16):
                for th in range(2):
                    G = ut * 2 + th
                    v.wait_ge(s_qmm.h, G + 1)
                    if ut >= 2 and th == 0:
                        v.wait_ge(s_qst[ut % 2].h, (ut // 2) * 16)
                    emit_rope(
                        v, PS[:, (G % 2) * 1024:(G % 2) * 1024 + 1024],
                        cfull[:, 1024 + th * 1024: 1024 + (th + 1) * 1024],
                        sfull[:, 1024 + th * 1024: 1024 + (th + 1) * 1024],
                        rsc,
                        qev[:, (ut % 2) * 2048 + th * 1024:
                            (ut % 2) * 2048 + (th + 1) * 1024],
                        1024, vseq)
                    v.sem_inc(s_qrp.h, 1)
                    s_qrp.n += 1

        if stages < 2:
            st01.close(); st12.close()
            return nc
        st01.close()
        st2 = ExitStack()
        msk = ph.sbuf("msk", [128, 2 * 12 * 512], BF16, st2)
        vwin = ph.sbuf("vwin", [128, 2 * 12 * 512], BF16, st2)
        qh = ph.sbuf("qh", [128, 2 * 512], BF16, st2)
        pbuf = ph.sbuf("pbuf", [128, 2 * 512], BF16, st2)
        pm = ph.sbuf("pm", [128, 2 * 512], BF16, st2)
        dacc = ph.sbuf("dacc", [128, 512], F16, st2)
        dsum = ph.sbuf("dsum", [1, 512], F32, st2)
        rec = ph.sbuf("rec", [1, 512], F32, st2)
        recbf = ph.sbuf("recbf", [1, 512], BF16, st2)
        rsb = ph.sbuf("rsb", [128, 512], F32, st2)
        atb = ph.sbuf("atb", [128, 2 * 512], BF16, st2)

        s_mld = [ph.sem("mld0"), ph.sem("mld1")]
        s_qhld = [ph.sem("qhld0"), ph.sem("qhld1")]
        s_smm = ph.sem("smm")
        s_exp = ph.sem("exp")
        s_msk = ph.sem("mskd")
        s_dac = ph.sem("dac")
        s_omm = ph.sem("omm")
        s_dbf = ph.sem("dbfs")
        s_dmm = ph.sem("dmm")
        s_dsm = ph.sem("dsm")
        s_rc = ph.sem("rc")
        s_rmm = ph.sem("rmm")
        s_rsb = ph.sem("rsbs")
        s_nrm = ph.sem("nrm")
        s_ast = [ph.sem("ast0"), ph.sem("ast1")]

        @ph.block.sync
        def _(sync):
            sync.wait_ge(s_qst[0].h, 8 * 16)
            sync.wait_ge(s_qst[1].h, 8 * 16)
            sync.wait_ge(s_vst[0].h, 12 * 16)
            sync.wait_ge(s_vst[1].h, 12 * 16)
            def emit_at3_store(sync, it2):
                qb2, h2 = it2 // 16, it2 % 16
                sync.wait_ge(s_nrm.h, it2 + 1)
                s_ast[it2 % 2].inc(sync.dma_start(
                    out=io["at3"][h2, qb2 * 4:(qb2 + 1) * 4]
                    .rearrange("j p t -> p j t"),
                    in_=atb[:, (it2 % 2) * 512:(it2 % 2 + 1) * 512]
                    .rearrange("p (j t) -> p j t", j=4)), 16)

            for qb in range(4):
                sq = qb % 2
                if qb >= 2:
                    sync.wait_ge(s_nrm.h, (qb - 1) * 16)
                s_mld[sq].inc(sync.dma_start(
                    out=msk[:, sq * 12 * 512:(sq + 1) * 12 * 512]
                    .rearrange("p (c q) -> p c q", q=512),
                    in_=io["l0mask"][qb].rearrange("c p q -> p c q")), 16)
                s_mld[sq].inc(sync.dma_start(
                    out=vwin[:, sq * 12 * 512:(sq + 1) * 12 * 512]
                    .rearrange("p (c u) -> p c u", u=512),
                    in_=io["vsc"][qb * 512: qb * 512 + 1536, :]
                    .rearrange("(c p) u -> p c u", p=128)), 16)
                for h in range(16):
                    it = qb * 16 + h
                    if it >= 2:
                        sync.wait_ge(s_smm.h, (it - 1) * 12)
                    s_qhld[it % 2].inc(sync.dma_start(
                        out=qh[:, (it % 2) * 512:(it % 2 + 1) * 512],
                        in_=io["qsc"][h * 128:(h + 1) * 128, qb * 512:(qb + 1) * 512]), 16)
                    if it >= 2:
                        emit_at3_store(sync, it - 2)
            emit_at3_store(sync, 62)
            emit_at3_store(sync, 63)

        @ph.block.tensor
        def _(t):
            t.wait_ge(s_qrp.h, 32)
            t.wait_ge(s_vst[0].h, 12 * 16)
            t.wait_ge(s_vst[1].h, 12 * 16)
            t.wait_ge(s_qst[0].h, 8 * 16)
            t.wait_ge(s_qst[1].h, 8 * 16)
            for qb in range(4):
                sq = qb % 2
                for h in range(16):
                    it = qb * 16 + h
                    g = h // 4
                    t.wait_ge(s_qhld[it % 2].h, (it // 2 + 1) * 16)
                    if h == 0:
                        t.wait_ge(s_mld[sq].h, (qb // 2 + 1) * 32)
                    if it >= 2:
                        t.wait_ge(s_nrm.h, it - 1)
                    ii_o = None

                    def emit_score(c):
                        ic = it * 12 + c
                        jc = qb * 4 + c
                        if ic >= 2:
                            t.wait_ge(s_exp.h, ic - 1)
                        ii = t.matmul(PS[:, (ic % 2) * 512:(ic % 2 + 1) * 512],
                                      kbar[:, g * TIN + jc * 128: g * TIN + (jc + 1) * 128],
                                      qh[:, (it % 2) * 512:(it % 2 + 1) * 512],
                                      start=True, stop=True)
                        s_smm.inc(ii)

                    def emit_av(c):
                        nonlocal ii_o
                        ic = it * 12 + c
                        jc = qb * 4 + c
                        t.wait_ge(s_msk.h, ic + 1)
                        ii_o = t.matmul(PS[:, 1024 + (it % 2) * 512:
                                           1024 + (it % 2 + 1) * 512],
                                        vwin[:, sq * 12 * 512 + c * 512 + g * 128:
                                             sq * 12 * 512 + c * 512 + g * 128 + 128],
                                        pm[:, (ic % 2) * 512:(ic % 2 + 1) * 512],
                                        start=(c == 0), stop=(c == 11))

                    emit_score(0)
                    for c in range(1, 12):
                        emit_score(c)
                        emit_av(c - 1)
                    emit_av(11)
                    s_omm.inc(ii_o)
                    t.wait_ge(s_dbf.h, it + 1)
                    if it >= 1:
                        t.wait_ge(s_dsm.h, it)
                    s_dmm.inc(t.matmul(PS[0:1, 2048:2560], ones_bf[:, :],
                                       dacc[:, :], start=True, stop=True))
                    t.wait_ge(s_rc.h, it + 1)
                    if it >= 1:
                        t.wait_ge(s_rsb.h, it)
                    s_rmm.inc(t.matmul(PS[:, 2560:3072], ones_row[:, :],
                                       recbf[:, :], start=True, stop=True))

        @ph.block.scalar
        def _(s):
            for qb in range(4):
                for h in range(16):
                    it = qb * 16 + h
                    for c in range(12):
                        ic = it * 12 + c
                        s.wait_ge(s_smm.h, ic + 1)
                        if ic >= 2:
                            s.wait_ge(s_msk.h, ic - 1)
                        s_exp.inc(s.activation(pbuf[:, (ic % 2) * 512:(ic % 2 + 1) * 512],
                                               PS[:, (ic % 2) * 512:(ic % 2 + 1) * 512],
                                               AF.Exp, scale=INVSQ_HD))
                    s.wait_ge(s_dmm.h, it + 1)
                    s_dsm.inc(s.copy(dsum[:, :], PS[0:1, 2048:2560]))
                    s.wait_ge(s_rmm.h, it + 1)
                    if it >= 1:
                        s.wait_ge(s_nrm.h, it)
                    s_rsb.inc(s.copy(rsb[:, :], PS[:, 2560:3072]))

        @ph.block.vector
        def _(v):
            for qb in range(4):
                sq = qb % 2
                for h in range(16):
                    it = qb * 16 + h
                    for c in range(12):
                        ic = it * 12 + c
                        v.wait_ge(s_exp.h, ic + 1)
                        edge(v, vseq, v.tensor_tensor(
                            pm[:, (ic % 2) * 512:(ic % 2 + 1) * 512],
                            pbuf[:, (ic % 2) * 512:(ic % 2 + 1) * 512],
                            msk[:, sq * 12 * 512 + c * 512: sq * 12 * 512 + (c + 1) * 512],
                            ALU.mult))
                        v.sem_inc(s_msk.h, 1)
                        s_msk.n += 1
                        if c == 0:
                            edge(v, vseq, v.tensor_copy(dacc[:, :],
                                                        pm[:, (ic % 2) * 512:(ic % 2 + 1) * 512]))
                        else:
                            edge(v, vseq, v.tensor_tensor(
                                dacc[:, :], dacc[:, :],
                                pm[:, (ic % 2) * 512:(ic % 2 + 1) * 512], ALU.add))
                    v.sem_inc(s_dbf.h, 1)
                    s_dbf.n += 1
                    v.wait_ge(s_dsm.h, it + 1)
                    if it >= 1:
                        v.wait_ge(s_rmm.h, it)
                    edge(v, vseq, v.reciprocal(rec[:, :], dsum[:, :]))
                    s_rc.inc(v.tensor_copy(recbf[:, :], rec[:, :]))
                    v.wait_ge(s_rsb.h, it + 1)
                    v.wait_ge(s_omm.h, it + 1)
                    if it >= 2:
                        v.wait_ge(s_ast[it % 2].h, (it // 2) * 16)
                    s_nrm.inc(v.tensor_tensor(atb[:, (it % 2) * 512:(it % 2 + 1) * 512],
                                              PS[:, 1024 + (it % 2) * 512:
                                                 1024 + (it % 2 + 1) * 512],
                                              rsb[:, :], ALU.mult))

        if stages < 3:
            st2.close(); st12.close()
            return nc
        st2.close()
        st12.close()
        wobf = ph.sbuf("wobf", [128, 16 * 2048], BF16)
        atc = ph.sbuf("atc", [128, 2 * 2048], BF16)
        fbuf = ph.sbuf("fbuf", [128, 2 * 2048], F32)
        lnf = ph.sbuf("lnf", [128, 2048], F32)
        xo = ph.sbuf("xo", [128, 2 * 2048], F32)
        obuf = ph.sbuf("obuf", [128, 2 * 2048], F32)
        fb6 = ph.sbuf("fb6", [128, 24], F32)
        fmv = ph.sbuf("fmv", [128, 2], F32)
        frs = ph.sbuf("frs", [128, 1], F32)
        fnm = ph.sbuf("fnm", [128, 1], F32)

        s_wo = ph.sem("wo")
        s_atc = [ph.sem("atc0"), ph.sem("atc1")]
        s_xo = [ph.sem("xo0"), ph.sem("xo1")]
        s_fmm = ph.sem("fmm")
        s_fev = ph.sem("fev")
        s_fbn = ph.sem("fbn")
        s_fmv = ph.sem("fmv")
        s_frs = ph.sem("frs")
        s_fnm = ph.sem("fnm")
        s_fap = ph.sem("fap")
        s_res = ph.sem("res")
        s_ost = [ph.sem("ost0"), ph.sem("ost1")]

        @ph.block.gpsimd
        def _(g):
            g.wait_ge(s_ast[0].h, 32 * 16)
            g.wait_ge(s_ast[1].h, 32 * 16)
            s_wo.inc(g.dma_start(
                out=wobf[:, :].rearrange("p (hc d) -> p hc d", d=2048),
                in_=io["wo"].rearrange("(hc p) d -> p hc d", p=128)), 16)

        @ph.block.sync
        def _(sync):
            sync.wait_ge(s_ast[0].h, 32 * 16)
            sync.wait_ge(s_ast[1].h, 32 * 16)
            for tt in range(16):
                sl = tt % 2
                if tt >= 2:
                    sync.wait_ge(s_fmm.h, tt - 1)
                s_atc[sl].inc(sync.dma_start(
                    out=atc[:, sl * 2048:(sl + 1) * 2048]
                    .rearrange("p (hc t) -> p hc t", t=128),
                    in_=io["at3"][:, tt].rearrange("hc p t -> p hc t")), 16)
                if tt >= 2:
                    sync.wait_ge(s_res.h, tt - 1)
                s_xo[sl].inc(sync.dma_start(
                    out=xo[:, sl * 2048:(sl + 1) * 2048],
                    in_=io["xin"][1024 + tt * 128: 1024 + (tt + 1) * 128, :]), 16)
                sync.wait_ge(s_res.h, tt + 1)
                s_ost[sl].inc(sync.dma_start(
                    out=io["x1"][tt * 128:(tt + 1) * 128, :],
                    in_=obuf[:, sl * 2048:(sl + 1) * 2048]), 16)
            sync.wait_ge(s_ost[0].h, 8 * 16)
            sync.wait_ge(s_ost[1].h, 8 * 16)

        @ph.block.tensor
        def _(t):
            t.wait_ge(s_nrm.h, 64)
            t.wait_ge(s_wo.h, 16)
            for tt in range(16):
                sl = tt % 2
                t.wait_ge(s_atc[sl].h, (tt // 2 + 1) * 16)
                if tt >= 1:
                    t.wait_ge(s_fev.h, tt)
                ii = None
                for hd in range(16):
                    for dc in range(4):
                        ii = t.matmul(PS[:, dc * 512:(dc + 1) * 512],
                                      atc[:, sl * 2048 + hd * 128: sl * 2048 + (hd + 1) * 128],
                                      wobf[:, hd * 2048 + dc * 512: hd * 2048 + (dc + 1) * 512],
                                      start=(hd == 0), stop=(hd == 15))
                s_fmm.inc(ii)

        @ph.block.scalar
        def _(s):
            for tt in range(16):
                sl = tt % 2
                s.wait_ge(s_fmm.h, tt + 1)
                if tt >= 2:
                    s.wait_ge(s_fbn.h, tt - 1)
                s_fev.inc(s.copy(fbuf[:, sl * 2048:(sl + 1) * 2048], PS[:, 0:2048]))
                s.wait_ge(s_fmv.h, tt + 1)
                edge(s, sseq, s.activation(frs[:, :], fmv[:, 1:2], AF.Ln, bias=eps[:, :]))
                s_frs.inc(s.activation(frs[:, :], frs[:, :], AF.Exp, scale=-0.5))
                s.wait_ge(s_frs.h, tt + 1)
                s.wait_ge(s_fnm.h, tt + 1)
                if tt >= 1:
                    s.wait_ge(s_res.h, tt)
                s_fap.inc(s.activation(lnf[:, :], fbuf[:, sl * 2048:(sl + 1) * 2048],
                                       AF.Identity, scale=frs[:, :], bias=fnm[:, :]))

        @ph.block.vector
        def _(v):
            for tt in range(16):
                sl = tt % 2
                v.wait_ge(s_fev.h, tt + 1)
                ii = None
                for j in range(4):
                    ii = v.bn_stats(fb6[:, j * 6:(j + 1) * 6],
                                    fbuf[:, sl * 2048 + j * 512: sl * 2048 + (j + 1) * 512])
                s_fbn.inc(ii)
                v.wait_ge(s_fbn.h, tt + 1)
                s_fmv.inc(v.bn_aggr(fmv[:, :], fb6[:, :]))
                v.wait_ge(s_fmv.h, tt + 1)
                v.wait_ge(s_frs.h, tt + 1)
                s_fnm.inc(v.tensor_scalar(fnm[:, :], fmv[:, 0:1], frs[:, :], -1.0,
                                          ALU.mult, ALU.mult))
                v.wait_ge(s_fap.h, tt + 1)
                v.wait_ge(s_xo[sl].h, (tt // 2 + 1) * 16)
                if tt >= 2:
                    v.wait_ge(s_ost[sl].h, (tt // 2) * 16)
                s_res.inc(v.tensor_tensor(obuf[:, sl * 2048:(sl + 1) * 2048],
                                          lnf[:, :], xo[:, sl * 2048:(sl + 1) * 2048],
                                          ALU.add))
    return nc



def rope_tables(positions):
    inv_freq = 1.0 / (10000.0 ** (np.arange(0, HD, 2, dtype=np.float64) / HD))
    freqs = np.asarray(positions, dtype=np.float64)[:, None] * inv_freq[None, :]
    emb = np.concatenate([freqs, freqs], axis=1)
    return (np.cos(emb).T.astype(np.float32).copy(),
            np.sin(emb).T.astype(np.float32).copy())


def l0_masks(half):
    import ml_dtypes
    m = np.zeros((4, 12, 128, 512), dtype=np.float32)
    for qb in range(4):
        for c in range(12):
            j = qb * 512 + c * 128 + np.arange(128)[:, None]
            qo = qb * 512 + np.arange(512)[None, :]
            ok = (j >= qo) & (j <= qo + 1024)
            if half == 0:
                ok &= (j >= 1024)
            m[qb, c] = ok.astype(np.float32)
    return m.astype(ml_dtypes.bfloat16)



def emit_resid_tail_scalar(ph, s, *, tt, sl, fbuf, lnf, eps, frs, fnm, fb6, fmv,
                           sems, sseq):
    s.wait_ge(sems["fmv"].h, tt + 1)
    edge(s, sseq, s.activation(frs[:, :], fmv[:, 1:2], AF.Ln, bias=eps[:, :]))
    sems["frs"].inc(s.activation(frs[:, :], frs[:, :], AF.Exp, scale=-0.5))
    s.wait_ge(sems["frs"].h, tt + 1)
    s.wait_ge(sems["fnm"].h, tt + 1)
    if tt >= 1:
        s.wait_ge(sems["res"].h, tt)
    sems["fap"].inc(s.activation(lnf[:, :], fbuf[:, sl * 2048:(sl + 1) * 2048],
                                 AF.Identity, scale=frs[:, :], bias=fnm[:, :]))


def build_phase_b(nc, io):
    with ExitStack() as stack:
        block = stack.enter_context(nc.Block())
        ph = Phase(nc, stack, block, "B")

        ident = ph.sbuf("ident", [128, 128], BF16)
        eps = ph.sbuf("eps", [128, 1], F32)
        ones_f = ph.sbuf("onesf", [128, 1], F32)
        onesr_f = ph.sbuf("onesrf", [1, 128], F32)
        ones_b = ph.sbuf("onesbB", [128, 1], BF16)
        psbf = ph.psum("psbf", [128, 2048], BF16)
        PS = ph.psum("ps", [128, 3072], F32)
        vseq = ph.sem("vseq")
        sseq = ph.sem("sseq")
        s_pre = ph.sem("pre")
        warmB = make_warm(ph, PS, ones_b, ident[:, 0:64], 2048)

        @ph.block.sync
        def _(sync):
            s_pre.inc(sync.dma_start(out=ident[:, :], in_=io["ident"][:, :]), 16)

        @ph.block.vector
        def _(v):
            v.memset(eps[:, :], 1e-5)
            v.memset(ones_f[:, :], 1.0)
            v.memset(onesr_f[:, :], 1.0)
            v.memset(ones_b[:, :], 1.0)
            v.wait_ge(s_pre.h, 16)

        gcol = ph.sbuf("gcol", [128, 104], F32)
        gsum = ph.sbuf("gsum", [1, 104], F32)
        hcol = ph.sbuf("hcol", [128, 6], F32)

        st0 = ExitStack()
        stLN = ExitStack()
        xhat = ph.sbuf("xhat", [128, 16 * 2048], BF16, st0)

        def dstb(ci, tc):
            return xhat[:, ci * 2048 + tc * 128: ci * 2048 + (tc + 1) * 128]

        def dstgb(g, tc):
            return xhat[:, :].rearrange("p (ci t) -> p ci t", t=2048)[
                :, g * 4:(g + 1) * 4, tc * 128:(tc + 1) * 128]

        lns = emit_ln_xhat(ph, src_hbm=io["x1"], n_tc=16, dst_tiles=dstb,
                           ident=ident, eps=eps, psbf=psbf, name="lnb",
                           bufstack=stLN, dst_group=dstgb, warm=warmB)
        s_ev = lns["ev"]
        EV_ALL = s_ev.n
        stLN.close()

        stG = ExitStack()
        gbuf = ph.sbuf("gbuf", [128, 2 * 5632], F32, stG)
        gscr = ph.sbuf("gscr", [128, 5632], BF16, stG)

        s_gld = [ph.sem("gld0"), ph.sem("gld1")]
        s_gab = ph.sem("gab")
        s_gmm = ph.sem("gmm")
        s_gsm = ph.sem("gsm")
        s_hred = ph.sem("hred")
        s_hmm = ph.sem("hmm")
        s_hcp = ph.sem("hcp")

        g_chunks = [("w1", i, 2048) for i in range(44)] + \
                   [("w2", i, 2048) for i in range(44)] + \
                   [("w3", i, 5632) for i in range(16)]

        @ph.block.gpsimd
        def _(g):
            g.wait_ge(s_ev.h, EV_ALL)
            for k, (wn, i, w) in enumerate(g_chunks):
                sl = k % 2
                if k >= 2:
                    g.wait_ge(s_gab.h, k - 1)
                s_gld[sl].inc(g.dma_start(
                    out=gbuf[:, sl * 5632: sl * 5632 + w],
                    in_=io[wn][i * 128:(i + 1) * 128, :]), 16)

        @ph.block.scalar
        def _(s):
            for k, (wn, i, w) in enumerate(g_chunks):
                sl = k % 2
                s.wait_ge(s_gld[sl].h, (k // 2 + 1) * 16)
                if k >= 1:
                    s.wait_ge(s_gab.h, k)
                s_gab.inc(s.activation(gscr[:, 0:w], gbuf[:, sl * 5632: sl * 5632 + w],
                                       AF.Abs, accum_out=gcol[:, k:k + 1]))

        @ph.block.tensor
        def _(t):
            t.wait_ge(s_gab.h, 104)
            s_gmm.inc(t.matmul(PS[0:1, 2048:2152], ones_f[:, :], gcol[:, 0:104],
                               start=True, stop=True))

        @ph.block.scalar
        def _(s):
            s.wait_ge(s_gmm.h, 1)
            s_gsm.inc(s.copy(gsum[:, :], PS[0:1, 2048:2152]))

        @ph.block.vector
        def _(v):
            v.wait_ge(s_gsm.h, 1)
            for wi, (lo, hi, numel) in enumerate([(0, 44, 5632 * 2048),
                                                  (44, 88, 5632 * 2048),
                                                  (88, 104, 2048 * 5632)]):
                edge(v, vseq, v.reduce_sum(gsum[0:1, lo:lo + 1] if False else
                                           hcol[0:1, wi:wi + 1],
                                           gsum[0:1, lo:hi], mybir.AxisListType.X))
                edge(v, vseq, v.tensor_scalar(hcol[0:1, wi:wi + 1], hcol[0:1, wi:wi + 1],
                                              0.5 / numel, 0.5e-8, ALU.mult, ALU.add))
            s_hred.inc(v.tensor_scalar(hcol[0:1, 3:6], hcol[0:1, 0:3], -1.0, None,
                                       ALU.mult))

        @ph.block.tensor
        def _(t):
            t.wait_ge(s_hred.h, 1)
            s_hmm.inc(t.matmul(PS[:, 2560:2566], onesr_f[:, :], hcol[0:1, 0:6],
                               start=True, stop=True))

        @ph.block.scalar
        def _(s):
            s.wait_ge(s_hmm.h, 1)
            s_hcp.inc(s.copy(hcol[:, 0:6], PS[:, 2560:2566]))

        stG.close()
        wbuf = ph.sbuf("wbuf", [128, 2 * 2048], F32, st0)
        wbuf2 = ph.sbuf("wbuf2", [128, 2 * 2048], F32, st0)
        p1s = ph.sbuf("p1s", [128, 2048], F32, st0)
        qn = ph.sbuf("qn", [128, 2 * 2048], BF16, st0)
        qn2 = ph.sbuf("qn2", [128, 2 * 2048], BF16, st0)
        q1t = ph.sbuf("q1t", [128, 2 * 2048], BF16, st0)
        q2t = ph.sbuf("q2t", [128, 2 * 2048], BF16, st0)
        sgb = ph.sbuf("sgb", [128, 2 * 512], F32, st0)
        hid = ph.sbuf("hid", [128, 2 * 512], BF16, st0)

        s_w1ld = [ph.sem("w1ld0"), ph.sem("w1ld1")]
        s_w2ld = [ph.sem("w2ld0"), ph.sem("w2ld1")]
        s_qz = ph.sem("qz")
        s_qtp = ph.sem("qtp")
        s_qev = ph.sem("qev")
        s_xmm = ph.sem("xmm")
        s_sg = ph.sem("sg")
        s_hd = ph.sem("hd")
        s_hst = [ph.sem("hst0"), ph.sem("hst1")]

        @ph.block.sync
        def _(sync):
            sync.wait_ge(s_gab.h, 104)
            def emit_h3_store(hcx):
                for tb in range(4):
                    it = hcx * 4 + tb
                    sync.wait_ge(s_hd.h, it + 1)
                    s_hst[it % 2].inc(sync.dma_start(
                        out=io["h3"][hcx, tb * 4:(tb + 1) * 4]
                        .rearrange("j p t -> p j t"),
                        in_=hid[:, (it % 2) * 512:(it % 2 + 1) * 512]
                        .rearrange("p (j t) -> p j t", j=4)), 16)

            for hc in range(44):
                sl = hc % 2
                if hc >= 2:
                    sync.wait_ge(s_qz.h, 2 * (hc - 1))
                s_w1ld[sl].inc(sync.dma_start(
                    out=wbuf[:, sl * 2048:(sl + 1) * 2048],
                    in_=io["w1"][hc * 128:(hc + 1) * 128, :]), 16)
                s_w2ld[sl].inc(sync.dma_start(
                    out=wbuf2[:, sl * 2048:(sl + 1) * 2048],
                    in_=io["w2"][hc * 128:(hc + 1) * 128, :]), 16)
                if hc >= 1:
                    emit_h3_store(hc - 1)
            emit_h3_store(43)

        @ph.block.vector
        def _(v):
            v.wait_ge(s_hcp.h, 1)

            def emit_hidden(hcx):
                for tb in range(4):
                    it = hcx * 4 + tb
                    v.wait_ge(s_sg.h, it + 1)
                    v.wait_ge(s_xmm.h, hcx * 4 + (tb // 2) * 2 + 2)
                    if it >= 2:
                        v.wait_ge(s_hst[it % 2].h, (it // 2) * 16)
                    s_hd.inc(v.tensor_tensor(
                        hid[:, (it % 2) * 512:(it % 2 + 1) * 512],
                        sgb[:, (it % 2) * 512:(it % 2 + 1) * 512],
                        PS[:, 1024 + (it % 2) * 512: 1024 + (it % 2 + 1) * 512],
                        ALU.mult))

            for hc in range(44):
                sl = hc % 2
                v.wait_ge(s_w1ld[sl].h, (hc // 2 + 1) * 16)
                if hc >= 2:
                    v.wait_ge(s_qtp.h, 16 * (hc - 1))
                edge(v, vseq, v.tensor_scalar(p1s[:, :], wbuf[:, sl * 2048:(sl + 1) * 2048],
                                              hcol[:, 0:1], -1.0, ALU.is_ge, ALU.add))
                edge(v, vseq, v.scalar_tensor_tensor(
                    qn[:, sl * 2048:(sl + 1) * 2048], wbuf[:, sl * 2048:(sl + 1) * 2048],
                    hcol[:, 3:4], p1s[:, :], ALU.is_ge, ALU.add))
                v.sem_inc(s_qz.h, 1)
                s_qz.n += 1
                v.wait_ge(s_w2ld[sl].h, (hc // 2 + 1) * 16)
                edge(v, vseq, v.tensor_scalar(p1s[:, :], wbuf2[:, sl * 2048:(sl + 1) * 2048],
                                              hcol[:, 1:2], -1.0, ALU.is_ge, ALU.add))
                edge(v, vseq, v.scalar_tensor_tensor(
                    qn2[:, sl * 2048:(sl + 1) * 2048], wbuf2[:, sl * 2048:(sl + 1) * 2048],
                    hcol[:, 4:5], p1s[:, :], ALU.is_ge, ALU.add))
                v.sem_inc(s_qz.h, 1)
                s_qz.n += 1
                for g8 in range(16):
                    G = hc * 16 + g8
                    v.wait_ge(s_qtp.h, G + 1)
                    if hc >= 2 and g8 == 0:
                        v.wait_ge(s_xmm.h, 4 * (hc - 1))
                    wq, dstq = (qn, q1t) if g8 < 8 else (qn2, q2t)
                    gg = g8 % 8
                    half = G % 2
                    ii = None
                    for j in range(2):
                        ci = gg * 2 + j
                        ii = v.tensor_copy(
                            dstq[:, sl * 2048 + ci * 128: sl * 2048 + (ci + 1) * 128],
                            psbf[:, half * 1024 + j * 128: half * 1024 + (j + 1) * 128])
                    s_qev.inc(ii)
                if hc >= 1:
                    emit_hidden(hc - 1)
            emit_hidden(43)

        @ph.block.tensor
        def _(t):
            t.wait_ge(s_hcp.h, 1)

            def emit_tp(hc):
                sl = hc % 2
                for g8 in range(16):
                    G = hc * 16 + g8
                    t.wait_ge(s_qz.h, 2 * hc + (1 if g8 < 8 else 2))
                    if G >= 2:
                        t.wait_ge(s_qev.h, G - 1)
                    warmB(t)
                    wq = qn if g8 < 8 else qn2
                    gg = g8 % 8
                    half = G % 2
                    ii = None
                    for j in range(2):
                        ci = gg * 2 + j
                        ii = t.transpose(psbf[:, half * 1024 + j * 128:
                                              half * 1024 + (j + 1) * 128],
                                         wq[:, sl * 2048 + ci * 128: sl * 2048 + (ci + 1) * 128],
                                         ident[:, :])
                    s_qtp.inc(ii)

            def emit_mm(hc):
                sl = hc % 2
                t.wait_ge(s_qev.h, 16 * (hc + 1))
                for tp2 in range(2):
                    it0 = hc * 4 + tp2 * 2
                    t.wait_ge(s_sg.h, max(0, it0))
                    t.wait_ge(s_hd.h, max(0, it0))
                    ii = None
                    for ci in range(16):
                        for tb in (tp2 * 2, tp2 * 2 + 1):
                            ii = t.matmul(PS[:, (tb % 2) * 512:(tb % 2 + 1) * 512],
                                          q1t[:, sl * 2048 + ci * 128: sl * 2048 + (ci + 1) * 128],
                                          xhat[:, ci * 2048 + tb * 512: ci * 2048 + (tb + 1) * 512],
                                          start=(ci == 0), stop=(ci == 15))
                    s_xmm.inc(ii)
                    ii = None
                    for ci in range(16):
                        for tb in (tp2 * 2, tp2 * 2 + 1):
                            ii = t.matmul(PS[:, 1024 + (tb % 2) * 512: 1024 + (tb % 2 + 1) * 512],
                                          q2t[:, sl * 2048 + ci * 128: sl * 2048 + (ci + 1) * 128],
                                          xhat[:, ci * 2048 + tb * 512: ci * 2048 + (tb + 1) * 512],
                                          start=(ci == 0), stop=(ci == 15))
                    s_xmm.inc(ii)

            emit_tp(0)
            for hc in range(1, 44):
                emit_tp(hc)
                emit_mm(hc - 1)
            emit_mm(43)

        @ph.block.scalar
        def _(s):
            s.wait_ge(s_hcp.h, 1)
            for hc in range(44):
                for tb in range(4):
                    it = hc * 4 + tb
                    s.wait_ge(s_xmm.h, hc * 4 + (tb // 2) * 2 + 1)
                    if it >= 2:
                        s.wait_ge(s_hd.h, it - 1)
                    s_sg.inc(s.activation(sgb[:, (it % 2) * 512:(it % 2 + 1) * 512],
                                          PS[:, (it % 2) * 512:(it % 2 + 1) * 512],
                                          AF.Sigmoid))

        st0.close()
        st2 = ExitStack()
        w3b = ph.sbuf("w3b", [128, 2 * 5632], F32, st2)
        p3s = ph.sbuf("p3s", [128, 5632], F32, st2)
        q3n = ph.sbuf("q3n", [128, 2 * 5632], BF16, st2)
        q3t = ph.sbuf("q3t", [128, 2 * 44 * 512], BF16, st2)
        hc3 = ph.sbuf("hc3", [128, 2 * 5632], BF16, st2)
        fdb = ph.sbuf("fdb", [128, 2 * 512], F32, st2)

        s_w3ld = [ph.sem("w3ld0"), ph.sem("w3ld1")]
        s_q3z = ph.sem("q3z")
        s_q3tp = ph.sem("q3tp")
        s_q3ev = ph.sem("q3ev")
        s_h3ld = [ph.sem("h3ld0"), ph.sem("h3ld1")]
        s_fmm = ph.sem("fmmB")
        s_fevB = ph.sem("fevB")
        s_fst = [ph.sem("fst0"), ph.sem("fst1")]

        NG3 = 11

        @ph.block.sync
        def _(sync):
            sync.wait_ge(s_hst[0].h, 88 * 16)
            sync.wait_ge(s_hst[1].h, 88 * 16)
            def emit_fio(dbx):
                for tt in range(16):
                    it = dbx * 16 + tt
                    sl = it % 2
                    if it >= 2:
                        sync.wait_ge(s_fmm.h, it - 1)
                    s_h3ld[sl].inc(sync.dma_start(
                        out=hc3[:, sl * 5632: sl * 5632 + 44 * 128]
                        .rearrange("p (hcx t) -> p hcx t", t=128),
                        in_=io["h3"][:, tt].rearrange("hcx p t -> p hcx t")), 16)
                    sync.wait_ge(s_fevB.h, it + 1)
                    s_fst[sl].inc(sync.dma_start(
                        out=io["fsc"][tt * 128:(tt + 1) * 128, dbx * 512:(dbx + 1) * 512],
                        in_=fdb[:, sl * 512:(sl + 1) * 512]), 16)

            for db in range(4):
                for dr in range(4):
                    k = db * 4 + dr
                    sl = k % 2
                    if k >= 2:
                        sync.wait_ge(s_q3z.h, k - 1)
                    s_w3ld[sl].inc(sync.dma_start(
                        out=w3b[:, sl * 5632:(sl + 1) * 5632],
                        in_=io["w3"][k * 128:(k + 1) * 128, :]), 16)
                if db >= 1:
                    emit_fio(db - 1)
            emit_fio(3)

        @ph.block.vector
        def _(v):
            for db in range(4):
                for dr in range(4):
                    k = db * 4 + dr
                    sl = k % 2
                    v.wait_ge(s_w3ld[sl].h, (k // 2 + 1) * 16)
                    if k >= 2:
                        v.wait_ge(s_q3tp.h, NG3 * (k - 1))
                    edge(v, vseq, v.tensor_scalar(p3s[:, :], w3b[:, sl * 5632:(sl + 1) * 5632],
                                                  hcol[:, 2:3], -1.0, ALU.is_ge, ALU.add))
                    edge(v, vseq, v.scalar_tensor_tensor(
                        q3n[:, sl * 5632:(sl + 1) * 5632], w3b[:, sl * 5632:(sl + 1) * 5632],
                        hcol[:, 5:6], p3s[:, :], ALU.is_ge, ALU.add))
                    v.sem_inc(s_q3z.h, 1)
                    s_q3z.n += 1
                    for g in range(NG3):
                        G = k * NG3 + g
                        v.wait_ge(s_q3tp.h, G + 1)
                        if db > 1 and dr == 0 and g == 0:
                            v.wait_ge(s_fmm.h, (db - 1) * 16)
                        qoff = (db % 2) * 44 * 512
                        ii = None
                        for j in range(4):
                            hcx = g * 4 + j
                            if hcx >= 44:
                                break
                            ii = v.tensor_copy(
                                q3t[:, qoff + hcx * 512 + dr * 128:
                                    qoff + hcx * 512 + (dr + 1) * 128],
                                psbf[:, (G % 2) * 1024 + j * 128:
                                     (G % 2) * 1024 + (j + 1) * 128])
                        s_q3ev.inc(ii)

        @ph.block.tensor
        def _(t):
            t.wait_ge(s_hst[0].h, 88 * 16)
            t.wait_ge(s_hst[1].h, 88 * 16)

            def emit_tp3(db):
                for dr in range(4):
                    k = db * 4 + dr
                    sl = k % 2
                    t.wait_ge(s_q3z.h, k + 1)
                    for g in range(NG3):
                        G = k * NG3 + g
                        if G >= 2:
                            t.wait_ge(s_q3ev.h, G - 1)
                        warmB(t)
                        ii = None
                        for j in range(4):
                            hcx = g * 4 + j
                            if hcx >= 44:
                                break
                            ii = t.transpose(psbf[:, (G % 2) * 1024 + j * 128:
                                                  (G % 2) * 1024 + (j + 1) * 128],
                                             q3n[:, sl * 5632 + hcx * 128:
                                                 sl * 5632 + (hcx + 1) * 128],
                                             ident[:, :])
                        s_q3tp.inc(ii)

            def emit_mm3(db):
                qoff = (db % 2) * 44 * 512
                t.wait_ge(s_q3ev.h, NG3 * 4 * (db + 1))
                for tt in range(16):
                    it = db * 16 + tt
                    sl = it % 2
                    t.wait_ge(s_h3ld[sl].h, (it // 2 + 1) * 16)
                    if it >= 2:
                        t.wait_ge(s_fevB.h, it - 1)
                    ii = None
                    for hcx in range(44):
                        ii = t.matmul(PS[:, sl * 512:(sl + 1) * 512],
                                      hc3[:, sl * 5632 + hcx * 128: sl * 5632 + (hcx + 1) * 128],
                                      q3t[:, qoff + hcx * 512: qoff + hcx * 512 + 512],
                                      start=(hcx == 0), stop=(hcx == 43))
                    s_fmm.inc(ii)

            emit_tp3(0)
            for db in range(1, 4):
                emit_tp3(db)
                emit_mm3(db - 1)
            emit_mm3(3)

        @ph.block.scalar
        def _(s):
            for db in range(4):
                for tt in range(16):
                    it = db * 16 + tt
                    sl = it % 2
                    s.wait_ge(s_fmm.h, it + 1)
                    if it >= 2:
                        s.wait_ge(s_fst[sl].h, (it // 2) * 16)
                    s_fevB.inc(s.copy(fdb[:, sl * 512:(sl + 1) * 512],
                                      PS[:, sl * 512:(sl + 1) * 512]))

        st2.close()
        fb = ph.sbuf("fb", [128, 2 * 2048], F32)
        xr = ph.sbuf("xr", [128, 2 * 2048], F32)
        ob = ph.sbuf("ob", [128, 2 * 2048], F32)
        lnf = ph.sbuf("lnfB", [128, 2048], F32)
        fb6 = ph.sbuf("fb6B", [128, 24], F32)
        fmv = ph.sbuf("fmvB", [128, 2], F32)
        frs = ph.sbuf("frsB", [128, 1], F32)
        fnm = ph.sbuf("fnmB", [128, 1], F32)

        s_fld = [ph.sem("fld0"), ph.sem("fld1")]
        s_xld = [ph.sem("xld0"), ph.sem("xld1")]
        sems3 = {n: ph.sem(n + "3") for n in
                 ["fbn", "fmv", "frs", "fnm", "fap", "res"]}
        s_ost = [ph.sem("ostB0"), ph.sem("ostB1")]

        @ph.block.sync
        def _(sync):
            sync.wait_ge(s_fst[0].h, 32 * 16)
            sync.wait_ge(s_fst[1].h, 32 * 16)
            for tc in range(16):
                sl = tc % 2
                if tc >= 2:
                    sync.wait_ge(sems3["fbn"].h, tc - 1)
                s_fld[sl].inc(sync.dma_start(
                    out=fb[:, sl * 2048:(sl + 1) * 2048],
                    in_=io["fsc"][tc * 128:(tc + 1) * 128, :]), 16)
                if tc >= 2:
                    sync.wait_ge(sems3["res"].h, tc - 1)
                s_xld[sl].inc(sync.dma_start(
                    out=xr[:, sl * 2048:(sl + 1) * 2048],
                    in_=io["x1"][tc * 128:(tc + 1) * 128, :]), 16)
                sync.wait_ge(sems3["res"].h, tc + 1)
                s_ost[sl].inc(sync.dma_start(
                    out=io["x2"][tc * 128:(tc + 1) * 128, :],
                    in_=ob[:, sl * 2048:(sl + 1) * 2048]), 16)
            sync.wait_ge(s_ost[0].h, 8 * 16)
            sync.wait_ge(s_ost[1].h, 8 * 16)

        @ph.block.vector
        def _(v):
            for tc in range(16):
                sl = tc % 2
                v.wait_ge(s_fld[sl].h, (tc // 2 + 1) * 16)
                if tc >= 2:
                    v.wait_ge(sems3["fap"].h, tc - 1)
                ii = None
                for j in range(4):
                    ii = v.bn_stats(fb6[:, j * 6:(j + 1) * 6],
                                    fb[:, sl * 2048 + j * 512: sl * 2048 + (j + 1) * 512])
                sems3["fbn"].inc(ii)
                v.wait_ge(sems3["fbn"].h, tc + 1)
                sems3["fmv"].inc(v.bn_aggr(fmv[:, :], fb6[:, :]))
                v.wait_ge(sems3["fmv"].h, tc + 1)
                v.wait_ge(sems3["frs"].h, tc + 1)
                sems3["fnm"].inc(v.tensor_scalar(fnm[:, :], fmv[:, 0:1], frs[:, :],
                                                 -1.0, ALU.mult, ALU.mult))
                v.wait_ge(sems3["fap"].h, tc + 1)
                v.wait_ge(s_xld[sl].h, (tc // 2 + 1) * 16)
                if tc >= 2:
                    v.wait_ge(s_ost[sl].h, (tc // 2) * 16)
                sems3["res"].inc(v.tensor_tensor(ob[:, sl * 2048:(sl + 1) * 2048],
                                                 lnf[:, :], xr[:, sl * 2048:(sl + 1) * 2048],
                                                 ALU.add))

        @ph.block.scalar
        def _(s):
            for tc in range(16):
                sl = tc % 2
                s.wait_ge(sems3["fmv"].h, tc + 1)
                edge(s, sseq, s.activation(frs[:, :], fmv[:, 1:2], AF.Ln, bias=eps[:, :]))
                sems3["frs"].inc(s.activation(frs[:, :], frs[:, :], AF.Exp, scale=-0.5))
                s.wait_ge(sems3["frs"].h, tc + 1)
                s.wait_ge(sems3["fnm"].h, tc + 1)
                if tc >= 1:
                    s.wait_ge(sems3["res"].h, tc)
                sems3["fap"].inc(s.activation(lnf[:, :], fb[:, sl * 2048:(sl + 1) * 2048],
                                              AF.Identity, scale=frs[:, :], bias=fnm[:, :]))
    return nc



def build_phase_c(nc, io, use_cc=True):
    build_phase_c_pre(nc, io)
    if use_cc:
        pairs = [[0, 1], [2, 3], [4, 5], [6, 7]]
        with ExitStack() as stack:
            block = stack.enter_context(nc.Block())
            ph = Phase(nc, stack, block, "Ccc")
            s_cc = ph.sem("cc")

            @ph.block.gpsimd
            def _(g):
                for src, dst in [("kvc1", "kvg1"), ("kvc2", "kvg2"),
                                 ("kvcv", "kvgv")]:
                    s_cc.inc(g.collective_compute(
                        "AllGather", ALU.bypass, replica_groups=pairs,
                        ins=[io[src + "_t"].ap().opt()],
                        outs=[io[dst + "_t"].ap().opt()]))
                g.wait_ge(s_cc.h, 3)
    build_phase_c_post(nc, io)
    return nc


def build_phase_c_pre(nc, io):
    with ExitStack() as stack:
        block = stack.enter_context(nc.Block())
        ph = Phase(nc, stack, block, "C")

        ident = ph.sbuf("ident", [128, 128], BF16)
        eps = ph.sbuf("eps", [128, 1], F32)
        ones_b = ph.sbuf("onesbC", [128, 1], BF16)
        psbf = ph.psum("psbf", [128, 2048], BF16)
        PS = ph.psum("ps", [128, 3072], F32)
        vseq = ph.sem("vseq")
        s_pre = ph.sem("pre")
        warmC = make_warm(ph, PS, ones_b, ident[:, 0:64], 2048)

        @ph.block.sync
        def _(sync):
            s_pre.inc(sync.dma_start(out=ident[:, :], in_=io["ident"][:, :]), 16)

        @ph.block.vector
        def _(v):
            v.memset(eps[:, :], 1e-5)
            v.memset(ones_b[:, :], 1.0)
            v.wait_ge(s_pre.h, 16)

        st0 = ExitStack()
        stLN = ExitStack()
        xhat = ph.sbuf("xhat", [128, 16 * 2048], BF16, st0)

        def dstc(ci, tc):
            return xhat[:, ci * 2048 + tc * 128: ci * 2048 + (tc + 1) * 128]

        def dstgc(g, tc):
            return xhat[:, :].rearrange("p (ci t) -> p ci t", t=2048)[
                :, g * 4:(g + 1) * 4, tc * 128:(tc + 1) * 128]

        lns = emit_ln_xhat(ph, src_hbm=io["x2"], n_tc=16, dst_tiles=dstc,
                           ident=ident, eps=eps, psbf=psbf, name="lnc",
                           bufstack=stLN, dst_group=dstgc, warm=warmC)
        s_ev = lns["ev"]
        EV_ALL = s_ev.n
        stLN.close()

        wnat = ph.sbuf("wnat", [128, 2 * 2048], BF16, st0)
        wtb = ph.sbuf("wtb", [128, 2 * 2048], BF16, st0)
        pev = ph.sbuf("pev", [128, 2 * 2048], BF16, st0)

        W_LIST = [("k1w", "kvc1"), ("k2w", "kvc2"), ("vw", "kvcv"),
                  ("q1w", "qsc1"), ("q2w", "qsc2")]
        s_wn = [ph.sem("wn0"), ph.sem("wn1")]
        s_wtp = ph.sem("wtp")
        s_wev = ph.sem("wev")
        s_pmm = ph.sem("pmmC")
        s_pevs = ph.sem("pevsC")
        s_pstg = [ph.sem("pstg0"), ph.sem("pstg1")]
        s_pw = ph.sem("pw")
        s_cc = ph.sem("ccC")

        pairs = [[0, 1], [2, 3], [4, 5], [6, 7]]
        AG_LIST = [("kvc1", "kvg1"), ("kvc2", "kvg2"), ("kvcv", "kvgv")]

        @ph.block.gpsimd
        def _(g):
            g.wait_ge(s_ev.h, EV_ALL)
            for u in range(80):
                wi, ut = u // 16, u % 16
                wsrc = W_LIST[wi][0]
                sl = u % 2
                if u >= 2:
                    g.wait_ge(s_wtp.h, 4 * (u - 1))
                s_wn[sl].inc(g.dma_start(
                    out=wnat[:, sl * 2048:(sl + 1) * 2048],
                    in_=io[wsrc][ut * 128:(ut + 1) * 128, :]), 16)
                if u in (17, 33, 49):
                    agw = (u - 17) // 16
                    g.wait_ge(s_pw.h, agw + 1)
                    src, dst = AG_LIST[agw]
                    s_cc.inc(g.collective_compute(
                        "AllGather", ALU.bypass, replica_groups=pairs,
                        ins=[io[src + "_t"].ap().opt()],
                        outs=[io[dst + "_t"].ap().opt()]))
            g.wait_ge(s_cc.h, 3)

        @ph.block.tensor
        def _(t):
            t.wait_ge(s_ev.h, EV_ALL)
            for u in range(80):
                wi, ut = u // 16, u % 16
                sl = u % 2
                t.wait_ge(s_wn[sl].h, (u // 2 + 1) * 16)
                for g4 in range(4):
                    G = u * 4 + g4
                    if G >= 2:
                        t.wait_ge(s_wev.h, G - 1)
                    warmC(t)
                    ii = None
                    for j in range(4):
                        ci = g4 * 4 + j
                        ii = t.transpose(psbf[:, (G % 2) * 1024 + j * 128:
                                              (G % 2) * 1024 + (j + 1) * 128],
                                         wnat[:, sl * 2048 + ci * 128:
                                              sl * 2048 + (ci + 1) * 128],
                                         ident[:, :])
                    s_wtp.inc(ii)
                t.wait_ge(s_wev.h, 4 * (u + 1))
                if u >= 1:
                    t.wait_ge(s_pevs.h, u)
                ii = None
                for ci in range(16):
                    for mo in range(4):
                        ii = t.matmul(PS[:, mo * 512:(mo + 1) * 512],
                                      wtb[:, sl * 2048 + ci * 128: sl * 2048 + (ci + 1) * 128],
                                      xhat[:, ci * 2048 + mo * 512: ci * 2048 + (mo + 1) * 512],
                                      start=(ci == 0), stop=(ci == 15))
                s_pmm.inc(ii)

        @ph.block.vector
        def _(v):
            for u in range(80):
                wi, ut = u // 16, u % 16
                wsrc = W_LIST[wi][0]
                sl = u % 2
                for g4 in range(4):
                    G = u * 4 + g4
                    v.wait_ge(s_wtp.h, G + 1)
                    if u >= 2 and g4 == 0:
                        v.wait_ge(s_pmm.h, u - 1)
                    ii = None
                    for j in range(4):
                        ci = g4 * 4 + j
                        ii = v.tensor_copy(
                            wtb[:, sl * 2048 + ci * 128: sl * 2048 + (ci + 1) * 128],
                            psbf[:, (G % 2) * 1024 + j * 128:
                                 (G % 2) * 1024 + (j + 1) * 128])
                    s_wev.inc(ii)
                v.wait_ge(s_pmm.h, u + 1)
                if u >= 2:
                    v.wait_ge(s_pstg[u % 2].h, (u // 2) * 16)
                s_pevs.inc(v.tensor_copy(pev[:, sl * 2048:(sl + 1) * 2048],
                                         PS[:, 0:2048]))

        @ph.block.sync
        def _(sync):
            for u in range(80):
                wi, ut = u // 16, u % 16
                wsrc, wdst = W_LIST[wi]
                sl = u % 2
                sync.wait_ge(s_pevs.h, u + 1)
                s_pstg[sl].inc(sync.dma_start(
                    out=io[wdst][ut * 128:(ut + 1) * 128, :],
                    in_=pev[:, sl * 2048:(sl + 1) * 2048]), 16)
                if u in (15, 31, 47):
                    k = (u + 1) // 2
                    sync.wait_ge(s_pstg[0].h, k * 16)
                    sync.wait_ge(s_pstg[1].h, k * 16)
                    sync.sem_inc(s_pw.h, 1)
                    s_pw.n += 1
            sync.wait_ge(s_pstg[0].h, 40 * 16)
            sync.wait_ge(s_pstg[1].h, 40 * 16)
        st0.close()
    return nc


def build_phase_c_post(nc, io):
    with ExitStack() as stack:
        block = stack.enter_context(nc.Block())
        ph = Phase(nc, stack, block, "Cp")

        ident = ph.sbuf("ident", [128, 128], BF16)
        eps = ph.sbuf("eps", [128, 1], F32)
        ones_bf = ph.sbuf("onesb", [128, 1], BF16)
        ones_row = ph.sbuf("onesr", [1, 128], BF16)
        lam = ph.sbuf("lam", [128, 16], F32)
        psbf = ph.psum("psbf", [128, 2048], BF16)
        PS = ph.psum("ps", [128, 3072], F32)
        vseq = ph.sem("vseq")
        sseq = ph.sem("sseq")
        s_pre = ph.sem("pre")

        @ph.block.sync
        def _(sync):
            s_pre.inc(sync.dma_start(out=ident[:, :], in_=io["ident"][:, :]), 16)
            s_pre.inc(sync.dma_start(out=lam[:, :], in_=io["lam_b"][:, :]), 16)

        @ph.block.vector
        def _(v):
            v.memset(eps[:, :], 1e-5)
            v.memset(ones_bf[:, :], 1.0)
            v.memset(ones_row[:, :], 1.0)
            v.wait_ge(s_pre.h, 32)

        st3 = ExitStack()
        k1h = ph.sbuf("k1h", [128, 2 * 4096], BF16, st3)
        k2h = ph.sbuf("k2h", [128, 2 * 4096], BF16, st3)
        vth = ph.sbuf("vth", [128, 2 * 4096], BF16, st3)
        vh = ph.sbuf("vh", [128, 2 * 4096], BF16, st3)
        q1h = ph.sbuf("q1h", [128, 2 * 2048], BF16, st3)
        q2h = ph.sbuf("q2h", [128, 2 * 2048], BF16, st3)
        pb = ph.sbuf("pb", [128, 2 * 1024], BF16, st3)
        dacc = ph.sbuf("daccC", [128, 1024], F16, st3)
        dsum = ph.sbuf("dsumC", [1, 1024], F32, st3)
        rec = ph.sbuf("recC", [1, 1024], F32, st3)
        recbf = ph.sbuf("recbfC", [1, 1024], BF16, st3)
        rsb = ph.sbuf("rsbC", [128, 1024], F32, st3)
        out1 = ph.sbuf("out1", [128, 1024], F32, st3)
        t2b = ph.sbuf("t2b", [128, 1024], F32, st3)
        atb = ph.sbuf("atbC", [128, 2 * 1024], BF16, st3)

        s_kld = [ph.sem("kld0"), ph.sem("kld1")]
        s_vtp = ph.sem("vtp")
        s_vev2 = ph.sem("vev2")
        s_smm = ph.sem("smmC")
        s_exp = ph.sem("expC")
        s_dac = ph.sem("dacC")
        s_omm = ph.sem("ommC")
        s_pod = ph.sem("pod")
        s_dbf = ph.sem("dbfsC")
        s_dmm = ph.sem("dmmC")
        s_dsm = ph.sem("dsmC")
        s_rc = ph.sem("rcC")
        s_rmm = ph.sem("rmmC")
        s_rsbs = ph.sem("rsbsC")
        s_rsd = ph.sem("rsd")
        s_nrm = ph.sem("nrmC")
        s_ast = [ph.sem("astC0"), ph.sem("astC1")]

        def its(h, qb, m):
            return (h * 2 + qb) * 2 + m

        @ph.block.sync
        def _(sync):
            for h in range(16):
                sl = h % 2
                if h >= 2:
                    sync.wait_ge(s_smm.h, 128 * (h - 1))
                    sync.wait_ge(s_vtp.h, 8 * (h - 1))
                for name, buf, src in [("k1", k1h, "kvg1"), ("k2", k2h, "kvg2"),
                                       ("vt", vth, "kvgv")]:
                    s_kld[sl].inc(sync.dma_start(
                        out=buf[:, sl * 4096: sl * 4096 + 2048],
                        in_=io[src][h * 128:(h + 1) * 128, :]), 16)
                    s_kld[sl].inc(sync.dma_start(
                        out=buf[:, sl * 4096 + 2048: sl * 4096 + 4096],
                        in_=io[src][2048 + h * 128: 2048 + (h + 1) * 128, :]), 16)
                s_kld[sl].inc(sync.dma_start(
                    out=q1h[:, sl * 2048:(sl + 1) * 2048],
                    in_=io["qsc1"][h * 128:(h + 1) * 128, :]), 16)
                s_kld[sl].inc(sync.dma_start(
                    out=q2h[:, sl * 2048:(sl + 1) * 2048],
                    in_=io["qsc2"][h * 128:(h + 1) * 128, :]), 16)
                if h >= 1:
                    for qb in range(2):
                        itq = (h - 1) * 2 + qb
                        sync.wait_ge(s_nrm.h, itq + 1)
                        s_ast[itq % 2].inc(sync.dma_start(
                            out=io["at3c"][h - 1, qb * 8:(qb + 1) * 8]
                            .rearrange("j p t -> p j t"),
                            in_=atb[:, (itq % 2) * 1024:(itq % 2 + 1) * 1024]
                            .rearrange("p (j t) -> p j t", j=8)), 16)
            for qb in range(2):
                itq = 15 * 2 + qb
                sync.wait_ge(s_nrm.h, itq + 1)
                s_ast[itq % 2].inc(sync.dma_start(
                    out=io["at3c"][15, qb * 8:(qb + 1) * 8]
                    .rearrange("j p t -> p j t"),
                    in_=atb[:, (itq % 2) * 1024:(itq % 2 + 1) * 1024]
                    .rearrange("p (j t) -> p j t", j=8)), 16)

        @ph.block.tensor
        def _(t):
            t.wait_ge(s_pre.h, 32)
            for h in range(16):
                sl = h % 2
                t.wait_ge(s_kld[sl].h, (h // 2 + 1) * 8 * 16)
                for g4 in range(8):
                    G = h * 8 + g4
                    if G >= 2:
                        t.wait_ge(s_vev2.h, G - 1)
                    ii = None
                    for j in range(4):
                        kc = g4 * 4 + j
                        ii = t.transpose(psbf[:, (G % 2) * 1024 + j * 128:
                                              (G % 2) * 1024 + (j + 1) * 128],
                                         vth[:, sl * 4096 + kc * 128: sl * 4096 + (kc + 1) * 128],
                                         ident[:, :])
                    s_vtp.inc(ii)
                t.wait_ge(s_vev2.h, 8 * (h + 1))
                for qb in range(2):
                    for m in range(2):
                        it2 = its(h, qb, m)
                        kmh = k1h if m == 0 else k2h
                        qmh = q1h if m == 0 else q2h
                        if it2 >= 1:
                            t.wait_ge(s_pod.h, it2)
                        ii_o = None

                        def emit_score(kc):
                            ic = it2 * 32 + kc
                            if ic >= 2:
                                t.wait_ge(s_exp.h, ic - 1)
                            if kc == 0 and it2 >= 1:
                                t.wait_ge(s_dsm.h, it2)
                            if kc == 1 and it2 >= 1:
                                t.wait_ge(s_rsbs.h, it2)
                            ii = None
                            for mo in range(2):
                                ii = t.matmul(
                                    PS[:, (ic % 2) * 1024 + mo * 512:
                                       (ic % 2) * 1024 + (mo + 1) * 512],
                                    kmh[:, sl * 4096 + kc * 128: sl * 4096 + (kc + 1) * 128],
                                    qmh[:, sl * 2048 + qb * 1024 + mo * 512:
                                        sl * 2048 + qb * 1024 + (mo + 1) * 512],
                                    start=True, stop=True)
                            s_smm.inc(ii)

                        def emit_av(kc):
                            nonlocal ii_o
                            ic = it2 * 32 + kc
                            t.wait_ge(s_exp.h, ic + 1)
                            for mo in range(2):
                                ii_o = t.matmul(
                                    PS[:, 2048 + mo * 512: 2048 + (mo + 1) * 512],
                                    vh[:, sl * 4096 + kc * 128: sl * 4096 + (kc + 1) * 128],
                                    pb[:, (ic % 2) * 1024 + mo * 512:
                                       (ic % 2) * 1024 + (mo + 1) * 512],
                                    start=(kc == 0), stop=(kc == 31))

                        emit_score(0)
                        for kc in range(1, 32):
                            emit_score(kc)
                            emit_av(kc - 1)
                        emit_av(31)
                        s_omm.inc(ii_o)
                        t.wait_ge(s_dbf.h, it2 + 1)
                        t.matmul(PS[0:1, 0:512], ones_bf[:, :],
                                 dacc[:, 0:512], start=True, stop=True)
                        s_dmm.inc(t.matmul(PS[0:1, 512:1024], ones_bf[:, :],
                                           dacc[:, 512:1024], start=True, stop=True))
                        t.wait_ge(s_rc.h, it2 + 1)
                        t.matmul(PS[:, 1024:1536], ones_row[:, :],
                                 recbf[:, 0:512], start=True, stop=True)
                        s_rmm.inc(t.matmul(PS[:, 1536:2048], ones_row[:, :],
                                           recbf[:, 512:1024], start=True, stop=True))

        @ph.block.scalar
        def _(s):
            for h in range(16):
                for qb in range(2):
                    for m in range(2):
                        it2 = its(h, qb, m)
                        for kc in range(32):
                            ic = it2 * 32 + kc
                            s.wait_ge(s_smm.h, ic + 1)
                            if ic >= 2:
                                s.wait_ge(s_dac.h, ic - 1)
                            s_exp.inc(s.activation(
                                pb[:, (ic % 2) * 1024:(ic % 2 + 1) * 1024],
                                PS[:, (ic % 2) * 1024:(ic % 2 + 1) * 1024],
                                AF.Exp, scale=INVSQ_HD))
                        s.wait_ge(s_dmm.h, it2 + 1)
                        s_dsm.inc(s.copy(dsum[:, :], PS[0:1, 0:1024]))
                        s.wait_ge(s_rmm.h, it2 + 1)
                        if it2 >= 1:
                            s.wait_ge(s_rsd.h, it2)
                        s_rsbs.inc(s.copy(rsb[:, :], PS[:, 1024:2048]))

        @ph.block.vector
        def _(v):
            for h in range(16):
                sl = h % 2
                for g4 in range(8):
                    G = h * 8 + g4
                    v.wait_ge(s_vtp.h, G + 1)
                    if h >= 2 and g4 == 0:
                        v.wait_ge(s_omm.h, (h - 1) * 4)
                    ii = None
                    for j in range(4):
                        kc = g4 * 4 + j
                        ii = v.tensor_copy(
                            vh[:, sl * 4096 + kc * 128: sl * 4096 + (kc + 1) * 128],
                            psbf[:, (G % 2) * 1024 + j * 128:
                                 (G % 2) * 1024 + (j + 1) * 128])
                    s_vev2.inc(ii)
                for qb in range(2):
                    for m in range(2):
                        it2 = its(h, qb, m)
                        for kc in range(32):
                            ic = it2 * 32 + kc
                            v.wait_ge(s_exp.h, ic + 1)
                            if kc == 0:
                                if it2 >= 1:
                                    v.wait_ge(s_dmm.h, it2)
                                edge(v, vseq, v.tensor_copy(
                                    dacc[:, :], pb[:, (ic % 2) * 1024:(ic % 2 + 1) * 1024]))
                            else:
                                edge(v, vseq, v.tensor_tensor(
                                    dacc[:, :], dacc[:, :],
                                    pb[:, (ic % 2) * 1024:(ic % 2 + 1) * 1024], ALU.add))
                            v.sem_inc(s_dac.h, 1)
                            s_dac.n += 1
                        v.sem_inc(s_dbf.h, 1)
                        s_dbf.n += 1
                        v.wait_ge(s_dsm.h, it2 + 1)
                        if it2 >= 1:
                            v.wait_ge(s_rmm.h, it2)
                        edge(v, vseq, v.reciprocal(rec[:, :], dsum[:, :]))
                        s_rc.inc(v.tensor_copy(recbf[:, :], rec[:, :]))
                        v.wait_ge(s_rsbs.h, it2 + 1)
                        v.wait_ge(s_omm.h, it2 + 1)
                        if m == 0:
                            edge(v, vseq, v.tensor_tensor(out1[:, :], PS[:, 2048:3072],
                                                          rsb[:, :], ALU.mult))
                            v.sem_inc(s_pod.h, 1)
                            s_pod.n += 1
                            v.sem_inc(s_rsd.h, 1)
                            s_rsd.n += 1
                        else:
                            itq = h * 2 + qb
                            edge(v, vseq, v.tensor_tensor(t2b[:, :], PS[:, 2048:3072],
                                                          rsb[:, :], ALU.mult))
                            v.sem_inc(s_pod.h, 1)
                            s_pod.n += 1
                            v.sem_inc(s_rsd.h, 1)
                            s_rsd.n += 1
                            if itq >= 2:
                                v.wait_ge(s_ast[itq % 2].h, (itq // 2) * 16)
                            edge(v, vseq, v.scalar_tensor_tensor(
                                atb[:, (itq % 2) * 1024:(itq % 2 + 1) * 1024],
                                t2b[:, :], lam[:, h:h + 1], out1[:, :],
                                ALU.mult, ALU.add))
                            v.sem_inc(s_nrm.h, 1)
                            s_nrm.n += 1

        st3.close()
        owt = ph.sbuf("owt", [128, 16 * 2048], BF16)
        ownat = ph.sbuf("ownat", [128, 2 * 2048], BF16)
        atc = ph.sbuf("atcC", [128, 2 * 2048], BF16)
        fbuf = ph.sbuf("fbufC", [128, 2 * 2048], F32)
        lnf = ph.sbuf("lnfC", [128, 2048], F32)
        xo = ph.sbuf("xoC", [128, 2 * 2048], F32)
        obuf = ph.sbuf("obufC", [128, 2 * 2048], F32)
        fb6 = ph.sbuf("fb6C", [128, 24], F32)
        fmv = ph.sbuf("fmvC", [128, 2], F32)
        frs = ph.sbuf("frsC", [128, 1], F32)
        fnm = ph.sbuf("fnmC", [128, 1], F32)

        s_ownl = [ph.sem("ownl0"), ph.sem("ownl1")]
        s_owtp = ph.sem("owtp")
        s_owev = ph.sem("owev")
        s_atc = [ph.sem("atcC0"), ph.sem("atcC1")]
        s_xoC = [ph.sem("xoC0"), ph.sem("xoC1")]
        s_fmm = ph.sem("fmmC")
        s_fev = ph.sem("fevC")
        s_fbn = ph.sem("fbnC")
        s_fmv = ph.sem("fmvsC")
        s_frs = ph.sem("frssC")
        s_fnm = ph.sem("fnmsC")
        s_fap = ph.sem("fapC")
        s_res = ph.sem("resC")
        s_ost = [ph.sem("ostC0"), ph.sem("ostC1")]

        @ph.block.gpsimd
        def _(g):
            g.wait_ge(s_ast[0].h, 16 * 16)
            g.wait_ge(s_ast[1].h, 16 * 16)
            for dr in range(16):
                sl = dr % 2
                if dr >= 2:
                    g.wait_ge(s_owtp.h, 4 * (dr - 1))
                s_ownl[sl].inc(g.dma_start(
                    out=ownat[:, sl * 2048:(sl + 1) * 2048],
                    in_=io["ow"][dr * 128:(dr + 1) * 128, :]), 16)

        @ph.block.tensor
        def _(t):
            for dr in range(16):
                sl = dr % 2
                t.wait_ge(s_ownl[sl].h, (dr // 2 + 1) * 16)
                for g4 in range(4):
                    G = dr * 4 + g4
                    if G >= 2:
                        t.wait_ge(s_owev.h, G - 1)
                    ii = None
                    for j in range(4):
                        hc = g4 * 4 + j
                        ii = t.transpose(psbf[:, (G % 2) * 1024 + j * 128:
                                              (G % 2) * 1024 + (j + 1) * 128],
                                         ownat[:, sl * 2048 + hc * 128:
                                               sl * 2048 + (hc + 1) * 128],
                                         ident[:, :])
                    s_owtp.inc(ii)
            for tt in range(16):
                slt = tt % 2
                t.wait_ge(s_atc[slt].h, (tt // 2 + 1) * 16)
                if tt == 0:
                    t.wait_ge(s_owev.h, 64)
                if tt >= 1:
                    t.wait_ge(s_fev.h, tt)
                ii = None
                for hc in range(16):
                    for dc in range(4):
                        ii = t.matmul(PS[:, dc * 512:(dc + 1) * 512],
                                      atc[:, slt * 2048 + hc * 128: slt * 2048 + (hc + 1) * 128],
                                      owt[:, hc * 2048 + dc * 512: hc * 2048 + (dc + 1) * 512],
                                      start=(hc == 0), stop=(hc == 15))
                s_fmm.inc(ii)

        @ph.block.vector
        def _(v):
            for dr in range(16):
                sl = dr % 2
                for g4 in range(4):
                    G = dr * 4 + g4
                    v.wait_ge(s_owtp.h, G + 1)
                    ii = None
                    for j in range(4):
                        hc = g4 * 4 + j
                        ii = v.tensor_copy(
                            owt[:, hc * 2048 + dr * 128: hc * 2048 + (dr + 1) * 128],
                            psbf[:, (G % 2) * 1024 + j * 128:
                                 (G % 2) * 1024 + (j + 1) * 128])
                    s_owev.inc(ii)
            for tt in range(16):
                slt = tt % 2
                v.wait_ge(s_fev.h, tt + 1)
                ii = None
                for j in range(4):
                    ii = v.bn_stats(fb6[:, j * 6:(j + 1) * 6],
                                    fbuf[:, slt * 2048 + j * 512: slt * 2048 + (j + 1) * 512])
                s_fbn.inc(ii)
                v.wait_ge(s_fbn.h, tt + 1)
                s_fmv.inc(v.bn_aggr(fmv[:, :], fb6[:, :]))
                v.wait_ge(s_fmv.h, tt + 1)
                v.wait_ge(s_frs.h, tt + 1)
                s_fnm.inc(v.tensor_scalar(fnm[:, :], fmv[:, 0:1], frs[:, :], -1.0,
                                          ALU.mult, ALU.mult))
                v.wait_ge(s_fap.h, tt + 1)
                v.wait_ge(s_xoC[slt].h, (tt // 2 + 1) * 16)
                if tt >= 2:
                    v.wait_ge(s_ost[slt].h, (tt // 2) * 16)
                s_res.inc(v.tensor_tensor(obuf[:, slt * 2048:(slt + 1) * 2048],
                                          lnf[:, :], xo[:, slt * 2048:(slt + 1) * 2048],
                                          ALU.add))

        @ph.block.scalar
        def _(s):
            for tt in range(16):
                slt = tt % 2
                s.wait_ge(s_fmm.h, tt + 1)
                if tt >= 2:
                    s.wait_ge(s_fbn.h, tt - 1)
                    s.wait_ge(s_fap.h, tt - 1)
                s_fev.inc(s.copy(fbuf[:, slt * 2048:(slt + 1) * 2048], PS[:, 0:2048]))
                s.wait_ge(s_fmv.h, tt + 1)
                edge(s, sseq, s.activation(frs[:, :], fmv[:, 1:2], AF.Ln, bias=eps[:, :]))
                s_frs.inc(s.activation(frs[:, :], frs[:, :], AF.Exp, scale=-0.5))
                s.wait_ge(s_frs.h, tt + 1)
                s.wait_ge(s_fnm.h, tt + 1)
                if tt >= 1:
                    s.wait_ge(s_res.h, tt)
                s_fap.inc(s.activation(lnf[:, :], fbuf[:, slt * 2048:(slt + 1) * 2048],
                                       AF.Identity, scale=frs[:, :], bias=fnm[:, :]))

        @ph.block.sync
        def _(sync):
            sync.wait_ge(s_ast[0].h, 16 * 16)
            sync.wait_ge(s_ast[1].h, 16 * 16)
            for tt in range(16):
                slt = tt % 2
                if tt >= 2:
                    sync.wait_ge(s_fmm.h, tt - 1)
                s_atc[slt].inc(sync.dma_start(
                    out=atc[:, slt * 2048:(slt + 1) * 2048]
                    .rearrange("p (hc t) -> p hc t", t=128),
                    in_=io["at3c"][:, tt].rearrange("hc p t -> p hc t")), 16)
                if tt >= 2:
                    sync.wait_ge(s_res.h, tt - 1)
                s_xoC[slt].inc(sync.dma_start(
                    out=xo[:, slt * 2048:(slt + 1) * 2048],
                    in_=io["x2"][tt * 128:(tt + 1) * 128, :]), 16)
                sync.wait_ge(s_res.h, tt + 1)
                s_ost[slt].inc(sync.dma_start(
                    out=io["x3"][tt * 128:(tt + 1) * 128, :],
                    in_=obuf[:, slt * 2048:(slt + 1) * 2048]), 16)
            sync.wait_ge(s_ost[0].h, 8 * 16)
            sync.wait_ge(s_ost[1].h, 8 * 16)
    return nc



def build_phase_d(nc, io, use_cc=True):
    pairs = [[0, 1], [2, 3], [4, 5], [6, 7]]
    with ExitStack() as stack:
        block = stack.enter_context(nc.Block())
        ph = Phase(nc, stack, block, "Dp")

        ident = ph.sbuf("ident", [128, 128], BF16)
        eps = ph.sbuf("eps", [128, 1], F32)
        ones_f = ph.sbuf("onesf", [128, 1], F32)
        onesr_f = ph.sbuf("onesrf", [1, 128], F32)
        psbf = ph.psum("psbf", [128, 2048], BF16)
        PS = ph.psum("ps", [128, 3072], F32)
        vseq = ph.sem("vseq")
        sseq = ph.sem("sseq")
        s_pre = ph.sem("pre")

        @ph.block.sync
        def _(sync):
            s_pre.inc(sync.dma_start(out=ident[:, :], in_=io["ident"][:, :]), 16)

        @ph.block.vector
        def _(v):
            v.memset(eps[:, :], 1e-5)
            v.memset(ones_f[:, :], 1.0)
            v.memset(onesr_f[:, :], 1.0)
            v.wait_ge(s_pre.h, 16)

        sr = ph.sbuf("sr", [128, 16], F32)
        s2 = ph.sbuf("s2", [128, 1], F32)
        gsc = ph.sbuf("gsc", [128, 512], BF16)
        simb = ph.sbuf("simb", [128, 16 * 32], F32)
        sqs = ph.sbuf("sqs", [128, 16 * 32], F32)
        smallf = ph.sbuf("smallf", [1, 64], F32)
        small2 = ph.sbuf("small2", [1, 40], F32)
        lgrow = ph.sbuf("lgrow", [1, 32], F32)
        lgc = ph.sbuf("lgc", [32, 1], F32)
        lrow = ph.sbuf("lrow", [32, 32], F32)
        m1 = ph.sbuf("m1", [32, 32], F32)
        e2 = ph.sbuf("e2", [32, 32], F32)
        ltm = ph.sbuf("ltm", [32, 32], F32)
        rankc = ph.sbuf("rankc", [32, 1], F32)
        fbc = ph.sbuf("fbc", [32, 1], F32)
        maskc = ph.sbuf("maskc", [32, 1], F32)
        gatedc = ph.sbuf("gatedc", [32, 1], F32)
        infb = ph.sbuf("infb", [32, 1], F32)
        zc = ph.sbuf("zc", [32, 1], F32)
        ezc = ph.sbuf("ezc", [32, 1], F32)
        rzb = ph.sbuf("rzb", [32, 1], F32)
        routc = ph.sbuf("routc", [32, 1], F32)
        gtsb = ph.sbuf("gtsb", [1, 32], F32)
        sgt = ph.sbuf("sgt", [1, 32], F32)
        rexp = ph.sbuf("rexp", [128, 16], F32)
        ir2 = ph.sbuf("ir2", [128, 64], BF16)
        pat = ph.sbuf("pat", [128, 16 * 64], BF16)
        selb = ph.sbuf("selb", [32, 2048], F32)

        st0 = ExitStack()
        stLN = ExitStack()
        xhat = ph.sbuf("xhatD", [128, 16 * 2048], BF16, st0)

        def dstd(ci, tc):
            return xhat[:, ci * 2048 + tc * 128: ci * 2048 + (tc + 1) * 128]

        def dstgd(g, tc):
            return xhat[:, :].rearrange("p (ci t) -> p ci t", t=2048)[
                :, g * 4:(g + 1) * 4, tc * 128:(tc + 1) * 128]

        lns = emit_ln_xhat(ph, src_hbm=io["x3"], n_tc=16, dst_tiles=dstd,
                           ident=ident, eps=eps, psbf=psbf, name="lnd",
                           bufstack=stLN, dst_group=dstgd)
        s_ev = lns["ev"]
        EV_ALL = s_ev.n
        stLN.close()

        s_srp = ph.sem("srp")
        s_sst = ph.sem("sst")
        s_cc = ph.sem("cc")
        s_srl = ph.sem("srl")

        @ph.block.vector
        def _(v):
            v.wait_ge(s_ev.h, EV_ALL)
            for ci in range(16):
                edge(v, vseq, v.reduce_sum(sr[:, ci:ci + 1],
                                           xhat[:, ci * 2048:(ci + 1) * 2048],
                                           mybir.AxisListType.X))
            v.sem_inc(s_srp.h, 1)
            s_srp.n += 1

        @ph.block.sync
        def _(sync):
            sync.wait_ge(s_srp.h, 1)
            s_sst.inc(sync.dma_start(out=io["srp"][:, :], in_=sr[:, :]), 16)

        @ph.block.gpsimd
        def _(g):
            g.wait_ge(s_sst.h, 16)
            if use_cc:
                s_cc.inc(g.collective_compute(
                    "AllReduce", ALU.add, replica_groups=pairs,
                    ins=[io["srp_t"].ap().opt()], outs=[io["srg_t"].ap().opt()]))
            else:
                g.sem_inc(s_cc.h, 1)
                s_cc.n += 1

        @ph.block.sync
        def _(sync):
            sync.wait_ge(s_cc.h, 1)
            s_srl.inc(sync.dma_start(out=sr[:, :], in_=io["srg"][:, :]), 16)
            s_srl.inc(sync.dma_start(
                out=simb[:, :].rearrange("p (ci e) -> p ci e", e=32),
                in_=io["sim"].rearrange("(ci p) e -> p ci e", p=128)), 16)
            s_srl.inc(sync.dma_start(out=gtsb[:, :], in_=io["gates"][:, :]), 16)
            s_srl.inc(sync.dma_start(out=ltm[:, :], in_=io["ltm"][:, :]), 16)
            s_srl.inc(sync.dma_start(out=ir2[:, :], in_=io["ir2"][:, :]), 16)
            s_srl.inc(sync.dma_start(out=selb[:, :], in_=io["selm"][:, :]), 16)

        wnat = ph.sbuf("wnatD", [128, 2 * 2048], BF16, st0)
        wtb = ph.sbuf("wtbD", [128, 2 * 2048], BF16, st0)
        vat = ph.sbuf("vat", [128, 16 * 2048], BF16, st0)

        s_wn = [ph.sem("wnD0"), ph.sem("wnD1")]
        s_wtp = ph.sem("wtpD")
        s_wev = ph.sem("wevD")
        s_vmm = ph.sem("vmmD")
        s_vev = ph.sem("vevD")

        @ph.block.gpsimd
        def _(g):
            g.wait_ge(s_ev.h, EV_ALL)
            for ut in range(16):
                sl = ut % 2
                if ut >= 2:
                    g.wait_ge(s_wtp.h, 4 * (ut - 1))
                s_wn[sl].inc(g.dma_start(
                    out=wnat[:, sl * 2048:(sl + 1) * 2048],
                    in_=io["wv"][ut * 128:(ut + 1) * 128, :]), 16)

        @ph.block.tensor
        def _(t):
            t.wait_ge(s_ev.h, EV_ALL)
            for ut in range(16):
                sl = ut % 2
                t.wait_ge(s_wn[sl].h, (ut // 2 + 1) * 16)
                for g4 in range(4):
                    G = ut * 4 + g4
                    if G >= 2:
                        t.wait_ge(s_wev.h, G - 1)
                    ii = None
                    for j in range(4):
                        ci = g4 * 4 + j
                        ii = t.transpose(psbf[:, (G % 2) * 1024 + j * 128:
                                              (G % 2) * 1024 + (j + 1) * 128],
                                         wnat[:, sl * 2048 + ci * 128:
                                              sl * 2048 + (ci + 1) * 128],
                                         ident[:, :])
                    s_wtp.inc(ii)
                t.wait_ge(s_wev.h, 4 * (ut + 1))
                if ut >= 1:
                    t.wait_ge(s_vev.h, ut)
                ii = None
                for ci in range(16):
                    for mo in range(4):
                        ii = t.matmul(PS[:, mo * 512:(mo + 1) * 512],
                                      wtb[:, sl * 2048 + ci * 128: sl * 2048 + (ci + 1) * 128],
                                      xhat[:, ci * 2048 + mo * 512: ci * 2048 + (mo + 1) * 512],
                                      start=(ci == 0), stop=(ci == 15))
                s_vmm.inc(ii)

        @ph.block.vector
        def _(v):
            for ut in range(16):
                sl = ut % 2
                for g4 in range(4):
                    G = ut * 4 + g4
                    v.wait_ge(s_wtp.h, G + 1)
                    if ut >= 2 and g4 == 0:
                        v.wait_ge(s_vmm.h, ut - 1)
                    ii = None
                    for j in range(4):
                        ci = g4 * 4 + j
                        ii = v.tensor_copy(
                            wtb[:, sl * 2048 + ci * 128: sl * 2048 + (ci + 1) * 128],
                            psbf[:, (G % 2) * 1024 + j * 128:
                                 (G % 2) * 1024 + (j + 1) * 128])
                    s_wev.inc(ii)
                v.wait_ge(s_vmm.h, ut + 1)
                s_vev.inc(v.tensor_copy(vat[:, ut * 2048:(ut + 1) * 2048], PS[:, 0:2048]))

        s_g1 = ph.sem("g1")
        s_g2 = ph.sem("g2")
        s_g3 = ph.sem("g3")
        s_g4 = ph.sem("g4")
        s_lgst = ph.sem("lgst")
        s_lgld = ph.sem("lgld")
        s_g5 = ph.sem("g5")
        s_g6 = ph.sem("g6")
        s_g7 = ph.sem("g7")
        s_g8 = ph.sem("g8")
        s_rex = ph.sem("rex")
        s_pat = ph.sem("pat")

        @ph.block.scalar
        def _(s):
            s.wait_ge(s_srl.h, 6 * 16)
            edge(s, sseq, s.activation(gsc[:, 0:16], sr[:, :], AF.Square,
                                       accum_out=s2[:, :]))
            s_g1.inc(s.activation(sqs[:, :], simb[:, :], AF.Square))

        @ph.block.tensor
        def _(t):
            t.wait_ge(s_g1.h, 1)
            t.matmul(PS[0:1, 2048:2049], ones_f[:, :], s2[:, :], start=True, stop=True)
            for ci in range(16):
                t.matmul(PS[0:1, 2049:2081], sr[:, ci:ci + 1],
                         simb[:, ci * 32:(ci + 1) * 32],
                         start=(ci == 0), stop=(ci == 15))
            ii = None
            for ci in range(16):
                ii = t.matmul(PS[0:1, 2081:2113], ones_f[:, :],
                              sqs[:, ci * 32:(ci + 1) * 32],
                              start=(ci == 0), stop=(ci == 15))
            s_g2.inc(ii)

        @ph.block.scalar
        def _(s):
            s.wait_ge(s_g2.h, 1)
            edge(s, sseq, s.copy(small2[0:1, 0:1], PS[0:1, 2048:2049]))
            edge(s, sseq, s.copy(smallf[0:1, 0:32], PS[0:1, 2049:2081]))
            edge(s, sseq, s.copy(smallf[0:1, 32:64], PS[0:1, 2081:2113]))
            edge(s, sseq, s.activation(small2[0:1, 1:2], small2[0:1, 0:1], AF.Ln))
            edge(s, sseq, s.activation(small2[0:1, 1:2], small2[0:1, 1:2], AF.Exp, scale=-0.5))
            edge(s, sseq, s.activation(small2[0:1, 2:34], smallf[0:1, 32:64], AF.Ln))
            edge(s, sseq, s.activation(small2[0:1, 2:34], small2[0:1, 2:34], AF.Exp, scale=-0.5))
            s_g3.inc(s.activation(sgt[0:1, :], gtsb[0:1, :], AF.Sigmoid))

        @ph.block.vector
        def _(v):
            v.wait_ge(s_g3.h, 1)
            edge(v, vseq, v.tensor_scalar(lgrow[0:1, :], smallf[0:1, 0:32],
                                          small2[0:1, 1:2], None, ALU.mult))
            edge(v, vseq, v.tensor_tensor(lgrow[0:1, :], lgrow[0:1, :],
                                          small2[0:1, 2:34], ALU.mult))
            edge(v, vseq, v.tensor_tensor(lgrow[0:1, :], lgrow[0:1, :],
                                          sgt[0:1, :], ALU.subtract))
            v.sem_inc(s_g4.h, 1)
            s_g4.n += 1

        @ph.block.sync
        def _(sync):
            sync.wait_ge(s_g4.h, 1)
            s_lgst.inc(sync.dma_start(out=io["lgd"][:, :], in_=lgrow[0:1, :]), 16)
            sync.wait_ge(s_lgst.h, 16)
            s_lgld.inc(sync.dma_start(out=lgc[:, :],
                                      in_=io["lgd"].rearrange("one e -> e one")), 16)

        @ph.block.tensor
        def _(t):
            t.wait_ge(s_g4.h, 1)
            s_g5.inc(t.matmul(PS[0:32, 2048:2080], onesr_f[0:1, 0:32],
                              lgrow[0:1, :], start=True, stop=True))

        @ph.block.scalar
        def _(s):
            s.wait_ge(s_g5.h, 1)
            s_g6.inc(s.copy(lrow[:, :], PS[0:32, 2048:2080]))

        @ph.block.vector
        def _(v):
            v.wait_ge(s_g6.h, 1)
            v.wait_ge(s_lgld.h, 16)
            edge(v, vseq, v.tensor_scalar(m1[:, :], lrow[:, :], lgc[:, :], None, ALU.is_gt))
            edge(v, vseq, v.tensor_scalar(e2[:, :], lrow[:, :], lgc[:, :], None, ALU.is_equal))
            edge(v, vseq, v.tensor_tensor(e2[:, :], e2[:, :], ltm[:, :], ALU.mult))
            edge(v, vseq, v.tensor_tensor(m1[:, :], m1[:, :], e2[:, :], ALU.add))
            edge(v, vseq, v.reduce_sum(rankc[:, :], m1[:, :], mybir.AxisListType.X))
            edge(v, vseq, v.tensor_scalar(fbc[:, :], rankc[:, :], 16.0, None, ALU.is_lt))
            edge(v, vseq, v.tensor_scalar(gatedc[:, :], lgc[:, :], 0.0, None, ALU.max))
            edge(v, vseq, v.tensor_scalar(maskc[:, :], lgc[:, :], 0.0, None, ALU.is_gt))
            v.sem_inc(s_g7.h, 1)
            s_g7.n += 1

        @ph.block.tensor
        def _(t):
            t.wait_ge(s_g7.h, 1)
            s_g8.inc(t.matmul(PS[0:1, 2080:2081], ones_f[0:32, :], maskc[:, :],
                              start=True, stop=True))

        @ph.block.scalar
        def _(s):
            s.wait_ge(s_g8.h, 1)
            edge(s, sseq, s.copy(small2[0:1, 34:35], PS[0:1, 2080:2081]))
            s.sem_inc(s_g8.h, 1)
            s_g8.n += 1

        @ph.block.vector
        def _(v):
            v.wait_ge(s_g8.h, 2)
            edge(v, vseq, v.tensor_scalar(small2[0:1, 35:36], small2[0:1, 34:35],
                                          0.0, None, ALU.is_equal))
            v.sem_inc(s_g8.h, 1)
            s_g8.n += 1

        @ph.block.tensor
        def _(t):
            t.wait_ge(s_g8.h, 3)
            s_g8.inc(t.matmul(PS[0:32, 2081:2082], onesr_f[0:1, 0:32],
                              small2[0:1, 35:36], start=True, stop=True))

        @ph.block.scalar
        def _(s):
            s.wait_ge(s_g8.h, 4)
            s_g8.inc(s.copy(infb[:, :], PS[0:32, 2081:2082]))

        @ph.block.vector
        def _(v):
            v.wait_ge(s_g8.h, 5)
            edge(v, vseq, v.tensor_tensor(zc[:, :], infb[:, :], fbc[:, :], ALU.mult))
            edge(v, vseq, v.tensor_tensor(zc[:, :], maskc[:, :], zc[:, :], ALU.add))
            edge(v, vseq, v.tensor_scalar(zc[:, :], zc[:, :], 1.0, None, ALU.min))
            edge(v, vseq, v.tensor_tensor(ezc[:, :], gatedc[:, :], zc[:, :], ALU.mult))
            edge(v, vseq, v.tensor_scalar(zc[:, :], zc[:, :], 3.4e38, -3.4e38,
                                          ALU.mult, ALU.add))
            edge(v, vseq, v.tensor_tensor(zc[:, :], ezc[:, :], zc[:, :], ALU.add))
            v.sem_inc(s_g8.h, 1)
            s_g8.n += 1

        @ph.block.scalar
        def _(s):
            s.wait_ge(s_g8.h, 6)
            s_g8.inc(s.activation(ezc[:, :], zc[:, :], AF.Exp))

        @ph.block.tensor
        def _(t):
            t.wait_ge(s_g8.h, 7)
            s_g8.inc(t.matmul(PS[0:1, 2082:2083], ones_f[0:32, :], ezc[:, :],
                              start=True, stop=True))

        @ph.block.scalar
        def _(s):
            s.wait_ge(s_g8.h, 8)
            s_g8.inc(s.copy(small2[0:1, 36:37], PS[0:1, 2082:2083]))

        @ph.block.vector
        def _(v):
            v.wait_ge(s_g8.h, 9)
            edge(v, vseq, v.reciprocal(small2[0:1, 37:38], small2[0:1, 36:37]))
            v.sem_inc(s_g8.h, 1)
            s_g8.n += 1

        @ph.block.tensor
        def _(t):
            t.wait_ge(s_g8.h, 10)
            s_g8.inc(t.matmul(PS[0:32, 2083:2084], onesr_f[0:1, 0:32],
                              small2[0:1, 37:38], start=True, stop=True))

        @ph.block.scalar
        def _(s):
            s.wait_ge(s_g8.h, 11)
            s_g8.inc(s.copy(rzb[:, :], PS[0:32, 2083:2084]))

        @ph.block.vector
        def _(v):
            v.wait_ge(s_g8.h, 12)
            edge(v, vseq, v.tensor_tensor(routc[:, :], ezc[:, :], rzb[:, :], ALU.mult))
            v.sem_inc(s_g8.h, 1)
            s_g8.n += 1

        @ph.block.tensor
        def _(t):
            t.wait_ge(s_g8.h, 13)
            ii = None
            for ci in range(16):
                ii = t.matmul(PS[:, 2084 + ci:2084 + ci + 1], selb[:, ci * 128:(ci + 1) * 128],
                              routc[:, :], start=True, stop=True)
            s_rex.inc(ii)

        @ph.block.scalar
        def _(s):
            s.wait_ge(s_rex.h, 1)
            s_rex.inc(s.copy(rexp[:, :], PS[:, 2084:2100]))

        @ph.block.vector
        def _(v):
            v.wait_ge(s_rex.h, 2)
            ii = None
            for ci in range(16):
                ii = v.tensor_scalar(pat[:, ci * 64:(ci + 1) * 64], ir2[:, :],
                                     rexp[:, ci:ci + 1], None, ALU.mult)
            s_pat.inc(ii)

        st0.close()
        owb = ph.sbuf("owbD", [128, 2 * 2048], BF16)
        wod = ph.sbuf("wod", [64, 2048], BF16)
        comb = ph.sbuf("comb", [64, 2048], BF16)
        fbuf = ph.sbuf("fbufD", [128, 2 * 2048], F32)
        lnf = ph.sbuf("lnfD", [128, 2048], F32)
        xo = ph.sbuf("xoD", [128, 2 * 2048], F32)
        obuf = ph.sbuf("obufD", [128, 2 * 2048], F32)
        fb6 = ph.sbuf("fb6D", [128, 24], F32)
        fmv = ph.sbuf("fmvD", [128, 2], F32)
        frs = ph.sbuf("frsD", [128, 1], F32)
        fnm = ph.sbuf("fnmD", [128, 1], F32)

        s_owl = [ph.sem("owl0"), ph.sem("owl1")]
        s_wodm = ph.sem("wodm")
        s_wode = ph.sem("wode")
        s_cbm = ph.sem("cbm")
        s_cbe = ph.sem("cbe")
        s_fmm = ph.sem("fmmD")
        s_fev = ph.sem("fevD")
        s_fbn = ph.sem("fbnD")
        s_fmv = ph.sem("fmvsD")
        s_frs = ph.sem("frssD")
        s_fnm = ph.sem("fnmsD")
        s_fap = ph.sem("fapD")
        s_res = ph.sem("resD")
        s_ost = [ph.sem("ostD0"), ph.sem("ostD1")]
        s_xoD = [ph.sem("xoD0"), ph.sem("xoD1")]

        @ph.block.gpsimd
        def _(g):
            g.wait_ge(s_vmm.h, 16)
            for ci in range(16):
                sl = ci % 2
                if ci >= 2:
                    g.wait_ge(s_wodm.h, ci - 1)
                s_owl[sl].inc(g.dma_start(
                    out=owb[:, sl * 2048:(sl + 1) * 2048],
                    in_=io["ow"][ci * 128:(ci + 1) * 128, :]), 16)

        @ph.block.tensor
        def _(t):
            t.wait_ge(s_pat.h, 1)
            t.wait_ge(s_vev.h, 16)
            for ci in range(16):
                sl = ci % 2
                t.wait_ge(s_owl[sl].h, (ci // 2 + 1) * 16)
                ii = None
                for mo in range(4):
                    ii = t.matmul(PS[0:64, mo * 512:(mo + 1) * 512],
                                  pat[:, ci * 64:(ci + 1) * 64],
                                  owb[:, sl * 2048 + mo * 512: sl * 2048 + (mo + 1) * 512],
                                  start=(ci == 0), stop=(ci == 15))
                s_wodm.inc(ii)
            t.wait_ge(s_vev.h, 16)
            t.wait_ge(s_wode.h, 1)
            ii = None
            for ci in range(16):
                for mo in range(4):
                    ii = t.matmul(PS[0:64, mo * 512:(mo + 1) * 512],
                                  pat[:, ci * 64:(ci + 1) * 64],
                                  vat[:, ci * 2048 + mo * 512: ci * 2048 + (mo + 1) * 512],
                                  start=(ci == 0), stop=(ci == 15))
            s_cbm.inc(ii)
            t.wait_ge(s_cbe.h, 1)
            for tt in range(16):
                if tt >= 1:
                    t.wait_ge(s_fev.h, tt)
                ii = None
                for dc in range(4):
                    ii = t.matmul(PS[:, dc * 512:(dc + 1) * 512],
                                  comb[0:64, tt * 128:(tt + 1) * 128],
                                  wod[0:64, dc * 512:(dc + 1) * 512],
                                  start=True, stop=True)
                s_fmm.inc(ii)

        @ph.block.vector
        def _(v):
            v.wait_ge(s_wodm.h, 16)
            s_wode.inc(v.tensor_copy(wod[:, :], PS[0:64, 0:2048]))
            v.wait_ge(s_cbm.h, 1)
            s_cbe.inc(v.tensor_copy(comb[:, :], PS[0:64, 0:2048]))
            for tt in range(16):
                slt = tt % 2
                v.wait_ge(s_fev.h, tt + 1)
                ii = None
                for j in range(4):
                    ii = v.bn_stats(fb6[:, j * 6:(j + 1) * 6],
                                    fbuf[:, slt * 2048 + j * 512: slt * 2048 + (j + 1) * 512])
                s_fbn.inc(ii)
                v.wait_ge(s_fbn.h, tt + 1)
                s_fmv.inc(v.bn_aggr(fmv[:, :], fb6[:, :]))
                v.wait_ge(s_fmv.h, tt + 1)
                v.wait_ge(s_frs.h, tt + 1)
                s_fnm.inc(v.tensor_scalar(fnm[:, :], fmv[:, 0:1], frs[:, :], -1.0,
                                          ALU.mult, ALU.mult))
                v.wait_ge(s_fap.h, tt + 1)
                v.wait_ge(s_xoD[slt].h, (tt // 2 + 1) * 16)
                if tt >= 2:
                    v.wait_ge(s_ost[slt].h, (tt // 2) * 16)
                s_res.inc(v.tensor_tensor(obuf[:, slt * 2048:(slt + 1) * 2048],
                                          lnf[:, :], xo[:, slt * 2048:(slt + 1) * 2048],
                                          ALU.add))

        @ph.block.scalar
        def _(s):
            for tt in range(16):
                slt = tt % 2
                s.wait_ge(s_fmm.h, tt + 1)
                if tt >= 2:
                    s.wait_ge(s_fbn.h, tt - 1)
                    s.wait_ge(s_fap.h, tt - 1)
                s_fev.inc(s.copy(fbuf[:, slt * 2048:(slt + 1) * 2048], PS[:, 0:2048]))
                s.wait_ge(s_fmv.h, tt + 1)
                edge(s, sseq, s.activation(frs[:, :], fmv[:, 1:2], AF.Ln, bias=eps[:, :]))
                s_frs.inc(s.activation(frs[:, :], frs[:, :], AF.Exp, scale=-0.5))
                s.wait_ge(s_frs.h, tt + 1)
                s.wait_ge(s_fnm.h, tt + 1)
                if tt >= 1:
                    s.wait_ge(s_res.h, tt)
                s_fap.inc(s.activation(lnf[:, :], fbuf[:, slt * 2048:(slt + 1) * 2048],
                                       AF.Identity, scale=frs[:, :], bias=fnm[:, :]))

        @ph.block.sync
        def _(sync):
            sync.wait_ge(s_vev.h, 16)
            for tt in range(16):
                slt = tt % 2
                if tt >= 2:
                    sync.wait_ge(s_res.h, tt - 1)
                s_xoD[slt].inc(sync.dma_start(
                    out=xo[:, slt * 2048:(slt + 1) * 2048],
                    in_=io["x3"][tt * 128:(tt + 1) * 128, :]), 16)
                sync.wait_ge(s_res.h, tt + 1)
                s_ost[slt].inc(sync.dma_start(
                    out=io["out"][tt * 128:(tt + 1) * 128, :],
                    in_=obuf[:, slt * 2048:(slt + 1) * 2048]), 16)
            sync.wait_ge(s_ost[0].h, 8 * 16)
            sync.wait_ge(s_ost[1].h, 8 * 16)
    return nc



def emit_sem_clear_block(nc):
    with ExitStack() as es:
        handles = []
        for i in range(128):
            try:
                handles.append(es.enter_context(nc.semaphore(f"clr{i}")))
            except Exception:
                break
        with nc.Block() as blk:
            @blk.gpsimd
            def _(g):
                g.dma_reset()
                for h in handles:
                    if h.num in USED_SEM_NUMS:
                        g.sem_clear(h)
    USED_SEM_NUMS.clear()


def build_mega(use_cc=True):
    nc = bass.Bass("TRN2", target_bir_lowering=False)
    F, B = F32, BF16
    d = {}

    def inp(name, shape, dt):
        d[name] = nc.dram_tensor(name, shape, dt, kind="ExternalInput").ap()

    def scr(name, shape, dt):
        t = nc.dram_tensor(name, shape, dt)
        d[name] = t.ap()
        d[name + "_t"] = t

    inp("xin", [3072, 2048], F)
    inp("l0wq", [2048, 2048], F); inp("l0wk", [2048, 512], F)
    inp("l0wv", [2048, 512], F); inp("l0wo", [2048, 2048], F)
    inp("cost", [128, 3072], F); inp("sint", [128, 3072], F)
    inp("l0mask", [4, 12, 128, 512], B)
    inp("ident", [128, 128], B)
    inp("w1", [5632, 2048], F); inp("w2", [5632, 2048], F); inp("w3", [2048, 5632], F)
    for n in ["q1w", "k1w", "q2w", "k2w", "cvw", "cow"]:
        inp(n, [2048, 2048], F)
    inp("lam_b", [128, 16], F)
    inp("dwv", [2048, 2048], F); inp("dow", [2048, 2048], F)
    inp("sim", [2048, 32], F); inp("gates", [1, 32], F)
    inp("selm", [32, 2048], F); inp("ir2", [128, 64], B); inp("ltm", [32, 32], F)
    d["out"] = nc.dram_tensor("out", [2048, 2048], F, kind="ExternalOutput").ap()

    scr("x1", [2048, 2048], F); scr("x2", [2048, 2048], F); scr("x3", [2048, 2048], F)
    scr("qsc", [2048, 2048], B); scr("vsc", [3072, 512], B)
    scr("at3", [16, 16, 128, 128], B)
    scr("h3", [44, 16, 128, 128], B); scr("fsc", [2048, 2048], F)
    for n in ["kvc1", "kvc2", "kvcv", "qsc1", "qsc2"]:
        scr(n, [2048, 2048], B)
    for n in ["kvg1", "kvg2", "kvgv"]:
        scr(n, [4096, 2048], B)
    scr("at3c", [16, 16, 128, 128], B)
    scr("srp", [128, 16], F); scr("srg", [128, 16], F); scr("lgd", [1, 32], F)

    ioA = {"xin": d["xin"], "wq": d["l0wq"], "wk": d["l0wk"], "wv": d["l0wv"],
           "wo": d["l0wo"], "cost": d["cost"], "sint": d["sint"],
           "l0mask": d["l0mask"], "ident": d["ident"], "x1": d["x1"],
           "qsc": d["qsc"], "vsc": d["vsc"], "at3": d["at3"]}
    build_phase_a(nc, ioA)
    emit_sem_clear_block(nc)

    ioB = {"x1": d["x1"], "w1": d["w1"], "w2": d["w2"], "w3": d["w3"],
           "ident": d["ident"], "x2": d["x2"], "h3": d["h3"], "fsc": d["fsc"]}
    build_phase_b(nc, ioB)
    emit_sem_clear_block(nc)

    ioC = {"x2": d["x2"], "q1w": d["q1w"], "k1w": d["k1w"], "q2w": d["q2w"],
           "k2w": d["k2w"], "vw": d["cvw"], "ow": d["cow"], "lam_b": d["lam_b"],
           "ident": d["ident"], "x3": d["x3"], "at3c": d["at3c"]}
    for n in ["kvc1", "kvc2", "kvcv", "kvg1", "kvg2", "kvgv", "qsc1", "qsc2"]:
        ioC[n] = d[n]
        ioC[n + "_t"] = d[n + "_t"]
    build_phase_c_pre(nc, ioC)
    emit_sem_clear_block(nc)
    build_phase_c_post(nc, ioC)
    emit_sem_clear_block(nc)

    ioD = {"x3": d["x3"], "wv": d["dwv"], "ow": d["dow"], "sim": d["sim"],
           "gates": d["gates"], "selm": d["selm"], "ir2": d["ir2"],
           "ltm": d["ltm"], "ident": d["ident"], "out": d["out"],
           "srp": d["srp"], "srp_t": d["srp_t"], "srg": d["srg"],
           "srg_t": d["srg_t"], "lgd": d["lgd"]}
    build_phase_d(nc, ioD, use_cc=use_cc)
    return nc



_CONSTS = None


def host_consts():
    global _CONSTS
    import ml_dtypes
    if _CONSTS is None:
        selm = np.zeros((32, 2048), dtype=np.float32)
        for ci in range(16):
            for p in range(128):
                selm[2 * ci + p // 64, ci * 128 + p] = 1.0
        ir2 = np.concatenate([np.eye(64), np.eye(64)], axis=0).astype(ml_dtypes.bfloat16)
        ltm = (np.arange(32)[None, :] < np.arange(32)[:, None]).astype(np.float32)
        ident = np.eye(128, dtype=ml_dtypes.bfloat16)
        _CONSTS = (selm, ir2, ltm, ident)
    return _CONSTS


def host_core_inputs(inputs, core):
    selm, ir2, ltm, ident = host_consts()
    b, half = core // 2, core % 2
    x = np.asarray(inputs["L0_attn_hidden_states"])
    xin = np.zeros((3072, 2048), dtype=np.float32)
    if half == 0:
        xin[1024:] = x[b, 0:2048]
        pos = np.arange(-1024, 2048)
    else:
        xin[:] = x[b, 1024:4096]
        pos = np.arange(1024, 4096)
    cost, sint = rope_tables(np.maximum(pos, 0))
    f32 = np.float32
    return {
        "xin": xin,
        "l0wq": np.ascontiguousarray(inputs["L0_attn_w_q"], dtype=f32),
        "l0wk": np.ascontiguousarray(inputs["L0_attn_w_k"], dtype=f32),
        "l0wv": np.ascontiguousarray(inputs["L0_attn_w_v"], dtype=f32),
        "l0wo": np.ascontiguousarray(inputs["L0_attn_w_o"], dtype=f32),
        "cost": cost, "sint": sint,
        "l0mask": np.asarray(l0_masks(half)),
        "ident": ident,
        "w1": np.ascontiguousarray(inputs["L0_ffn_w1"], dtype=f32),
        "w2": np.ascontiguousarray(inputs["L0_ffn_w2"], dtype=f32),
        "w3": np.ascontiguousarray(inputs["L0_ffn_w3"], dtype=f32),
        "q1w": np.ascontiguousarray(inputs["L1_attn_q1_w"], dtype=f32),
        "k1w": np.ascontiguousarray(inputs["L1_attn_k1_w"], dtype=f32),
        "q2w": np.ascontiguousarray(inputs["L1_attn_q2_w"], dtype=f32),
        "k2w": np.ascontiguousarray(inputs["L1_attn_k2_w"], dtype=f32),
        "cvw": np.ascontiguousarray(inputs["L1_attn_v_w"], dtype=f32),
        "cow": np.ascontiguousarray(inputs["L1_attn_o_w"], dtype=f32),
        "lam_b": np.ascontiguousarray(np.broadcast_to(
            np.asarray(inputs["L1_attn_lambda_param"], dtype=f32)[None, :], (128, 16))),
        "dwv": np.ascontiguousarray(
            np.asarray(inputs["L1_ffn_w_v"], dtype=f32).reshape(2048, 2048)),
        "dow": np.ascontiguousarray(
            np.asarray(inputs["L1_ffn_o_weights"], dtype=f32).reshape(2048, 2048)),
        "sim": np.ascontiguousarray(inputs["L1_ffn_sim_matrix"], dtype=f32),
        "gates": np.ascontiguousarray(
            np.asarray(inputs["L1_ffn_gates"], dtype=f32).reshape(1, 32)),
        "selm": selm, "ir2": ir2, "ltm": ltm,
    }


def kernel(**inputs):
    from concourse.bass_utils import run_bass_kernel_spmd
    assert inputs["L1_ffn_w_q"].shape == (32, 64, 2048)
    assert inputs["L1_ffn_w_k"].shape == (32, 64, 2048)
    nc = build_mega(use_cc=True)
    in_maps = [host_core_inputs(inputs, c) for c in range(8)]
    res = run_bass_kernel_spmd(nc, in_maps, core_ids=list(range(8)))
    out = np.zeros((4, 4096, 2048), dtype=np.float32)
    for c in range(8):
        b, half = c // 2, c % 2
        out[b, half * 2048:(half + 1) * 2048] = res.results[c]["out"]
    return out


# version 09 (final)
# speedup vs baseline: 1.0367x; 1.0186x over previous
import math
from contextlib import ExitStack

import numpy as np
import concourse.bass as bass
import concourse.mybir as mybir

F32 = mybir.dt.float32
BF16 = mybir.dt.bfloat16
F16 = mybir.dt.float16
AF = mybir.ActivationFunctionType
ALU = mybir.AluOpType

D = 2048
TOWN = 2048
TCTX = 1024
TIN = TOWN + TCTX
HD = 128
INVSQ_HD = 1.0 / math.sqrt(128.0)
FFN_H = 5632


USED_SEM_NUMS = set()


class Sem:
    def __init__(self, stack, nc, name):
        self.h = stack.enter_context(nc.semaphore(name))
        self.n = 0

    def inc(self, instr, k=1):
        instr.then_inc(self.h, k)
        self.n += k
        return instr


def make_warm(ph, PS, ones_col, rhs_ap, lo):
    wsem = ph.sem("warm")

    def warm(t):
        pass
    return warm


def edge(eng, sem, instr):
    sem.inc(instr)
    eng.wait_ge(sem.h, sem.n)


class Phase:
    def __init__(self, nc, stack, block, tag):
        self.nc, self.stack, self.block, self.tag = nc, stack, block, tag

    def sem(self, name):
        s = Sem(self.stack, self.nc, f"{self.tag}_{name}")
        USED_SEM_NUMS.add(s.h.num)
        return s

    def sbuf(self, name, shape, dtype, stack=None):
        return (stack or self.stack).enter_context(
            self.nc.sbuf_tensor(f"{self.tag}_{name}", shape, dtype))

    def psum(self, name, shape, dtype):
        return self.stack.enter_context(
            self.nc.psum_tensor(f"{self.tag}_{name}", shape, dtype))


def emit_ln_xhat(ph, *, src_hbm, n_tc, dst_tiles, ident, eps, psbf, name,
                 bufstack=None, skip_transpose=False, dst_group=None, warm=None):
    vseq = ph.sem(f"{name}_vq")
    sseq = ph.sem(f"{name}_sq")
    NS = 4
    xb = ph.sbuf(f"{name}_xb", [128, NS * D], F32, bufstack)
    xh = ph.sbuf(f"{name}_xh", [128, NS * D], BF16, bufstack)
    bn6 = ph.sbuf(f"{name}_bn6", [128, NS * 24], F32, bufstack)
    mv = ph.sbuf(f"{name}_mv", [128, NS * 2], F32, bufstack)
    rstd = ph.sbuf(f"{name}_rstd", [128, NS], F32, bufstack)
    nmr = ph.sbuf(f"{name}_nmr", [128, NS], F32, bufstack)

    s_ld = [ph.sem(f"{name}_ld{j}") for j in range(NS)]
    s_bn = ph.sem(f"{name}_bn")
    s_mv = ph.sem(f"{name}_mv")
    s_rs = ph.sem(f"{name}_rs")
    s_nm = ph.sem(f"{name}_nm")
    s_ap = ph.sem(f"{name}_ap")
    s_tp = ph.sem(f"{name}_tp")
    s_ev = ph.sem(f"{name}_ev")

    @ph.block.sync
    def _(sync):
        for tc in range(n_tc):
            sl = tc % NS
            if tc >= NS:
                sync.wait_ge(s_ap.h, tc - NS + 1)
            s_ld[sl].inc(sync.dma_start(
                out=xb[:, sl * D:(sl + 1) * D],
                in_=src_hbm[tc * 128:(tc + 1) * 128, :]), 16)

    @ph.block.vector
    def _(v):
        def emit_evict_group(G):
            tc, g = G // 4, G % 4
            v.wait_ge(s_tp.h, G + 1)
            if dst_group is not None:
                s_ev.inc(v.tensor_copy(
                    dst_group(g, tc),
                    psbf[:, (G % 2) * 1024:(G % 2) * 1024 + 512]
                    .rearrange("p (j t) -> p j t", t=128)))
                return
            ii = None
            for j in range(4):
                ci = g * 4 + j
                ii = v.tensor_copy(dst_tiles(ci, tc),
                                   psbf[:, (G % 2) * 1024 + j * 128:
                                        (G % 2) * 1024 + (j + 1) * 128])
            s_ev.inc(ii)

        for tc in range(n_tc):
            sl = tc % NS
            if tc >= 1:
                for g in range(4):
                    emit_evict_group((tc - 1) * 4 + g)
            v.wait_ge(s_ld[sl].h, (tc // NS + 1) * 16)
            ii = None
            for j in range(4):
                ii = v.bn_stats(bn6[:, sl * 24 + j * 6: sl * 24 + (j + 1) * 6],
                                xb[:, sl * D + j * 512: sl * D + (j + 1) * 512])
            s_bn.inc(ii)
            v.wait_ge(s_bn.h, tc + 1)
            s_mv.inc(v.bn_aggr(mv[:, sl * 2:(sl + 1) * 2],
                               bn6[:, sl * 24:(sl + 1) * 24]))
            v.wait_ge(s_rs.h, tc + 1)
            v.wait_ge(s_mv.h, tc + 1)
            s_nm.inc(v.tensor_scalar(nmr[:, sl:sl + 1], mv[:, sl * 2:sl * 2 + 1],
                                     rstd[:, sl:sl + 1], -1.0, ALU.mult, ALU.mult))
        for g in range(4):
            emit_evict_group((n_tc - 1) * 4 + g)

    @ph.block.scalar
    def _(s):
        for tc in range(n_tc):
            sl = tc % NS
            s.wait_ge(s_mv.h, tc + 1)
            edge(s, sseq, s.activation(rstd[:, sl:sl + 1], mv[:, sl * 2 + 1:sl * 2 + 2],
                                       AF.Ln, bias=eps[:, :]))
            s_rs.inc(s.activation(rstd[:, sl:sl + 1], rstd[:, sl:sl + 1],
                                  AF.Exp, scale=-0.5))
            s.wait_ge(s_rs.h, tc + 1)
            s.wait_ge(s_nm.h, tc + 1)
            if tc >= NS:
                s.wait_ge(s_tp.h, 4 * (tc - NS + 1))
            s_ap.inc(s.activation(xh[:, sl * D:(sl + 1) * D],
                                  xb[:, sl * D:(sl + 1) * D],
                                  AF.Identity, scale=rstd[:, sl:sl + 1],
                                  bias=nmr[:, sl:sl + 1]))

    @ph.block.tensor
    def _(t):
        for tc in range(n_tc):
            sl = tc % NS
            t.wait_ge(s_ap.h, tc + 1)
            for g in range(4):
                G = tc * 4 + g
                if G >= 2:
                    t.wait_ge(s_ev.h, G - 1)
                if warm is not None:
                    warm(t)
                ii = None
                for j in range(4):
                    ci = g * 4 + j
                    ii = t.transpose(psbf[:, (G % 2) * 1024 + j * 128:
                                          (G % 2) * 1024 + (j + 1) * 128],
                                     xh[:, sl * D + ci * 128:sl * D + (ci + 1) * 128],
                                     ident[:, :])
                s_tp.inc(ii)

    return {"ev": s_ev, "ap": s_ap}


def emit_rope(v, pk, cosap, sinap, scr, out_bf, W, vseq):
    sa = scr[:, 0:W]
    sb = scr[:, W:2 * W]
    vseq.inc(v.tensor_tensor(sa[0:64, :], pk[64:128, :], sinap[0:64, :], ALU.mult))
    vseq.inc(v.tensor_tensor(sa[64:128, :], pk[0:64, :], sinap[64:128, :], ALU.mult))
    vseq.inc(v.tensor_tensor(sb[0:64, :], pk[0:64, :], cosap[0:64, :], ALU.mult))
    vseq.inc(v.tensor_tensor(sb[64:128, :], pk[64:128, :], cosap[64:128, :], ALU.mult))
    v.wait_ge(vseq.h, vseq.n)
    vseq.inc(v.tensor_tensor(out_bf[0:64, :], sb[0:64, :], sa[0:64, :], ALU.subtract))
    vseq.inc(v.tensor_tensor(out_bf[64:128, :], sb[64:128, :], sa[64:128, :], ALU.add))
    v.wait_ge(vseq.h, vseq.n)


def build_phase_a(nc, io, tap=None, stages=4):
    with ExitStack() as stack:
        block = stack.enter_context(nc.Block())
        ph = Phase(nc, stack, block, "A")

        ident = ph.sbuf("ident", [128, 128], BF16)
        eps = ph.sbuf("eps", [128, 1], F32)
        ones_bf = ph.sbuf("onesb", [128, 1], BF16)
        ones_row = ph.sbuf("onesr", [1, 128], BF16)
        psbf = ph.psum("psbf", [128, 2048], BF16)
        PS = ph.psum("ps", [128, 3072], F32)
        st12 = ExitStack()
        st01 = ExitStack()
        stLN = ExitStack()
        kbar = ph.sbuf("kbar", [128, 4 * TIN], BF16, st12)
        xhat_ctx = ph.sbuf("xhc", [128, 16 * 1024], BF16, st01)
        xhat_own = ph.sbuf("xho", [128, 16 * 2048], BF16, st01)

        s_pre = ph.sem("pre")
        vseq = ph.sem("vseq")
        sseq = ph.sem("sseq")

        @ph.block.sync
        def _(sync):
            s_pre.inc(sync.dma_start(out=ident[:, :], in_=io["ident"][:, :]), 16)

        @ph.block.vector
        def _(v):
            v.memset(eps[:, :], 1e-5)
            v.memset(ones_bf[:, :], 1.0)
            v.memset(ones_row[:, :], 1.0)
            v.wait_ge(s_pre.h, 16)

        def dst(ci, tc):
            if tc < 8:
                return xhat_ctx[:, ci * 1024 + tc * 128: ci * 1024 + (tc + 1) * 128]
            u = tc - 8
            return xhat_own[:, ci * 2048 + u * 128: ci * 2048 + (u + 1) * 128]

        def dstg(g, tc):
            if tc < 8:
                return xhat_ctx[:, :].rearrange("p (ci t) -> p ci t", t=1024)[
                    :, g * 4:(g + 1) * 4, tc * 128:(tc + 1) * 128]
            u = tc - 8
            return xhat_own[:, :].rearrange("p (ci t) -> p ci t", t=2048)[
                :, g * 4:(g + 1) * 4, u * 128:(u + 1) * 128]

        warmA = make_warm(ph, PS, ones_bf, ident[:, 0:64], 2944)
        lns = emit_ln_xhat(ph, src_hbm=io["xin"], n_tc=24, dst_tiles=dst,
                           ident=ident, eps=eps, psbf=psbf, name="ln0",
                           bufstack=stLN, dst_group=dstg, warm=warmA)
        s_ev = lns["ev"]
        EV_ALL = s_ev.n
        stLN.close()

        if stages < 1:
            st01.close(); st12.close()
            return nc
        wcol = ph.sbuf("wcol", [128, 2 * 2048], BF16, st01)
        wvbf = ph.sbuf("wvbf", [128, 16 * 512], BF16, st01)
        cfull = ph.sbuf("cfull", [128, TIN], F32, st01)
        sfull = ph.sbuf("sfull", [128, TIN], F32, st01)
        rsc = ph.sbuf("rsc", [128, 2 * 1024], F32, st01)
        vev = ph.sbuf("vev", [128, 2 * 512], BF16, st01)
        qev = ph.sbuf("qev", [128, 2 * 2048], BF16, st01)

        s_wld = [ph.sem("wld0"), ph.sem("wld1")]
        s_wv = ph.sem("wv")
        s_qld = [ph.sem("qld0"), ph.sem("qld1")]
        s_cs = ph.sem("cs")
        s_kmm = ph.sem("kmm")
        s_krp = ph.sem("krp")
        s_vmm = ph.sem("vmm")
        s_vev = ph.sem("vev")
        s_vst = [ph.sem("vst0"), ph.sem("vst1")]
        s_qmm = ph.sem("qmm")
        s_qrp = ph.sem("qrp")
        s_qst = [ph.sem("qst0"), ph.sem("qst1")]

        def xcol512(ci, t0):
            if t0 < 1024:
                return xhat_ctx[:, ci * 1024 + t0: ci * 1024 + t0 + 512]
            u = t0 - 1024
            return xhat_own[:, ci * 2048 + u: ci * 2048 + u + 512]

        @ph.block.gpsimd
        def _(g):
            g.wait_ge(s_ev.h, EV_ALL)
            for ut in range(4):
                sl = ut % 2
                if ut >= 2:
                    g.wait_ge(s_kmm.h, (ut - 2) * 3 + 3)
                s_wld[sl].inc(g.dma_start(
                    out=wcol[:, sl * 2048: sl * 2048 + 2048]
                    .rearrange("p (ci u) -> p ci u", u=128),
                    in_=io["wk"].rearrange("(ci p) u -> p ci u", p=128)
                    [:, :, ut * 128:(ut + 1) * 128]), 16)
            s_wv.inc(g.dma_start(
                out=wvbf[:, :].rearrange("p (ci u) -> p ci u", u=512),
                in_=io["wv"].rearrange("(ci p) u -> p ci u", p=128)), 16)
            for ut in range(16):
                sl = ut % 2
                if ut < 2:
                    g.wait_ge(s_kmm.h, 12)
                else:
                    g.wait_ge(s_qmm.h, (ut - 2) * 2 + 2)
                s_qld[sl].inc(g.dma_start(
                    out=wcol[:, sl * 2048: sl * 2048 + 2048]
                    .rearrange("p (ci u) -> p ci u", u=128),
                    in_=io["wq"].rearrange("(ci p) u -> p ci u", p=128)
                    [:, :, ut * 128:(ut + 1) * 128]), 16)

        @ph.block.sync
        def _(sync):
            sync.wait_ge(s_ev.h, EV_ALL)
            s_cs.inc(sync.dma_start(out=cfull[:, :], in_=io["cost"][:, :]), 16)
            s_cs.inc(sync.dma_start(out=sfull[:, :], in_=io["sint"][:, :]), 16)
            for tt in range(24):
                sync.wait_ge(s_vev.h, tt + 1)
                s_vst[tt % 2].inc(sync.dma_start(
                    out=io["vsc"][tt * 128:(tt + 1) * 128, :],
                    in_=vev[:, (tt % 2) * 512:(tt % 2 + 1) * 512]), 16)
            for ut in range(16):
                sync.wait_ge(s_qrp.h, 2 * (ut + 1))
                s_qst[ut % 2].inc(sync.dma_start(
                    out=io["qsc"][ut * 128:(ut + 1) * 128, :],
                    in_=qev[:, (ut % 2) * 2048:(ut % 2 + 1) * 2048]), 16)

        @ph.block.tensor
        def _(t):
            t.wait_ge(s_ev.h, EV_ALL)
            for ut in range(4):
                sl = ut % 2
                t.wait_ge(s_wld[ut % 2].h, (ut // 2 + 1) * 16)
                for th in range(3):
                    G = ut * 3 + th
                    if G >= 2:
                        t.wait_ge(s_krp.h, G - 1)
                    ii = None
                    for ci in range(16):
                        for mo in range(2):
                            t0 = th * 1024 + mo * 512
                            ii = t.matmul(PS[:, (G % 2) * 1024 + mo * 512:
                                             (G % 2) * 1024 + (mo + 1) * 512],
                                          wcol[:, sl * 2048 + ci * 128:
                                               sl * 2048 + (ci + 1) * 128],
                                          xcol512(ci, t0),
                                          start=(ci == 0), stop=(ci == 15))
                    s_kmm.inc(ii)
            t.wait_ge(s_wv.h, 16)
            for tt in range(24):
                if tt >= 2:
                    t.wait_ge(s_vev.h, tt - 1)
                ii = None
                for ci in range(16):
                    ii = t.matmul(PS[:, 2048 + (tt % 2) * 512: 2048 + (tt % 2 + 1) * 512],
                                  dst(ci, tt), wvbf[:, ci * 512:(ci + 1) * 512],
                                  start=(ci == 0), stop=(ci == 15))
                s_vmm.inc(ii)
            t.wait_ge(s_krp.h, 12)
            for ut in range(16):
                sl = ut % 2
                t.wait_ge(s_qld[ut % 2].h, (ut // 2 + 1) * 16)
                for th in range(2):
                    G = ut * 2 + th
                    if G >= 2:
                        t.wait_ge(s_qrp.h, G - 1)
                    ii = None
                    for ci in range(16):
                        for mo in range(2):
                            u0 = th * 1024 + mo * 512
                            ii = t.matmul(PS[:, (G % 2) * 1024 + mo * 512:
                                             (G % 2) * 1024 + (mo + 1) * 512],
                                          wcol[:, sl * 2048 + ci * 128:
                                               sl * 2048 + (ci + 1) * 128],
                                          xhat_own[:, ci * 2048 + u0: ci * 2048 + u0 + 512],
                                          start=(ci == 0), stop=(ci == 15))
                    s_qmm.inc(ii)

        @ph.block.vector
        def _(v):
            v.wait_ge(s_cs.h, 32)
            for ut in range(4):
                for th in range(3):
                    G = ut * 3 + th
                    v.wait_ge(s_kmm.h, G + 1)
                    emit_rope(
                        v, PS[:, (G % 2) * 1024:(G % 2) * 1024 + 1024],
                        cfull[:, th * 1024:(th + 1) * 1024],
                        sfull[:, th * 1024:(th + 1) * 1024],
                        rsc,
                        kbar[:, ut * TIN + th * 1024: ut * TIN + (th + 1) * 1024],
                        1024, vseq)
                    v.sem_inc(s_krp.h, 1)
                    s_krp.n += 1
            for tt in range(24):
                v.wait_ge(s_vmm.h, tt + 1)
                if tt >= 2:
                    v.wait_ge(s_vst[tt % 2].h, (tt // 2) * 16)
                s_vev.inc(v.tensor_copy(vev[:, (tt % 2) * 512:(tt % 2 + 1) * 512],
                                        PS[:, 2048 + (tt % 2) * 512:
                                           2048 + (tt % 2 + 1) * 512]))
            for ut in range(16):
                for th in range(2):
                    G = ut * 2 + th
                    v.wait_ge(s_qmm.h, G + 1)
                    if ut >= 2 and th == 0:
                        v.wait_ge(s_qst[ut % 2].h, (ut // 2) * 16)
                    emit_rope(
                        v, PS[:, (G % 2) * 1024:(G % 2) * 1024 + 1024],
                        cfull[:, 1024 + th * 1024: 1024 + (th + 1) * 1024],
                        sfull[:, 1024 + th * 1024: 1024 + (th + 1) * 1024],
                        rsc,
                        qev[:, (ut % 2) * 2048 + th * 1024:
                            (ut % 2) * 2048 + (th + 1) * 1024],
                        1024, vseq)
                    v.sem_inc(s_qrp.h, 1)
                    s_qrp.n += 1

        if stages < 2:
            st01.close(); st12.close()
            return nc
        st01.close()
        st2 = ExitStack()
        msk = ph.sbuf("msk", [128, 2 * 12 * 512], BF16, st2)
        vwin = ph.sbuf("vwin", [128, 2 * 12 * 512], BF16, st2)
        qh = ph.sbuf("qh", [128, 2 * 512], BF16, st2)
        pbuf = ph.sbuf("pbuf", [128, 2 * 512], BF16, st2)
        pm = ph.sbuf("pm", [128, 2 * 512], BF16, st2)
        dacc = ph.sbuf("dacc", [128, 512], F16, st2)
        dsum = ph.sbuf("dsum", [1, 512], F32, st2)
        rec = ph.sbuf("rec", [1, 512], F32, st2)
        recbf = ph.sbuf("recbf", [1, 512], BF16, st2)
        rsb = ph.sbuf("rsb", [128, 512], F32, st2)
        atb = ph.sbuf("atb", [128, 2 * 512], BF16, st2)

        s_mld = [ph.sem("mld0"), ph.sem("mld1")]
        s_qhld = [ph.sem("qhld0"), ph.sem("qhld1")]
        s_smm = ph.sem("smm")
        s_exp = ph.sem("exp")
        s_msk = ph.sem("mskd")
        s_dac = ph.sem("dac")
        s_omm = ph.sem("omm")
        s_dbf = ph.sem("dbfs")
        s_dmm = ph.sem("dmm")
        s_dsm = ph.sem("dsm")
        s_rc = ph.sem("rc")
        s_rmm = ph.sem("rmm")
        s_rsb = ph.sem("rsbs")
        s_nrm = ph.sem("nrm")
        s_ast = [ph.sem("ast0"), ph.sem("ast1")]

        @ph.block.sync
        def _(sync):
            sync.wait_ge(s_qst[0].h, 8 * 16)
            sync.wait_ge(s_qst[1].h, 8 * 16)
            sync.wait_ge(s_vst[0].h, 12 * 16)
            sync.wait_ge(s_vst[1].h, 12 * 16)
            def emit_at3_store(sync, it2):
                qb2, h2 = it2 // 16, it2 % 16
                sync.wait_ge(s_nrm.h, it2 + 1)
                s_ast[it2 % 2].inc(sync.dma_start(
                    out=io["at3"][h2, qb2 * 4:(qb2 + 1) * 4]
                    .rearrange("j p t -> p j t"),
                    in_=atb[:, (it2 % 2) * 512:(it2 % 2 + 1) * 512]
                    .rearrange("p (j t) -> p j t", j=4)), 16)

            for qb in range(4):
                sq = qb % 2
                if qb >= 2:
                    sync.wait_ge(s_nrm.h, (qb - 1) * 16)
                s_mld[sq].inc(sync.dma_start(
                    out=msk[:, sq * 12 * 512:(sq + 1) * 12 * 512]
                    .rearrange("p (c q) -> p c q", q=512),
                    in_=io["l0mask"][qb].rearrange("c p q -> p c q")), 16)
                s_mld[sq].inc(sync.dma_start(
                    out=vwin[:, sq * 12 * 512:(sq + 1) * 12 * 512]
                    .rearrange("p (c u) -> p c u", u=512),
                    in_=io["vsc"][qb * 512: qb * 512 + 1536, :]
                    .rearrange("(c p) u -> p c u", p=128)), 16)
                for h in range(16):
                    it = qb * 16 + h
                    if it >= 2:
                        sync.wait_ge(s_smm.h, (it - 1) * 12)
                    s_qhld[it % 2].inc(sync.dma_start(
                        out=qh[:, (it % 2) * 512:(it % 2 + 1) * 512],
                        in_=io["qsc"][h * 128:(h + 1) * 128, qb * 512:(qb + 1) * 512]), 16)
                    if it >= 2:
                        emit_at3_store(sync, it - 2)
            emit_at3_store(sync, 62)
            emit_at3_store(sync, 63)

        @ph.block.tensor
        def _(t):
            t.wait_ge(s_qrp.h, 32)
            t.wait_ge(s_vst[0].h, 12 * 16)
            t.wait_ge(s_vst[1].h, 12 * 16)
            t.wait_ge(s_qst[0].h, 8 * 16)
            t.wait_ge(s_qst[1].h, 8 * 16)
            for qb in range(4):
                sq = qb % 2
                for h in range(16):
                    it = qb * 16 + h
                    g = h // 4
                    t.wait_ge(s_qhld[it % 2].h, (it // 2 + 1) * 16)
                    if h == 0:
                        t.wait_ge(s_mld[sq].h, (qb // 2 + 1) * 32)
                    if it >= 2:
                        t.wait_ge(s_nrm.h, it - 1)
                    ii_o = None

                    def emit_score(c):
                        ic = it * 12 + c
                        jc = qb * 4 + c
                        if ic >= 2:
                            t.wait_ge(s_exp.h, ic - 1)
                        ii = t.matmul(PS[:, (ic % 2) * 512:(ic % 2 + 1) * 512],
                                      kbar[:, g * TIN + jc * 128: g * TIN + (jc + 1) * 128],
                                      qh[:, (it % 2) * 512:(it % 2 + 1) * 512],
                                      start=True, stop=True)
                        s_smm.inc(ii)

                    def emit_av(c):
                        nonlocal ii_o
                        ic = it * 12 + c
                        jc = qb * 4 + c
                        t.wait_ge(s_msk.h, ic + 1)
                        ii_o = t.matmul(PS[:, 1024 + (it % 2) * 512:
                                           1024 + (it % 2 + 1) * 512],
                                        vwin[:, sq * 12 * 512 + c * 512 + g * 128:
                                             sq * 12 * 512 + c * 512 + g * 128 + 128],
                                        pm[:, (ic % 2) * 512:(ic % 2 + 1) * 512],
                                        start=(c == 0), stop=(c == 11))

                    emit_score(0)
                    for c in range(1, 12):
                        emit_score(c)
                        emit_av(c - 1)
                    emit_av(11)
                    s_omm.inc(ii_o)
                    t.wait_ge(s_dbf.h, it + 1)
                    if it >= 1:
                        t.wait_ge(s_dsm.h, it)
                    s_dmm.inc(t.matmul(PS[0:1, 2048:2560], ones_bf[:, :],
                                       dacc[:, :], start=True, stop=True))
                    t.wait_ge(s_rc.h, it + 1)
                    if it >= 1:
                        t.wait_ge(s_rsb.h, it)
                    s_rmm.inc(t.matmul(PS[:, 2560:3072], ones_row[:, :],
                                       recbf[:, :], start=True, stop=True))

        @ph.block.scalar
        def _(s):
            for qb in range(4):
                for h in range(16):
                    it = qb * 16 + h
                    for c in range(12):
                        ic = it * 12 + c
                        s.wait_ge(s_smm.h, ic + 1)
                        if ic >= 2:
                            s.wait_ge(s_msk.h, ic - 1)
                        s_exp.inc(s.activation(pbuf[:, (ic % 2) * 512:(ic % 2 + 1) * 512],
                                               PS[:, (ic % 2) * 512:(ic % 2 + 1) * 512],
                                               AF.Exp, scale=INVSQ_HD))
                    s.wait_ge(s_dmm.h, it + 1)
                    s_dsm.inc(s.copy(dsum[:, :], PS[0:1, 2048:2560]))
                    s.wait_ge(s_rmm.h, it + 1)
                    if it >= 1:
                        s.wait_ge(s_nrm.h, it)
                    s_rsb.inc(s.copy(rsb[:, :], PS[:, 2560:3072]))

        @ph.block.vector
        def _(v):
            for qb in range(4):
                sq = qb % 2
                for h in range(16):
                    it = qb * 16 + h
                    for c in range(12):
                        ic = it * 12 + c
                        v.wait_ge(s_exp.h, ic + 1)
                        edge(v, vseq, v.tensor_tensor(
                            pm[:, (ic % 2) * 512:(ic % 2 + 1) * 512],
                            pbuf[:, (ic % 2) * 512:(ic % 2 + 1) * 512],
                            msk[:, sq * 12 * 512 + c * 512: sq * 12 * 512 + (c + 1) * 512],
                            ALU.mult))
                        v.sem_inc(s_msk.h, 1)
                        s_msk.n += 1
                        if c == 0:
                            edge(v, vseq, v.tensor_copy(dacc[:, :],
                                                        pm[:, (ic % 2) * 512:(ic % 2 + 1) * 512]))
                        else:
                            edge(v, vseq, v.tensor_tensor(
                                dacc[:, :], dacc[:, :],
                                pm[:, (ic % 2) * 512:(ic % 2 + 1) * 512], ALU.add))
                    v.sem_inc(s_dbf.h, 1)
                    s_dbf.n += 1
                    v.wait_ge(s_dsm.h, it + 1)
                    if it >= 1:
                        v.wait_ge(s_rmm.h, it)
                    edge(v, vseq, v.reciprocal(rec[:, :], dsum[:, :]))
                    s_rc.inc(v.tensor_copy(recbf[:, :], rec[:, :]))
                    v.wait_ge(s_rsb.h, it + 1)
                    v.wait_ge(s_omm.h, it + 1)
                    if it >= 2:
                        v.wait_ge(s_ast[it % 2].h, (it // 2) * 16)
                    s_nrm.inc(v.tensor_tensor(atb[:, (it % 2) * 512:(it % 2 + 1) * 512],
                                              PS[:, 1024 + (it % 2) * 512:
                                                 1024 + (it % 2 + 1) * 512],
                                              rsb[:, :], ALU.mult))

        if stages < 3:
            st2.close(); st12.close()
            return nc
        st2.close()
        st12.close()
        wobf = ph.sbuf("wobf", [128, 16 * 2048], BF16)
        atc = ph.sbuf("atc", [128, 2 * 2048], BF16)
        fbuf = ph.sbuf("fbuf", [128, 2 * 2048], F32)
        lnf = ph.sbuf("lnf", [128, 2048], F32)
        xo = ph.sbuf("xo", [128, 2 * 2048], F32)
        obuf = ph.sbuf("obuf", [128, 2 * 2048], F32)
        fb6 = ph.sbuf("fb6", [128, 24], F32)
        fmv = ph.sbuf("fmv", [128, 2], F32)
        frs = ph.sbuf("frs", [128, 1], F32)
        fnm = ph.sbuf("fnm", [128, 1], F32)

        s_wo = ph.sem("wo")
        s_atc = [ph.sem("atc0"), ph.sem("atc1")]
        s_xo = [ph.sem("xo0"), ph.sem("xo1")]
        s_fmm = ph.sem("fmm")
        s_fev = ph.sem("fev")
        s_fbn = ph.sem("fbn")
        s_fmv = ph.sem("fmv")
        s_frs = ph.sem("frs")
        s_fnm = ph.sem("fnm")
        s_fap = ph.sem("fap")
        s_res = ph.sem("res")
        s_ost = [ph.sem("ost0"), ph.sem("ost1")]

        @ph.block.gpsimd
        def _(g):
            g.wait_ge(s_ast[0].h, 32 * 16)
            g.wait_ge(s_ast[1].h, 32 * 16)
            s_wo.inc(g.dma_start(
                out=wobf[:, :].rearrange("p (hc d) -> p hc d", d=2048),
                in_=io["wo"].rearrange("(hc p) d -> p hc d", p=128)), 16)

        @ph.block.sync
        def _(sync):
            sync.wait_ge(s_ast[0].h, 32 * 16)
            sync.wait_ge(s_ast[1].h, 32 * 16)
            for tt in range(16):
                sl = tt % 2
                if tt >= 2:
                    sync.wait_ge(s_fmm.h, tt - 1)
                s_atc[sl].inc(sync.dma_start(
                    out=atc[:, sl * 2048:(sl + 1) * 2048]
                    .rearrange("p (hc t) -> p hc t", t=128),
                    in_=io["at3"][:, tt].rearrange("hc p t -> p hc t")), 16)
                if tt >= 2:
                    sync.wait_ge(s_res.h, tt - 1)
                s_xo[sl].inc(sync.dma_start(
                    out=xo[:, sl * 2048:(sl + 1) * 2048],
                    in_=io["xin"][1024 + tt * 128: 1024 + (tt + 1) * 128, :]), 16)
                sync.wait_ge(s_res.h, tt + 1)
                s_ost[sl].inc(sync.dma_start(
                    out=io["x1"][tt * 128:(tt + 1) * 128, :],
                    in_=obuf[:, sl * 2048:(sl + 1) * 2048]), 16)
            sync.wait_ge(s_ost[0].h, 8 * 16)
            sync.wait_ge(s_ost[1].h, 8 * 16)

        @ph.block.tensor
        def _(t):
            t.wait_ge(s_nrm.h, 64)
            t.wait_ge(s_wo.h, 16)
            for tt in range(16):
                sl = tt % 2
                t.wait_ge(s_atc[sl].h, (tt // 2 + 1) * 16)
                if tt >= 1:
                    t.wait_ge(s_fev.h, tt)
                ii = None
                for hd in range(16):
                    for dc in range(4):
                        ii = t.matmul(PS[:, dc * 512:(dc + 1) * 512],
                                      atc[:, sl * 2048 + hd * 128: sl * 2048 + (hd + 1) * 128],
                                      wobf[:, hd * 2048 + dc * 512: hd * 2048 + (dc + 1) * 512],
                                      start=(hd == 0), stop=(hd == 15))
                s_fmm.inc(ii)

        @ph.block.scalar
        def _(s):
            for tt in range(16):
                sl = tt % 2
                s.wait_ge(s_fmm.h, tt + 1)
                if tt >= 2:
                    s.wait_ge(s_fbn.h, tt - 1)
                s_fev.inc(s.copy(fbuf[:, sl * 2048:(sl + 1) * 2048], PS[:, 0:2048]))
                s.wait_ge(s_fmv.h, tt + 1)
                edge(s, sseq, s.activation(frs[:, :], fmv[:, 1:2], AF.Ln, bias=eps[:, :]))
                s_frs.inc(s.activation(frs[:, :], frs[:, :], AF.Exp, scale=-0.5))
                s.wait_ge(s_frs.h, tt + 1)
                s.wait_ge(s_fnm.h, tt + 1)
                if tt >= 1:
                    s.wait_ge(s_res.h, tt)
                s_fap.inc(s.activation(lnf[:, :], fbuf[:, sl * 2048:(sl + 1) * 2048],
                                       AF.Identity, scale=frs[:, :], bias=fnm[:, :]))

        @ph.block.vector
        def _(v):
            for tt in range(16):
                sl = tt % 2
                v.wait_ge(s_fev.h, tt + 1)
                ii = None
                for j in range(4):
                    ii = v.bn_stats(fb6[:, j * 6:(j + 1) * 6],
                                    fbuf[:, sl * 2048 + j * 512: sl * 2048 + (j + 1) * 512])
                s_fbn.inc(ii)
                v.wait_ge(s_fbn.h, tt + 1)
                s_fmv.inc(v.bn_aggr(fmv[:, :], fb6[:, :]))
                v.wait_ge(s_fmv.h, tt + 1)
                v.wait_ge(s_frs.h, tt + 1)
                s_fnm.inc(v.tensor_scalar(fnm[:, :], fmv[:, 0:1], frs[:, :], -1.0,
                                          ALU.mult, ALU.mult))
                v.wait_ge(s_fap.h, tt + 1)
                v.wait_ge(s_xo[sl].h, (tt // 2 + 1) * 16)
                if tt >= 2:
                    v.wait_ge(s_ost[sl].h, (tt // 2) * 16)
                s_res.inc(v.tensor_tensor(obuf[:, sl * 2048:(sl + 1) * 2048],
                                          lnf[:, :], xo[:, sl * 2048:(sl + 1) * 2048],
                                          ALU.add))
    return nc



def rope_tables(positions):
    inv_freq = 1.0 / (10000.0 ** (np.arange(0, HD, 2, dtype=np.float64) / HD))
    freqs = np.asarray(positions, dtype=np.float64)[:, None] * inv_freq[None, :]
    emb = np.concatenate([freqs, freqs], axis=1)
    return (np.cos(emb).T.astype(np.float32).copy(),
            np.sin(emb).T.astype(np.float32).copy())


def l0_masks(half):
    import ml_dtypes
    m = np.zeros((4, 12, 128, 512), dtype=np.float32)
    for qb in range(4):
        for c in range(12):
            j = qb * 512 + c * 128 + np.arange(128)[:, None]
            qo = qb * 512 + np.arange(512)[None, :]
            ok = (j >= qo) & (j <= qo + 1024)
            if half == 0:
                ok &= (j >= 1024)
            m[qb, c] = ok.astype(np.float32)
    return m.astype(ml_dtypes.bfloat16)



def emit_resid_tail_scalar(ph, s, *, tt, sl, fbuf, lnf, eps, frs, fnm, fb6, fmv,
                           sems, sseq):
    s.wait_ge(sems["fmv"].h, tt + 1)
    edge(s, sseq, s.activation(frs[:, :], fmv[:, 1:2], AF.Ln, bias=eps[:, :]))
    sems["frs"].inc(s.activation(frs[:, :], frs[:, :], AF.Exp, scale=-0.5))
    s.wait_ge(sems["frs"].h, tt + 1)
    s.wait_ge(sems["fnm"].h, tt + 1)
    if tt >= 1:
        s.wait_ge(sems["res"].h, tt)
    sems["fap"].inc(s.activation(lnf[:, :], fbuf[:, sl * 2048:(sl + 1) * 2048],
                                 AF.Identity, scale=frs[:, :], bias=fnm[:, :]))


def build_phase_b(nc, io):
    with ExitStack() as stack:
        block = stack.enter_context(nc.Block())
        ph = Phase(nc, stack, block, "B")

        ident = ph.sbuf("ident", [128, 128], BF16)
        eps = ph.sbuf("eps", [128, 1], F32)
        ones_f = ph.sbuf("onesf", [128, 1], F32)
        onesr_f = ph.sbuf("onesrf", [1, 128], F32)
        ones_b = ph.sbuf("onesbB", [128, 1], BF16)
        psbf = ph.psum("psbf", [128, 2048], BF16)
        PS = ph.psum("ps", [128, 3072], F32)
        vseq = ph.sem("vseq")
        sseq = ph.sem("sseq")
        s_pre = ph.sem("pre")
        warmB = make_warm(ph, PS, ones_b, ident[:, 0:64], 2048)

        @ph.block.sync
        def _(sync):
            s_pre.inc(sync.dma_start(out=ident[:, :], in_=io["ident"][:, :]), 16)

        @ph.block.vector
        def _(v):
            v.memset(eps[:, :], 1e-5)
            v.memset(ones_f[:, :], 1.0)
            v.memset(onesr_f[:, :], 1.0)
            v.memset(ones_b[:, :], 1.0)
            v.wait_ge(s_pre.h, 16)

        gcol = ph.sbuf("gcol", [128, 104], F32)
        gsum = ph.sbuf("gsum", [1, 104], F32)
        hcol = ph.sbuf("hcol", [128, 6], F32)

        st0 = ExitStack()
        stLN = ExitStack()
        xhat = ph.sbuf("xhat", [128, 16 * 2048], BF16, st0)

        def dstb(ci, tc):
            return xhat[:, ci * 2048 + tc * 128: ci * 2048 + (tc + 1) * 128]

        def dstgb(g, tc):
            return xhat[:, :].rearrange("p (ci t) -> p ci t", t=2048)[
                :, g * 4:(g + 1) * 4, tc * 128:(tc + 1) * 128]

        lns = emit_ln_xhat(ph, src_hbm=io["x1"], n_tc=16, dst_tiles=dstb,
                           ident=ident, eps=eps, psbf=psbf, name="lnb",
                           bufstack=stLN, dst_group=dstgb, warm=warmB)
        s_ev = lns["ev"]
        EV_ALL = s_ev.n
        stLN.close()

        stG = ExitStack()
        gbuf = ph.sbuf("gbuf", [128, 2 * 5632], F32, stG)
        gscr = ph.sbuf("gscr", [128, 5632], BF16, stG)

        s_gld = [ph.sem("gld0"), ph.sem("gld1")]
        s_gab = ph.sem("gab")
        s_gmm = ph.sem("gmm")
        s_gsm = ph.sem("gsm")
        s_hred = ph.sem("hred")
        s_hmm = ph.sem("hmm")
        s_hcp = ph.sem("hcp")

        g_chunks = [("w1", i, 2048) for i in range(44)] + \
                   [("w2", i, 2048) for i in range(44)] + \
                   [("w3", i, 5632) for i in range(16)]

        @ph.block.gpsimd
        def _(g):
            g.wait_ge(s_ev.h, EV_ALL)
            for k, (wn, i, w) in enumerate(g_chunks):
                sl = k % 2
                if k >= 2:
                    g.wait_ge(s_gab.h, k - 1)
                s_gld[sl].inc(g.dma_start(
                    out=gbuf[:, sl * 5632: sl * 5632 + w],
                    in_=io[wn][i * 128:(i + 1) * 128, :]), 16)

        @ph.block.scalar
        def _(s):
            for k, (wn, i, w) in enumerate(g_chunks):
                sl = k % 2
                s.wait_ge(s_gld[sl].h, (k // 2 + 1) * 16)
                if k >= 1:
                    s.wait_ge(s_gab.h, k)
                s_gab.inc(s.activation(gscr[:, 0:w], gbuf[:, sl * 5632: sl * 5632 + w],
                                       AF.Abs, accum_out=gcol[:, k:k + 1]))

        @ph.block.tensor
        def _(t):
            t.wait_ge(s_gab.h, 104)
            s_gmm.inc(t.matmul(PS[0:1, 2048:2152], ones_f[:, :], gcol[:, 0:104],
                               start=True, stop=True))

        @ph.block.scalar
        def _(s):
            s.wait_ge(s_gmm.h, 1)
            s_gsm.inc(s.copy(gsum[:, :], PS[0:1, 2048:2152]))

        @ph.block.vector
        def _(v):
            v.wait_ge(s_gsm.h, 1)
            for wi, (lo, hi, numel) in enumerate([(0, 44, 5632 * 2048),
                                                  (44, 88, 5632 * 2048),
                                                  (88, 104, 2048 * 5632)]):
                edge(v, vseq, v.reduce_sum(gsum[0:1, lo:lo + 1] if False else
                                           hcol[0:1, wi:wi + 1],
                                           gsum[0:1, lo:hi], mybir.AxisListType.X))
                edge(v, vseq, v.tensor_scalar(hcol[0:1, wi:wi + 1], hcol[0:1, wi:wi + 1],
                                              0.5 / numel, 0.5e-8, ALU.mult, ALU.add))
            s_hred.inc(v.tensor_scalar(hcol[0:1, 3:6], hcol[0:1, 0:3], -1.0, None,
                                       ALU.mult))

        @ph.block.tensor
        def _(t):
            t.wait_ge(s_hred.h, 1)
            s_hmm.inc(t.matmul(PS[:, 2560:2566], onesr_f[:, :], hcol[0:1, 0:6],
                               start=True, stop=True))

        @ph.block.scalar
        def _(s):
            s.wait_ge(s_hmm.h, 1)
            s_hcp.inc(s.copy(hcol[:, 0:6], PS[:, 2560:2566]))

        stG.close()
        wbuf = ph.sbuf("wbuf", [128, 2 * 2048], F32, st0)
        wbuf2 = ph.sbuf("wbuf2", [128, 2 * 2048], F32, st0)
        p1s = ph.sbuf("p1s", [128, 2048], F32, st0)
        qn = ph.sbuf("qn", [128, 2 * 2048], BF16, st0)
        qn2 = ph.sbuf("qn2", [128, 2 * 2048], BF16, st0)
        q1t = ph.sbuf("q1t", [128, 2 * 2048], BF16, st0)
        q2t = ph.sbuf("q2t", [128, 2 * 2048], BF16, st0)
        sgb = ph.sbuf("sgb", [128, 2 * 512], F32, st0)
        hid = ph.sbuf("hid", [128, 2 * 512], BF16, st0)

        s_w1ld = [ph.sem("w1ld0"), ph.sem("w1ld1")]
        s_w2ld = [ph.sem("w2ld0"), ph.sem("w2ld1")]
        s_qz = ph.sem("qz")
        s_qtp = ph.sem("qtp")
        s_qev = ph.sem("qev")
        s_xmm = ph.sem("xmm")
        s_sg = ph.sem("sg")
        s_hd = ph.sem("hd")
        s_hst = [ph.sem("hst0"), ph.sem("hst1")]

        @ph.block.sync
        def _(sync):
            sync.wait_ge(s_gab.h, 104)
            def emit_h3_store(hcx):
                for tb in range(4):
                    it = hcx * 4 + tb
                    sync.wait_ge(s_hd.h, it + 1)
                    s_hst[it % 2].inc(sync.dma_start(
                        out=io["h3"][hcx, tb * 4:(tb + 1) * 4]
                        .rearrange("j p t -> p j t"),
                        in_=hid[:, (it % 2) * 512:(it % 2 + 1) * 512]
                        .rearrange("p (j t) -> p j t", j=4)), 16)

            for hc in range(44):
                sl = hc % 2
                if hc >= 2:
                    sync.wait_ge(s_qz.h, 2 * (hc - 1))
                s_w1ld[sl].inc(sync.dma_start(
                    out=wbuf[:, sl * 2048:(sl + 1) * 2048],
                    in_=io["w1"][hc * 128:(hc + 1) * 128, :]), 16)
                s_w2ld[sl].inc(sync.dma_start(
                    out=wbuf2[:, sl * 2048:(sl + 1) * 2048],
                    in_=io["w2"][hc * 128:(hc + 1) * 128, :]), 16)
                if hc >= 1:
                    emit_h3_store(hc - 1)
            emit_h3_store(43)

        @ph.block.vector
        def _(v):
            v.wait_ge(s_hcp.h, 1)

            def emit_hidden(hcx):
                for tb in range(4):
                    it = hcx * 4 + tb
                    v.wait_ge(s_sg.h, it + 1)
                    v.wait_ge(s_xmm.h, hcx * 4 + (tb // 2) * 2 + 2)
                    if it >= 2:
                        v.wait_ge(s_hst[it % 2].h, (it // 2) * 16)
                    s_hd.inc(v.tensor_tensor(
                        hid[:, (it % 2) * 512:(it % 2 + 1) * 512],
                        sgb[:, (it % 2) * 512:(it % 2 + 1) * 512],
                        PS[:, 1024 + (it % 2) * 512: 1024 + (it % 2 + 1) * 512],
                        ALU.mult))

            for hc in range(44):
                sl = hc % 2
                v.wait_ge(s_w1ld[sl].h, (hc // 2 + 1) * 16)
                if hc >= 2:
                    v.wait_ge(s_qtp.h, 16 * (hc - 1))
                edge(v, vseq, v.tensor_scalar(p1s[:, :], wbuf[:, sl * 2048:(sl + 1) * 2048],
                                              hcol[:, 0:1], -1.0, ALU.is_ge, ALU.add))
                edge(v, vseq, v.scalar_tensor_tensor(
                    qn[:, sl * 2048:(sl + 1) * 2048], wbuf[:, sl * 2048:(sl + 1) * 2048],
                    hcol[:, 3:4], p1s[:, :], ALU.is_ge, ALU.add))
                v.sem_inc(s_qz.h, 1)
                s_qz.n += 1
                v.wait_ge(s_w2ld[sl].h, (hc // 2 + 1) * 16)
                edge(v, vseq, v.tensor_scalar(p1s[:, :], wbuf2[:, sl * 2048:(sl + 1) * 2048],
                                              hcol[:, 1:2], -1.0, ALU.is_ge, ALU.add))
                edge(v, vseq, v.scalar_tensor_tensor(
                    qn2[:, sl * 2048:(sl + 1) * 2048], wbuf2[:, sl * 2048:(sl + 1) * 2048],
                    hcol[:, 4:5], p1s[:, :], ALU.is_ge, ALU.add))
                v.sem_inc(s_qz.h, 1)
                s_qz.n += 1
                for g8 in range(16):
                    G = hc * 16 + g8
                    v.wait_ge(s_qtp.h, G + 1)
                    if hc >= 2 and g8 == 0:
                        v.wait_ge(s_xmm.h, 4 * (hc - 1))
                    wq, dstq = (qn, q1t) if g8 < 8 else (qn2, q2t)
                    gg = g8 % 8
                    half = G % 2
                    s_qev.inc(v.tensor_copy(
                        dstq[:, sl * 2048 + gg * 256: sl * 2048 + (gg + 1) * 256],
                        psbf[:, half * 1024: half * 1024 + 256]))
                if hc >= 1:
                    emit_hidden(hc - 1)
            emit_hidden(43)

        @ph.block.tensor
        def _(t):
            t.wait_ge(s_hcp.h, 1)

            def emit_tp(hc):
                sl = hc % 2
                for g8 in range(16):
                    G = hc * 16 + g8
                    t.wait_ge(s_qz.h, 2 * hc + (1 if g8 < 8 else 2))
                    if G >= 2:
                        t.wait_ge(s_qev.h, G - 1)
                    warmB(t)
                    wq = qn if g8 < 8 else qn2
                    gg = g8 % 8
                    half = G % 2
                    ii = None
                    for j in range(2):
                        ci = gg * 2 + j
                        ii = t.transpose(psbf[:, half * 1024 + j * 128:
                                              half * 1024 + (j + 1) * 128],
                                         wq[:, sl * 2048 + ci * 128: sl * 2048 + (ci + 1) * 128],
                                         ident[:, :])
                    s_qtp.inc(ii)

            def emit_mm(hc):
                sl = hc % 2
                t.wait_ge(s_qev.h, 16 * (hc + 1))
                for tp2 in range(2):
                    it0 = hc * 4 + tp2 * 2
                    t.wait_ge(s_sg.h, max(0, it0))
                    t.wait_ge(s_hd.h, max(0, it0))
                    ii = None
                    for ci in range(16):
                        for tb in (tp2 * 2, tp2 * 2 + 1):
                            ii = t.matmul(PS[:, (tb % 2) * 512:(tb % 2 + 1) * 512],
                                          q1t[:, sl * 2048 + ci * 128: sl * 2048 + (ci + 1) * 128],
                                          xhat[:, ci * 2048 + tb * 512: ci * 2048 + (tb + 1) * 512],
                                          start=(ci == 0), stop=(ci == 15))
                    s_xmm.inc(ii)
                    ii = None
                    for ci in range(16):
                        for tb in (tp2 * 2, tp2 * 2 + 1):
                            ii = t.matmul(PS[:, 1024 + (tb % 2) * 512: 1024 + (tb % 2 + 1) * 512],
                                          q2t[:, sl * 2048 + ci * 128: sl * 2048 + (ci + 1) * 128],
                                          xhat[:, ci * 2048 + tb * 512: ci * 2048 + (tb + 1) * 512],
                                          start=(ci == 0), stop=(ci == 15))
                    s_xmm.inc(ii)

            emit_tp(0)
            for hc in range(1, 44):
                emit_tp(hc)
                emit_mm(hc - 1)
            emit_mm(43)

        @ph.block.scalar
        def _(s):
            s.wait_ge(s_hcp.h, 1)
            for hc in range(44):
                for tb in range(4):
                    it = hc * 4 + tb
                    s.wait_ge(s_xmm.h, hc * 4 + (tb // 2) * 2 + 1)
                    if it >= 2:
                        s.wait_ge(s_hd.h, it - 1)
                    s_sg.inc(s.activation(sgb[:, (it % 2) * 512:(it % 2 + 1) * 512],
                                          PS[:, (it % 2) * 512:(it % 2 + 1) * 512],
                                          AF.Sigmoid))

        st0.close()
        st2 = ExitStack()
        w3b = ph.sbuf("w3b", [128, 2 * 5632], F32, st2)
        p3s = ph.sbuf("p3s", [128, 5632], F32, st2)
        q3n = ph.sbuf("q3n", [128, 2 * 5632], BF16, st2)
        q3t = ph.sbuf("q3t", [128, 2 * 44 * 512], BF16, st2)
        hc3 = ph.sbuf("hc3", [128, 2 * 5632], BF16, st2)
        fdb = ph.sbuf("fdb", [128, 2 * 512], F32, st2)

        s_w3ld = [ph.sem("w3ld0"), ph.sem("w3ld1")]
        s_q3z = ph.sem("q3z")
        s_q3tp = ph.sem("q3tp")
        s_q3ev = ph.sem("q3ev")
        s_h3ld = [ph.sem("h3ld0"), ph.sem("h3ld1")]
        s_fmm = ph.sem("fmmB")
        s_fevB = ph.sem("fevB")
        s_fst = [ph.sem("fst0"), ph.sem("fst1")]

        NG3 = 11

        @ph.block.sync
        def _(sync):
            sync.wait_ge(s_hst[0].h, 88 * 16)
            sync.wait_ge(s_hst[1].h, 88 * 16)
            def emit_fio(dbx):
                for tt in range(16):
                    it = dbx * 16 + tt
                    sl = it % 2
                    if it >= 2:
                        sync.wait_ge(s_fmm.h, it - 1)
                    s_h3ld[sl].inc(sync.dma_start(
                        out=hc3[:, sl * 5632: sl * 5632 + 44 * 128]
                        .rearrange("p (hcx t) -> p hcx t", t=128),
                        in_=io["h3"][:, tt].rearrange("hcx p t -> p hcx t")), 16)
                    sync.wait_ge(s_fevB.h, it + 1)
                    s_fst[sl].inc(sync.dma_start(
                        out=io["fsc"][tt * 128:(tt + 1) * 128, dbx * 512:(dbx + 1) * 512],
                        in_=fdb[:, sl * 512:(sl + 1) * 512]), 16)

            for db in range(4):
                for dr in range(4):
                    k = db * 4 + dr
                    sl = k % 2
                    if k >= 2:
                        sync.wait_ge(s_q3z.h, k - 1)
                    s_w3ld[sl].inc(sync.dma_start(
                        out=w3b[:, sl * 5632:(sl + 1) * 5632],
                        in_=io["w3"][k * 128:(k + 1) * 128, :]), 16)
                if db >= 1:
                    emit_fio(db - 1)
            emit_fio(3)

        @ph.block.vector
        def _(v):
            for db in range(4):
                for dr in range(4):
                    k = db * 4 + dr
                    sl = k % 2
                    v.wait_ge(s_w3ld[sl].h, (k // 2 + 1) * 16)
                    if k >= 2:
                        v.wait_ge(s_q3tp.h, NG3 * (k - 1))
                    edge(v, vseq, v.tensor_scalar(p3s[:, :], w3b[:, sl * 5632:(sl + 1) * 5632],
                                                  hcol[:, 2:3], -1.0, ALU.is_ge, ALU.add))
                    edge(v, vseq, v.scalar_tensor_tensor(
                        q3n[:, sl * 5632:(sl + 1) * 5632], w3b[:, sl * 5632:(sl + 1) * 5632],
                        hcol[:, 5:6], p3s[:, :], ALU.is_ge, ALU.add))
                    v.sem_inc(s_q3z.h, 1)
                    s_q3z.n += 1
                    for g in range(NG3):
                        G = k * NG3 + g
                        v.wait_ge(s_q3tp.h, G + 1)
                        if db > 1 and dr == 0 and g == 0:
                            v.wait_ge(s_fmm.h, (db - 1) * 16)
                        qoff = (db % 2) * 44 * 512
                        ii = None
                        for j in range(4):
                            hcx = g * 4 + j
                            if hcx >= 44:
                                break
                            ii = v.tensor_copy(
                                q3t[:, qoff + hcx * 512 + dr * 128:
                                    qoff + hcx * 512 + (dr + 1) * 128],
                                psbf[:, (G % 2) * 1024 + j * 128:
                                     (G % 2) * 1024 + (j + 1) * 128])
                        s_q3ev.inc(ii)

        @ph.block.tensor
        def _(t):
            t.wait_ge(s_hst[0].h, 88 * 16)
            t.wait_ge(s_hst[1].h, 88 * 16)

            def emit_tp3(db):
                for dr in range(4):
                    k = db * 4 + dr
                    sl = k % 2
                    t.wait_ge(s_q3z.h, k + 1)
                    for g in range(NG3):
                        G = k * NG3 + g
                        if G >= 2:
                            t.wait_ge(s_q3ev.h, G - 1)
                        warmB(t)
                        ii = None
                        for j in range(4):
                            hcx = g * 4 + j
                            if hcx >= 44:
                                break
                            ii = t.transpose(psbf[:, (G % 2) * 1024 + j * 128:
                                                  (G % 2) * 1024 + (j + 1) * 128],
                                             q3n[:, sl * 5632 + hcx * 128:
                                                 sl * 5632 + (hcx + 1) * 128],
                                             ident[:, :])
                        s_q3tp.inc(ii)

            def emit_mm3(db):
                qoff = (db % 2) * 44 * 512
                t.wait_ge(s_q3ev.h, NG3 * 4 * (db + 1))
                for tt in range(16):
                    it = db * 16 + tt
                    sl = it % 2
                    t.wait_ge(s_h3ld[sl].h, (it // 2 + 1) * 16)
                    if it >= 2:
                        t.wait_ge(s_fevB.h, it - 1)
                    ii = None
                    for hcx in range(44):
                        ii = t.matmul(PS[:, sl * 512:(sl + 1) * 512],
                                      hc3[:, sl * 5632 + hcx * 128: sl * 5632 + (hcx + 1) * 128],
                                      q3t[:, qoff + hcx * 512: qoff + hcx * 512 + 512],
                                      start=(hcx == 0), stop=(hcx == 43))
                    s_fmm.inc(ii)

            emit_tp3(0)
            for db in range(1, 4):
                emit_tp3(db)
                emit_mm3(db - 1)
            emit_mm3(3)

        @ph.block.scalar
        def _(s):
            for db in range(4):
                for tt in range(16):
                    it = db * 16 + tt
                    sl = it % 2
                    s.wait_ge(s_fmm.h, it + 1)
                    if it >= 2:
                        s.wait_ge(s_fst[sl].h, (it // 2) * 16)
                    s_fevB.inc(s.copy(fdb[:, sl * 512:(sl + 1) * 512],
                                      PS[:, sl * 512:(sl + 1) * 512]))

        st2.close()
        fb = ph.sbuf("fb", [128, 2 * 2048], F32)
        xr = ph.sbuf("xr", [128, 2 * 2048], F32)
        ob = ph.sbuf("ob", [128, 2 * 2048], F32)
        lnf = ph.sbuf("lnfB", [128, 2048], F32)
        fb6 = ph.sbuf("fb6B", [128, 24], F32)
        fmv = ph.sbuf("fmvB", [128, 2], F32)
        frs = ph.sbuf("frsB", [128, 1], F32)
        fnm = ph.sbuf("fnmB", [128, 1], F32)

        s_fld = [ph.sem("fld0"), ph.sem("fld1")]
        s_xld = [ph.sem("xld0"), ph.sem("xld1")]
        sems3 = {n: ph.sem(n + "3") for n in
                 ["fbn", "fmv", "frs", "fnm", "fap", "res"]}
        s_ost = [ph.sem("ostB0"), ph.sem("ostB1")]

        @ph.block.sync
        def _(sync):
            sync.wait_ge(s_fst[0].h, 32 * 16)
            sync.wait_ge(s_fst[1].h, 32 * 16)
            for tc in range(16):
                sl = tc % 2
                if tc >= 2:
                    sync.wait_ge(sems3["fbn"].h, tc - 1)
                s_fld[sl].inc(sync.dma_start(
                    out=fb[:, sl * 2048:(sl + 1) * 2048],
                    in_=io["fsc"][tc * 128:(tc + 1) * 128, :]), 16)
                if tc >= 2:
                    sync.wait_ge(sems3["res"].h, tc - 1)
                s_xld[sl].inc(sync.dma_start(
                    out=xr[:, sl * 2048:(sl + 1) * 2048],
                    in_=io["x1"][tc * 128:(tc + 1) * 128, :]), 16)
                sync.wait_ge(sems3["res"].h, tc + 1)
                s_ost[sl].inc(sync.dma_start(
                    out=io["x2"][tc * 128:(tc + 1) * 128, :],
                    in_=ob[:, sl * 2048:(sl + 1) * 2048]), 16)
            sync.wait_ge(s_ost[0].h, 8 * 16)
            sync.wait_ge(s_ost[1].h, 8 * 16)

        @ph.block.vector
        def _(v):
            for tc in range(16):
                sl = tc % 2
                v.wait_ge(s_fld[sl].h, (tc // 2 + 1) * 16)
                if tc >= 2:
                    v.wait_ge(sems3["fap"].h, tc - 1)
                ii = None
                for j in range(4):
                    ii = v.bn_stats(fb6[:, j * 6:(j + 1) * 6],
                                    fb[:, sl * 2048 + j * 512: sl * 2048 + (j + 1) * 512])
                sems3["fbn"].inc(ii)
                v.wait_ge(sems3["fbn"].h, tc + 1)
                sems3["fmv"].inc(v.bn_aggr(fmv[:, :], fb6[:, :]))
                v.wait_ge(sems3["fmv"].h, tc + 1)
                v.wait_ge(sems3["frs"].h, tc + 1)
                sems3["fnm"].inc(v.tensor_scalar(fnm[:, :], fmv[:, 0:1], frs[:, :],
                                                 -1.0, ALU.mult, ALU.mult))
                v.wait_ge(sems3["fap"].h, tc + 1)
                v.wait_ge(s_xld[sl].h, (tc // 2 + 1) * 16)
                if tc >= 2:
                    v.wait_ge(s_ost[sl].h, (tc // 2) * 16)
                sems3["res"].inc(v.tensor_tensor(ob[:, sl * 2048:(sl + 1) * 2048],
                                                 lnf[:, :], xr[:, sl * 2048:(sl + 1) * 2048],
                                                 ALU.add))

        @ph.block.scalar
        def _(s):
            for tc in range(16):
                sl = tc % 2
                s.wait_ge(sems3["fmv"].h, tc + 1)
                edge(s, sseq, s.activation(frs[:, :], fmv[:, 1:2], AF.Ln, bias=eps[:, :]))
                sems3["frs"].inc(s.activation(frs[:, :], frs[:, :], AF.Exp, scale=-0.5))
                s.wait_ge(sems3["frs"].h, tc + 1)
                s.wait_ge(sems3["fnm"].h, tc + 1)
                if tc >= 1:
                    s.wait_ge(sems3["res"].h, tc)
                sems3["fap"].inc(s.activation(lnf[:, :], fb[:, sl * 2048:(sl + 1) * 2048],
                                              AF.Identity, scale=frs[:, :], bias=fnm[:, :]))
    return nc



def build_phase_c(nc, io, use_cc=True):
    build_phase_c_pre(nc, io)
    if use_cc:
        pairs = [[0, 1], [2, 3], [4, 5], [6, 7]]
        with ExitStack() as stack:
            block = stack.enter_context(nc.Block())
            ph = Phase(nc, stack, block, "Ccc")
            s_cc = ph.sem("cc")

            @ph.block.gpsimd
            def _(g):
                for src, dst in [("kvc1", "kvg1"), ("kvc2", "kvg2"),
                                 ("kvcv", "kvgv")]:
                    s_cc.inc(g.collective_compute(
                        "AllGather", ALU.bypass, replica_groups=pairs,
                        ins=[io[src + "_t"].ap().opt()],
                        outs=[io[dst + "_t"].ap().opt()]))
                g.wait_ge(s_cc.h, 3)
    build_phase_c_post(nc, io)
    return nc


def build_phase_c_pre(nc, io):
    with ExitStack() as stack:
        block = stack.enter_context(nc.Block())
        ph = Phase(nc, stack, block, "C")

        ident = ph.sbuf("ident", [128, 128], BF16)
        eps = ph.sbuf("eps", [128, 1], F32)
        ones_b = ph.sbuf("onesbC", [128, 1], BF16)
        psbf = ph.psum("psbf", [128, 2048], BF16)
        PS = ph.psum("ps", [128, 3072], F32)
        vseq = ph.sem("vseq")
        s_pre = ph.sem("pre")
        warmC = make_warm(ph, PS, ones_b, ident[:, 0:64], 2048)

        @ph.block.sync
        def _(sync):
            s_pre.inc(sync.dma_start(out=ident[:, :], in_=io["ident"][:, :]), 16)

        @ph.block.vector
        def _(v):
            v.memset(eps[:, :], 1e-5)
            v.memset(ones_b[:, :], 1.0)
            v.wait_ge(s_pre.h, 16)

        st0 = ExitStack()
        stLN = ExitStack()
        xhat = ph.sbuf("xhat", [128, 16 * 2048], BF16, st0)

        def dstc(ci, tc):
            return xhat[:, ci * 2048 + tc * 128: ci * 2048 + (tc + 1) * 128]

        def dstgc(g, tc):
            return xhat[:, :].rearrange("p (ci t) -> p ci t", t=2048)[
                :, g * 4:(g + 1) * 4, tc * 128:(tc + 1) * 128]

        lns = emit_ln_xhat(ph, src_hbm=io["x2"], n_tc=16, dst_tiles=dstc,
                           ident=ident, eps=eps, psbf=psbf, name="lnc",
                           bufstack=stLN, dst_group=dstgc, warm=warmC)
        s_ev = lns["ev"]
        EV_ALL = s_ev.n
        stLN.close()

        wnat = ph.sbuf("wnat", [128, 2 * 2048], BF16, st0)
        wtb = ph.sbuf("wtb", [128, 2 * 2048], BF16, st0)
        pev = ph.sbuf("pev", [128, 2 * 2048], BF16, st0)

        W_LIST = [("k1w", "kvc1"), ("k2w", "kvc2"), ("vw", "kvcv"),
                  ("q1w", "qsc1"), ("q2w", "qsc2")]
        s_wn = [ph.sem("wn0"), ph.sem("wn1")]
        s_wtp = ph.sem("wtp")
        s_wev = ph.sem("wev")
        s_pmm = ph.sem("pmmC")
        s_pevs = ph.sem("pevsC")
        s_pstg = [ph.sem("pstg0"), ph.sem("pstg1")]
        s_pw = ph.sem("pw")
        s_cc = ph.sem("ccC")

        pairs = [[0, 1], [2, 3], [4, 5], [6, 7]]
        AG_LIST = [("kvc1", "kvg1"), ("kvc2", "kvg2"), ("kvcv", "kvgv")]

        @ph.block.gpsimd
        def _(g):
            g.wait_ge(s_ev.h, EV_ALL)
            for u in range(80):
                wi, ut = u // 16, u % 16
                wsrc = W_LIST[wi][0]
                sl = u % 2
                if u >= 2:
                    g.wait_ge(s_wtp.h, 4 * (u - 1))
                s_wn[sl].inc(g.dma_start(
                    out=wnat[:, sl * 2048:(sl + 1) * 2048],
                    in_=io[wsrc][ut * 128:(ut + 1) * 128, :]), 16)
                if u in (17, 33, 49):
                    agw = (u - 17) // 16
                    g.wait_ge(s_pw.h, agw + 1)
                    src, dst = AG_LIST[agw]
                    s_cc.inc(g.collective_compute(
                        "AllGather", ALU.bypass, replica_groups=pairs,
                        ins=[io[src + "_t"].ap().opt()],
                        outs=[io[dst + "_t"].ap().opt()]))
            g.wait_ge(s_cc.h, 3)

        @ph.block.tensor
        def _(t):
            t.wait_ge(s_ev.h, EV_ALL)
            for u in range(80):
                wi, ut = u // 16, u % 16
                sl = u % 2
                t.wait_ge(s_wn[sl].h, (u // 2 + 1) * 16)
                for g4 in range(4):
                    G = u * 4 + g4
                    if G >= 2:
                        t.wait_ge(s_wev.h, G - 1)
                    warmC(t)
                    ii = None
                    for j in range(4):
                        ci = g4 * 4 + j
                        ii = t.transpose(psbf[:, (G % 2) * 1024 + j * 128:
                                              (G % 2) * 1024 + (j + 1) * 128],
                                         wnat[:, sl * 2048 + ci * 128:
                                              sl * 2048 + (ci + 1) * 128],
                                         ident[:, :])
                    s_wtp.inc(ii)
                t.wait_ge(s_wev.h, 4 * (u + 1))
                if u >= 1:
                    t.wait_ge(s_pevs.h, u)
                ii = None
                for ci in range(16):
                    for mo in range(4):
                        ii = t.matmul(PS[:, mo * 512:(mo + 1) * 512],
                                      wtb[:, sl * 2048 + ci * 128: sl * 2048 + (ci + 1) * 128],
                                      xhat[:, ci * 2048 + mo * 512: ci * 2048 + (mo + 1) * 512],
                                      start=(ci == 0), stop=(ci == 15))
                s_pmm.inc(ii)

        @ph.block.vector
        def _(v):
            for u in range(80):
                wi, ut = u // 16, u % 16
                wsrc = W_LIST[wi][0]
                sl = u % 2
                for g4 in range(4):
                    G = u * 4 + g4
                    v.wait_ge(s_wtp.h, G + 1)
                    if u >= 2 and g4 == 0:
                        v.wait_ge(s_pmm.h, u - 1)
                    ii = None
                    for j in range(4):
                        ci = g4 * 4 + j
                        ii = v.tensor_copy(
                            wtb[:, sl * 2048 + ci * 128: sl * 2048 + (ci + 1) * 128],
                            psbf[:, (G % 2) * 1024 + j * 128:
                                 (G % 2) * 1024 + (j + 1) * 128])
                    s_wev.inc(ii)
                v.wait_ge(s_pmm.h, u + 1)
                if u >= 2:
                    v.wait_ge(s_pstg[u % 2].h, (u // 2) * 16)
                s_pevs.inc(v.tensor_copy(pev[:, sl * 2048:(sl + 1) * 2048],
                                         PS[:, 0:2048]))

        @ph.block.sync
        def _(sync):
            for u in range(80):
                wi, ut = u // 16, u % 16
                wsrc, wdst = W_LIST[wi]
                sl = u % 2
                sync.wait_ge(s_pevs.h, u + 1)
                s_pstg[sl].inc(sync.dma_start(
                    out=io[wdst][ut * 128:(ut + 1) * 128, :],
                    in_=pev[:, sl * 2048:(sl + 1) * 2048]), 16)
                if u in (15, 31, 47):
                    k = (u + 1) // 2
                    sync.wait_ge(s_pstg[0].h, k * 16)
                    sync.wait_ge(s_pstg[1].h, k * 16)
                    sync.sem_inc(s_pw.h, 1)
                    s_pw.n += 1
            sync.wait_ge(s_pstg[0].h, 40 * 16)
            sync.wait_ge(s_pstg[1].h, 40 * 16)
        st0.close()
    return nc


def build_phase_c_post(nc, io):
    with ExitStack() as stack:
        block = stack.enter_context(nc.Block())
        ph = Phase(nc, stack, block, "Cp")

        ident = ph.sbuf("ident", [128, 128], BF16)
        eps = ph.sbuf("eps", [128, 1], F32)
        ones_bf = ph.sbuf("onesb", [128, 1], BF16)
        ones_row = ph.sbuf("onesr", [1, 128], BF16)
        lam = ph.sbuf("lam", [128, 16], F32)
        psbf = ph.psum("psbf", [128, 2048], BF16)
        PS = ph.psum("ps", [128, 3072], F32)
        vseq = ph.sem("vseq")
        sseq = ph.sem("sseq")
        s_pre = ph.sem("pre")

        @ph.block.sync
        def _(sync):
            s_pre.inc(sync.dma_start(out=ident[:, :], in_=io["ident"][:, :]), 16)
            s_pre.inc(sync.dma_start(out=lam[:, :], in_=io["lam_b"][:, :]), 16)

        @ph.block.vector
        def _(v):
            v.memset(eps[:, :], 1e-5)
            v.memset(ones_bf[:, :], 1.0)
            v.memset(ones_row[:, :], 1.0)
            v.wait_ge(s_pre.h, 32)

        st3 = ExitStack()
        k1h = ph.sbuf("k1h", [128, 2 * 4096], BF16, st3)
        k2h = ph.sbuf("k2h", [128, 2 * 4096], BF16, st3)
        vth = ph.sbuf("vth", [128, 2 * 4096], BF16, st3)
        vh = ph.sbuf("vh", [128, 2 * 4096], BF16, st3)
        q1h = ph.sbuf("q1h", [128, 2 * 2048], BF16, st3)
        q2h = ph.sbuf("q2h", [128, 2 * 2048], BF16, st3)
        pb = ph.sbuf("pb", [128, 2 * 1024], BF16, st3)
        dacc = ph.sbuf("daccC", [128, 1024], F16, st3)
        dsum = ph.sbuf("dsumC", [1, 1024], F32, st3)
        rec = ph.sbuf("recC", [1, 1024], F32, st3)
        recbf = ph.sbuf("recbfC", [1, 1024], BF16, st3)
        rsb = ph.sbuf("rsbC", [128, 1024], F32, st3)
        out1 = ph.sbuf("out1", [128, 1024], F32, st3)
        t2b = ph.sbuf("t2b", [128, 1024], F32, st3)
        atb = ph.sbuf("atbC", [128, 2 * 1024], BF16, st3)

        s_kld = [ph.sem("kld0"), ph.sem("kld1")]
        s_vtp = ph.sem("vtp")
        s_vev2 = ph.sem("vev2")
        s_smm = ph.sem("smmC")
        s_exp = ph.sem("expC")
        s_dac = ph.sem("dacC")
        s_omm = ph.sem("ommC")
        s_pod = ph.sem("pod")
        s_dbf = ph.sem("dbfsC")
        s_dmm = ph.sem("dmmC")
        s_dsm = ph.sem("dsmC")
        s_rc = ph.sem("rcC")
        s_rmm = ph.sem("rmmC")
        s_rsbs = ph.sem("rsbsC")
        s_rsd = ph.sem("rsd")
        s_nrm = ph.sem("nrmC")
        s_ast = [ph.sem("astC0"), ph.sem("astC1")]

        def its(h, qb, m):
            return (h * 2 + qb) * 2 + m

        @ph.block.sync
        def _(sync):
            for h in range(16):
                sl = h % 2
                if h >= 2:
                    sync.wait_ge(s_smm.h, 128 * (h - 1))
                    sync.wait_ge(s_vtp.h, 8 * (h - 1))
                for name, buf, src in [("k1", k1h, "kvg1"), ("k2", k2h, "kvg2"),
                                       ("vt", vth, "kvgv")]:
                    s_kld[sl].inc(sync.dma_start(
                        out=buf[:, sl * 4096: sl * 4096 + 2048],
                        in_=io[src][h * 128:(h + 1) * 128, :]), 16)
                    s_kld[sl].inc(sync.dma_start(
                        out=buf[:, sl * 4096 + 2048: sl * 4096 + 4096],
                        in_=io[src][2048 + h * 128: 2048 + (h + 1) * 128, :]), 16)
                s_kld[sl].inc(sync.dma_start(
                    out=q1h[:, sl * 2048:(sl + 1) * 2048],
                    in_=io["qsc1"][h * 128:(h + 1) * 128, :]), 16)
                s_kld[sl].inc(sync.dma_start(
                    out=q2h[:, sl * 2048:(sl + 1) * 2048],
                    in_=io["qsc2"][h * 128:(h + 1) * 128, :]), 16)
                if h >= 1:
                    for qb in range(2):
                        itq = (h - 1) * 2 + qb
                        sync.wait_ge(s_nrm.h, itq + 1)
                        s_ast[itq % 2].inc(sync.dma_start(
                            out=io["at3c"][h - 1, qb * 8:(qb + 1) * 8]
                            .rearrange("j p t -> p j t"),
                            in_=atb[:, (itq % 2) * 1024:(itq % 2 + 1) * 1024]
                            .rearrange("p (j t) -> p j t", j=8)), 16)
            for qb in range(2):
                itq = 15 * 2 + qb
                sync.wait_ge(s_nrm.h, itq + 1)
                s_ast[itq % 2].inc(sync.dma_start(
                    out=io["at3c"][15, qb * 8:(qb + 1) * 8]
                    .rearrange("j p t -> p j t"),
                    in_=atb[:, (itq % 2) * 1024:(itq % 2 + 1) * 1024]
                    .rearrange("p (j t) -> p j t", j=8)), 16)

        @ph.block.tensor
        def _(t):
            t.wait_ge(s_pre.h, 32)
            for h in range(16):
                sl = h % 2
                t.wait_ge(s_kld[sl].h, (h // 2 + 1) * 8 * 16)
                for g4 in range(8):
                    G = h * 8 + g4
                    if G >= 2:
                        t.wait_ge(s_vev2.h, G - 1)
                    ii = None
                    for j in range(4):
                        kc = g4 * 4 + j
                        ii = t.transpose(psbf[:, (G % 2) * 1024 + j * 128:
                                              (G % 2) * 1024 + (j + 1) * 128],
                                         vth[:, sl * 4096 + kc * 128: sl * 4096 + (kc + 1) * 128],
                                         ident[:, :])
                    s_vtp.inc(ii)
                t.wait_ge(s_vev2.h, 8 * (h + 1))
                for qb in range(2):
                    for m in range(2):
                        it2 = its(h, qb, m)
                        kmh = k1h if m == 0 else k2h
                        qmh = q1h if m == 0 else q2h
                        if it2 >= 1:
                            t.wait_ge(s_pod.h, it2)
                        ii_o = None

                        def emit_score(kc):
                            ic = it2 * 32 + kc
                            if ic >= 2:
                                t.wait_ge(s_exp.h, ic - 1)
                            if kc == 0 and it2 >= 1:
                                t.wait_ge(s_dsm.h, it2)
                            if kc == 1 and it2 >= 1:
                                t.wait_ge(s_rsbs.h, it2)
                            ii = None
                            for mo in range(2):
                                ii = t.matmul(
                                    PS[:, (ic % 2) * 1024 + mo * 512:
                                       (ic % 2) * 1024 + (mo + 1) * 512],
                                    kmh[:, sl * 4096 + kc * 128: sl * 4096 + (kc + 1) * 128],
                                    qmh[:, sl * 2048 + qb * 1024 + mo * 512:
                                        sl * 2048 + qb * 1024 + (mo + 1) * 512],
                                    start=True, stop=True)
                            s_smm.inc(ii)

                        def emit_av(kc):
                            nonlocal ii_o
                            ic = it2 * 32 + kc
                            t.wait_ge(s_exp.h, ic + 1)
                            for mo in range(2):
                                ii_o = t.matmul(
                                    PS[:, 2048 + mo * 512: 2048 + (mo + 1) * 512],
                                    vh[:, sl * 4096 + kc * 128: sl * 4096 + (kc + 1) * 128],
                                    pb[:, (ic % 2) * 1024 + mo * 512:
                                       (ic % 2) * 1024 + (mo + 1) * 512],
                                    start=(kc == 0), stop=(kc == 31))

                        emit_score(0)
                        for kc in range(1, 32):
                            emit_score(kc)
                            emit_av(kc - 1)
                        emit_av(31)
                        s_omm.inc(ii_o)
                        t.wait_ge(s_dbf.h, it2 + 1)
                        t.matmul(PS[0:1, 0:512], ones_bf[:, :],
                                 dacc[:, 0:512], start=True, stop=True)
                        s_dmm.inc(t.matmul(PS[0:1, 512:1024], ones_bf[:, :],
                                           dacc[:, 512:1024], start=True, stop=True))
                        t.wait_ge(s_rc.h, it2 + 1)
                        t.matmul(PS[:, 1024:1536], ones_row[:, :],
                                 recbf[:, 0:512], start=True, stop=True)
                        s_rmm.inc(t.matmul(PS[:, 1536:2048], ones_row[:, :],
                                           recbf[:, 512:1024], start=True, stop=True))

        @ph.block.scalar
        def _(s):
            for h in range(16):
                for qb in range(2):
                    for m in range(2):
                        it2 = its(h, qb, m)
                        for kc in range(32):
                            ic = it2 * 32 + kc
                            s.wait_ge(s_smm.h, ic + 1)
                            if ic >= 2:
                                s.wait_ge(s_dac.h, ic - 1)
                            s_exp.inc(s.activation(
                                pb[:, (ic % 2) * 1024:(ic % 2 + 1) * 1024],
                                PS[:, (ic % 2) * 1024:(ic % 2 + 1) * 1024],
                                AF.Exp, scale=INVSQ_HD))
                        s.wait_ge(s_dmm.h, it2 + 1)
                        s_dsm.inc(s.copy(dsum[:, :], PS[0:1, 0:1024]))
                        s.wait_ge(s_rmm.h, it2 + 1)
                        if it2 >= 1:
                            s.wait_ge(s_rsd.h, it2)
                        s_rsbs.inc(s.copy(rsb[:, :], PS[:, 1024:2048]))

        @ph.block.vector
        def _(v):
            for h in range(16):
                sl = h % 2
                for g4 in range(8):
                    G = h * 8 + g4
                    v.wait_ge(s_vtp.h, G + 1)
                    if h >= 2 and g4 == 0:
                        v.wait_ge(s_omm.h, (h - 1) * 4)
                    ii = None
                    for j in range(4):
                        kc = g4 * 4 + j
                        ii = v.tensor_copy(
                            vh[:, sl * 4096 + kc * 128: sl * 4096 + (kc + 1) * 128],
                            psbf[:, (G % 2) * 1024 + j * 128:
                                 (G % 2) * 1024 + (j + 1) * 128])
                    s_vev2.inc(ii)
                for qb in range(2):
                    for m in range(2):
                        it2 = its(h, qb, m)
                        for kc in range(32):
                            ic = it2 * 32 + kc
                            v.wait_ge(s_exp.h, ic + 1)
                            if kc == 0:
                                if it2 >= 1:
                                    v.wait_ge(s_dmm.h, it2)
                                edge(v, vseq, v.tensor_copy(
                                    dacc[:, :], pb[:, (ic % 2) * 1024:(ic % 2 + 1) * 1024]))
                            else:
                                edge(v, vseq, v.tensor_tensor(
                                    dacc[:, :], dacc[:, :],
                                    pb[:, (ic % 2) * 1024:(ic % 2 + 1) * 1024], ALU.add))
                            v.sem_inc(s_dac.h, 1)
                            s_dac.n += 1
                        v.sem_inc(s_dbf.h, 1)
                        s_dbf.n += 1
                        v.wait_ge(s_dsm.h, it2 + 1)
                        if it2 >= 1:
                            v.wait_ge(s_rmm.h, it2)
                        edge(v, vseq, v.reciprocal(rec[:, :], dsum[:, :]))
                        s_rc.inc(v.tensor_copy(recbf[:, :], rec[:, :]))
                        v.wait_ge(s_rsbs.h, it2 + 1)
                        v.wait_ge(s_omm.h, it2 + 1)
                        if m == 0:
                            edge(v, vseq, v.tensor_tensor(out1[:, :], PS[:, 2048:3072],
                                                          rsb[:, :], ALU.mult))
                            v.sem_inc(s_pod.h, 1)
                            s_pod.n += 1
                            v.sem_inc(s_rsd.h, 1)
                            s_rsd.n += 1
                        else:
                            itq = h * 2 + qb
                            edge(v, vseq, v.tensor_tensor(t2b[:, :], PS[:, 2048:3072],
                                                          rsb[:, :], ALU.mult))
                            v.sem_inc(s_pod.h, 1)
                            s_pod.n += 1
                            v.sem_inc(s_rsd.h, 1)
                            s_rsd.n += 1
                            if itq >= 2:
                                v.wait_ge(s_ast[itq % 2].h, (itq // 2) * 16)
                            edge(v, vseq, v.scalar_tensor_tensor(
                                atb[:, (itq % 2) * 1024:(itq % 2 + 1) * 1024],
                                t2b[:, :], lam[:, h:h + 1], out1[:, :],
                                ALU.mult, ALU.add))
                            v.sem_inc(s_nrm.h, 1)
                            s_nrm.n += 1

        st3.close()
        owt = ph.sbuf("owt", [128, 16 * 2048], BF16)
        ownat = ph.sbuf("ownat", [128, 2 * 2048], BF16)
        atc = ph.sbuf("atcC", [128, 2 * 2048], BF16)
        fbuf = ph.sbuf("fbufC", [128, 2 * 2048], F32)
        lnf = ph.sbuf("lnfC", [128, 2048], F32)
        xo = ph.sbuf("xoC", [128, 2 * 2048], F32)
        obuf = ph.sbuf("obufC", [128, 2 * 2048], F32)
        fb6 = ph.sbuf("fb6C", [128, 24], F32)
        fmv = ph.sbuf("fmvC", [128, 2], F32)
        frs = ph.sbuf("frsC", [128, 1], F32)
        fnm = ph.sbuf("fnmC", [128, 1], F32)

        s_ownl = [ph.sem("ownl0"), ph.sem("ownl1")]
        s_owtp = ph.sem("owtp")
        s_owev = ph.sem("owev")
        s_atc = [ph.sem("atcC0"), ph.sem("atcC1")]
        s_xoC = [ph.sem("xoC0"), ph.sem("xoC1")]
        s_fmm = ph.sem("fmmC")
        s_fev = ph.sem("fevC")
        s_fbn = ph.sem("fbnC")
        s_fmv = ph.sem("fmvsC")
        s_frs = ph.sem("frssC")
        s_fnm = ph.sem("fnmsC")
        s_fap = ph.sem("fapC")
        s_res = ph.sem("resC")
        s_ost = [ph.sem("ostC0"), ph.sem("ostC1")]

        @ph.block.gpsimd
        def _(g):
            g.wait_ge(s_ast[0].h, 16 * 16)
            g.wait_ge(s_ast[1].h, 16 * 16)
            for dr in range(16):
                sl = dr % 2
                if dr >= 2:
                    g.wait_ge(s_owtp.h, 4 * (dr - 1))
                s_ownl[sl].inc(g.dma_start(
                    out=ownat[:, sl * 2048:(sl + 1) * 2048],
                    in_=io["ow"][dr * 128:(dr + 1) * 128, :]), 16)

        @ph.block.tensor
        def _(t):
            for dr in range(16):
                sl = dr % 2
                t.wait_ge(s_ownl[sl].h, (dr // 2 + 1) * 16)
                for g4 in range(4):
                    G = dr * 4 + g4
                    if G >= 2:
                        t.wait_ge(s_owev.h, G - 1)
                    ii = None
                    for j in range(4):
                        hc = g4 * 4 + j
                        ii = t.transpose(psbf[:, (G % 2) * 1024 + j * 128:
                                              (G % 2) * 1024 + (j + 1) * 128],
                                         ownat[:, sl * 2048 + hc * 128:
                                               sl * 2048 + (hc + 1) * 128],
                                         ident[:, :])
                    s_owtp.inc(ii)
            for tt in range(16):
                slt = tt % 2
                t.wait_ge(s_atc[slt].h, (tt // 2 + 1) * 16)
                if tt == 0:
                    t.wait_ge(s_owev.h, 64)
                if tt >= 1:
                    t.wait_ge(s_fev.h, tt)
                ii = None
                for hc in range(16):
                    for dc in range(4):
                        ii = t.matmul(PS[:, dc * 512:(dc + 1) * 512],
                                      atc[:, slt * 2048 + hc * 128: slt * 2048 + (hc + 1) * 128],
                                      owt[:, hc * 2048 + dc * 512: hc * 2048 + (dc + 1) * 512],
                                      start=(hc == 0), stop=(hc == 15))
                s_fmm.inc(ii)

        @ph.block.vector
        def _(v):
            for dr in range(16):
                sl = dr % 2
                for g4 in range(4):
                    G = dr * 4 + g4
                    v.wait_ge(s_owtp.h, G + 1)
                    ii = None
                    for j in range(4):
                        hc = g4 * 4 + j
                        ii = v.tensor_copy(
                            owt[:, hc * 2048 + dr * 128: hc * 2048 + (dr + 1) * 128],
                            psbf[:, (G % 2) * 1024 + j * 128:
                                 (G % 2) * 1024 + (j + 1) * 128])
                    s_owev.inc(ii)
            for tt in range(16):
                slt = tt % 2
                v.wait_ge(s_fev.h, tt + 1)
                ii = None
                for j in range(4):
                    ii = v.bn_stats(fb6[:, j * 6:(j + 1) * 6],
                                    fbuf[:, slt * 2048 + j * 512: slt * 2048 + (j + 1) * 512])
                s_fbn.inc(ii)
                v.wait_ge(s_fbn.h, tt + 1)
                s_fmv.inc(v.bn_aggr(fmv[:, :], fb6[:, :]))
                v.wait_ge(s_fmv.h, tt + 1)
                v.wait_ge(s_frs.h, tt + 1)
                s_fnm.inc(v.tensor_scalar(fnm[:, :], fmv[:, 0:1], frs[:, :], -1.0,
                                          ALU.mult, ALU.mult))
                v.wait_ge(s_fap.h, tt + 1)
                v.wait_ge(s_xoC[slt].h, (tt // 2 + 1) * 16)
                if tt >= 2:
                    v.wait_ge(s_ost[slt].h, (tt // 2) * 16)
                s_res.inc(v.tensor_tensor(obuf[:, slt * 2048:(slt + 1) * 2048],
                                          lnf[:, :], xo[:, slt * 2048:(slt + 1) * 2048],
                                          ALU.add))

        @ph.block.scalar
        def _(s):
            for tt in range(16):
                slt = tt % 2
                s.wait_ge(s_fmm.h, tt + 1)
                if tt >= 2:
                    s.wait_ge(s_fbn.h, tt - 1)
                    s.wait_ge(s_fap.h, tt - 1)
                s_fev.inc(s.copy(fbuf[:, slt * 2048:(slt + 1) * 2048], PS[:, 0:2048]))
                s.wait_ge(s_fmv.h, tt + 1)
                edge(s, sseq, s.activation(frs[:, :], fmv[:, 1:2], AF.Ln, bias=eps[:, :]))
                s_frs.inc(s.activation(frs[:, :], frs[:, :], AF.Exp, scale=-0.5))
                s.wait_ge(s_frs.h, tt + 1)
                s.wait_ge(s_fnm.h, tt + 1)
                if tt >= 1:
                    s.wait_ge(s_res.h, tt)
                s_fap.inc(s.activation(lnf[:, :], fbuf[:, slt * 2048:(slt + 1) * 2048],
                                       AF.Identity, scale=frs[:, :], bias=fnm[:, :]))

        @ph.block.sync
        def _(sync):
            sync.wait_ge(s_ast[0].h, 16 * 16)
            sync.wait_ge(s_ast[1].h, 16 * 16)
            for tt in range(16):
                slt = tt % 2
                if tt >= 2:
                    sync.wait_ge(s_fmm.h, tt - 1)
                s_atc[slt].inc(sync.dma_start(
                    out=atc[:, slt * 2048:(slt + 1) * 2048]
                    .rearrange("p (hc t) -> p hc t", t=128),
                    in_=io["at3c"][:, tt].rearrange("hc p t -> p hc t")), 16)
                if tt >= 2:
                    sync.wait_ge(s_res.h, tt - 1)
                s_xoC[slt].inc(sync.dma_start(
                    out=xo[:, slt * 2048:(slt + 1) * 2048],
                    in_=io["x2"][tt * 128:(tt + 1) * 128, :]), 16)
                sync.wait_ge(s_res.h, tt + 1)
                s_ost[slt].inc(sync.dma_start(
                    out=io["x3"][tt * 128:(tt + 1) * 128, :],
                    in_=obuf[:, slt * 2048:(slt + 1) * 2048]), 16)
            sync.wait_ge(s_ost[0].h, 8 * 16)
            sync.wait_ge(s_ost[1].h, 8 * 16)
    return nc



def build_phase_d(nc, io, use_cc=True):
    pairs = [[0, 1], [2, 3], [4, 5], [6, 7]]
    with ExitStack() as stack:
        block = stack.enter_context(nc.Block())
        ph = Phase(nc, stack, block, "Dp")

        ident = ph.sbuf("ident", [128, 128], BF16)
        eps = ph.sbuf("eps", [128, 1], F32)
        ones_f = ph.sbuf("onesf", [128, 1], F32)
        onesr_f = ph.sbuf("onesrf", [1, 128], F32)
        psbf = ph.psum("psbf", [128, 2048], BF16)
        PS = ph.psum("ps", [128, 3072], F32)
        vseq = ph.sem("vseq")
        sseq = ph.sem("sseq")
        s_pre = ph.sem("pre")

        @ph.block.sync
        def _(sync):
            s_pre.inc(sync.dma_start(out=ident[:, :], in_=io["ident"][:, :]), 16)

        @ph.block.vector
        def _(v):
            v.memset(eps[:, :], 1e-5)
            v.memset(ones_f[:, :], 1.0)
            v.memset(onesr_f[:, :], 1.0)
            v.wait_ge(s_pre.h, 16)

        sr = ph.sbuf("sr", [128, 16], F32)
        s2 = ph.sbuf("s2", [128, 1], F32)
        gsc = ph.sbuf("gsc", [128, 512], BF16)
        simb = ph.sbuf("simb", [128, 16 * 32], F32)
        sqs = ph.sbuf("sqs", [128, 16 * 32], F32)
        smallf = ph.sbuf("smallf", [1, 64], F32)
        small2 = ph.sbuf("small2", [1, 40], F32)
        lgrow = ph.sbuf("lgrow", [1, 32], F32)
        lgc = ph.sbuf("lgc", [32, 1], F32)
        lrow = ph.sbuf("lrow", [32, 32], F32)
        m1 = ph.sbuf("m1", [32, 32], F32)
        e2 = ph.sbuf("e2", [32, 32], F32)
        ltm = ph.sbuf("ltm", [32, 32], F32)
        rankc = ph.sbuf("rankc", [32, 1], F32)
        fbc = ph.sbuf("fbc", [32, 1], F32)
        maskc = ph.sbuf("maskc", [32, 1], F32)
        gatedc = ph.sbuf("gatedc", [32, 1], F32)
        infb = ph.sbuf("infb", [32, 1], F32)
        zc = ph.sbuf("zc", [32, 1], F32)
        ezc = ph.sbuf("ezc", [32, 1], F32)
        rzb = ph.sbuf("rzb", [32, 1], F32)
        routc = ph.sbuf("routc", [32, 1], F32)
        gtsb = ph.sbuf("gtsb", [1, 32], F32)
        sgt = ph.sbuf("sgt", [1, 32], F32)
        rexp = ph.sbuf("rexp", [128, 16], F32)
        ir2 = ph.sbuf("ir2", [128, 64], BF16)
        pat = ph.sbuf("pat", [128, 16 * 64], BF16)
        selb = ph.sbuf("selb", [32, 2048], F32)

        st0 = ExitStack()
        stLN = ExitStack()
        xhat = ph.sbuf("xhatD", [128, 16 * 2048], BF16, st0)

        def dstd(ci, tc):
            return xhat[:, ci * 2048 + tc * 128: ci * 2048 + (tc + 1) * 128]

        def dstgd(g, tc):
            return xhat[:, :].rearrange("p (ci t) -> p ci t", t=2048)[
                :, g * 4:(g + 1) * 4, tc * 128:(tc + 1) * 128]

        lns = emit_ln_xhat(ph, src_hbm=io["x3"], n_tc=16, dst_tiles=dstd,
                           ident=ident, eps=eps, psbf=psbf, name="lnd",
                           bufstack=stLN, dst_group=dstgd)
        s_ev = lns["ev"]
        EV_ALL = s_ev.n
        stLN.close()

        s_srp = ph.sem("srp")
        s_sst = ph.sem("sst")
        s_cc = ph.sem("cc")
        s_srl = ph.sem("srl")

        @ph.block.vector
        def _(v):
            v.wait_ge(s_ev.h, EV_ALL)
            for ci in range(16):
                edge(v, vseq, v.reduce_sum(sr[:, ci:ci + 1],
                                           xhat[:, ci * 2048:(ci + 1) * 2048],
                                           mybir.AxisListType.X))
            v.sem_inc(s_srp.h, 1)
            s_srp.n += 1

        @ph.block.sync
        def _(sync):
            sync.wait_ge(s_srp.h, 1)
            s_sst.inc(sync.dma_start(out=io["srp"][:, :], in_=sr[:, :]), 16)

        @ph.block.gpsimd
        def _(g):
            g.wait_ge(s_sst.h, 16)
            if use_cc:
                s_cc.inc(g.collective_compute(
                    "AllReduce", ALU.add, replica_groups=pairs,
                    ins=[io["srp_t"].ap().opt()], outs=[io["srg_t"].ap().opt()]))
            else:
                g.sem_inc(s_cc.h, 1)
                s_cc.n += 1

        @ph.block.sync
        def _(sync):
            sync.wait_ge(s_cc.h, 1)
            s_srl.inc(sync.dma_start(out=sr[:, :], in_=io["srg"][:, :]), 16)
            s_srl.inc(sync.dma_start(
                out=simb[:, :].rearrange("p (ci e) -> p ci e", e=32),
                in_=io["sim"].rearrange("(ci p) e -> p ci e", p=128)), 16)
            s_srl.inc(sync.dma_start(out=gtsb[:, :], in_=io["gates"][:, :]), 16)
            s_srl.inc(sync.dma_start(out=ltm[:, :], in_=io["ltm"][:, :]), 16)
            s_srl.inc(sync.dma_start(out=ir2[:, :], in_=io["ir2"][:, :]), 16)
            s_srl.inc(sync.dma_start(out=selb[:, :], in_=io["selm"][:, :]), 16)

        wnat = ph.sbuf("wnatD", [128, 2 * 2048], BF16, st0)
        wtb = ph.sbuf("wtbD", [128, 2 * 2048], BF16, st0)
        vat = ph.sbuf("vat", [128, 16 * 2048], BF16, st0)

        s_wn = [ph.sem("wnD0"), ph.sem("wnD1")]
        s_wtp = ph.sem("wtpD")
        s_wev = ph.sem("wevD")
        s_vmm = ph.sem("vmmD")
        s_vev = ph.sem("vevD")

        @ph.block.gpsimd
        def _(g):
            g.wait_ge(s_ev.h, EV_ALL)
            for ut in range(16):
                sl = ut % 2
                if ut >= 2:
                    g.wait_ge(s_wtp.h, 4 * (ut - 1))
                s_wn[sl].inc(g.dma_start(
                    out=wnat[:, sl * 2048:(sl + 1) * 2048],
                    in_=io["wv"][ut * 128:(ut + 1) * 128, :]), 16)

        @ph.block.tensor
        def _(t):
            t.wait_ge(s_ev.h, EV_ALL)
            for ut in range(16):
                sl = ut % 2
                t.wait_ge(s_wn[sl].h, (ut // 2 + 1) * 16)
                for g4 in range(4):
                    G = ut * 4 + g4
                    if G >= 2:
                        t.wait_ge(s_wev.h, G - 1)
                    ii = None
                    for j in range(4):
                        ci = g4 * 4 + j
                        ii = t.transpose(psbf[:, (G % 2) * 1024 + j * 128:
                                              (G % 2) * 1024 + (j + 1) * 128],
                                         wnat[:, sl * 2048 + ci * 128:
                                              sl * 2048 + (ci + 1) * 128],
                                         ident[:, :])
                    s_wtp.inc(ii)
                t.wait_ge(s_wev.h, 4 * (ut + 1))
                if ut >= 1:
                    t.wait_ge(s_vev.h, ut)
                ii = None
                for ci in range(16):
                    for mo in range(4):
                        ii = t.matmul(PS[:, mo * 512:(mo + 1) * 512],
                                      wtb[:, sl * 2048 + ci * 128: sl * 2048 + (ci + 1) * 128],
                                      xhat[:, ci * 2048 + mo * 512: ci * 2048 + (mo + 1) * 512],
                                      start=(ci == 0), stop=(ci == 15))
                s_vmm.inc(ii)

        @ph.block.vector
        def _(v):
            for ut in range(16):
                sl = ut % 2
                for g4 in range(4):
                    G = ut * 4 + g4
                    v.wait_ge(s_wtp.h, G + 1)
                    if ut >= 2 and g4 == 0:
                        v.wait_ge(s_vmm.h, ut - 1)
                    ii = None
                    for j in range(4):
                        ci = g4 * 4 + j
                        ii = v.tensor_copy(
                            wtb[:, sl * 2048 + ci * 128: sl * 2048 + (ci + 1) * 128],
                            psbf[:, (G % 2) * 1024 + j * 128:
                                 (G % 2) * 1024 + (j + 1) * 128])
                    s_wev.inc(ii)
                v.wait_ge(s_vmm.h, ut + 1)
                s_vev.inc(v.tensor_copy(vat[:, ut * 2048:(ut + 1) * 2048], PS[:, 0:2048]))

        s_g1 = ph.sem("g1")
        s_g2 = ph.sem("g2")
        s_g3 = ph.sem("g3")
        s_g4 = ph.sem("g4")
        s_lgst = ph.sem("lgst")
        s_lgld = ph.sem("lgld")
        s_g5 = ph.sem("g5")
        s_g6 = ph.sem("g6")
        s_g7 = ph.sem("g7")
        s_g8 = ph.sem("g8")
        s_rex = ph.sem("rex")
        s_pat = ph.sem("pat")

        @ph.block.scalar
        def _(s):
            s.wait_ge(s_srl.h, 6 * 16)
            edge(s, sseq, s.activation(gsc[:, 0:16], sr[:, :], AF.Square,
                                       accum_out=s2[:, :]))
            s_g1.inc(s.activation(sqs[:, :], simb[:, :], AF.Square))

        @ph.block.tensor
        def _(t):
            t.wait_ge(s_g1.h, 1)
            t.matmul(PS[0:1, 2048:2049], ones_f[:, :], s2[:, :], start=True, stop=True)
            for ci in range(16):
                t.matmul(PS[0:1, 2049:2081], sr[:, ci:ci + 1],
                         simb[:, ci * 32:(ci + 1) * 32],
                         start=(ci == 0), stop=(ci == 15))
            ii = None
            for ci in range(16):
                ii = t.matmul(PS[0:1, 2081:2113], ones_f[:, :],
                              sqs[:, ci * 32:(ci + 1) * 32],
                              start=(ci == 0), stop=(ci == 15))
            s_g2.inc(ii)

        @ph.block.scalar
        def _(s):
            s.wait_ge(s_g2.h, 1)
            edge(s, sseq, s.copy(small2[0:1, 0:1], PS[0:1, 2048:2049]))
            edge(s, sseq, s.copy(smallf[0:1, 0:32], PS[0:1, 2049:2081]))
            edge(s, sseq, s.copy(smallf[0:1, 32:64], PS[0:1, 2081:2113]))
            edge(s, sseq, s.activation(small2[0:1, 1:2], small2[0:1, 0:1], AF.Ln))
            edge(s, sseq, s.activation(small2[0:1, 1:2], small2[0:1, 1:2], AF.Exp, scale=-0.5))
            edge(s, sseq, s.activation(small2[0:1, 2:34], smallf[0:1, 32:64], AF.Ln))
            edge(s, sseq, s.activation(small2[0:1, 2:34], small2[0:1, 2:34], AF.Exp, scale=-0.5))
            s_g3.inc(s.activation(sgt[0:1, :], gtsb[0:1, :], AF.Sigmoid))

        @ph.block.vector
        def _(v):
            v.wait_ge(s_g3.h, 1)
            edge(v, vseq, v.tensor_scalar(lgrow[0:1, :], smallf[0:1, 0:32],
                                          small2[0:1, 1:2], None, ALU.mult))
            edge(v, vseq, v.tensor_tensor(lgrow[0:1, :], lgrow[0:1, :],
                                          small2[0:1, 2:34], ALU.mult))
            edge(v, vseq, v.tensor_tensor(lgrow[0:1, :], lgrow[0:1, :],
                                          sgt[0:1, :], ALU.subtract))
            v.sem_inc(s_g4.h, 1)
            s_g4.n += 1

        @ph.block.sync
        def _(sync):
            sync.wait_ge(s_g4.h, 1)
            s_lgst.inc(sync.dma_start(out=io["lgd"][:, :], in_=lgrow[0:1, :]), 16)
            sync.wait_ge(s_lgst.h, 16)
            s_lgld.inc(sync.dma_start(out=lgc[:, :],
                                      in_=io["lgd"].rearrange("one e -> e one")), 16)

        @ph.block.tensor
        def _(t):
            t.wait_ge(s_g4.h, 1)
            s_g5.inc(t.matmul(PS[0:32, 2048:2080], onesr_f[0:1, 0:32],
                              lgrow[0:1, :], start=True, stop=True))

        @ph.block.scalar
        def _(s):
            s.wait_ge(s_g5.h, 1)
            s_g6.inc(s.copy(lrow[:, :], PS[0:32, 2048:2080]))

        @ph.block.vector
        def _(v):
            v.wait_ge(s_g6.h, 1)
            v.wait_ge(s_lgld.h, 16)
            edge(v, vseq, v.tensor_scalar(m1[:, :], lrow[:, :], lgc[:, :], None, ALU.is_gt))
            edge(v, vseq, v.tensor_scalar(e2[:, :], lrow[:, :], lgc[:, :], None, ALU.is_equal))
            edge(v, vseq, v.tensor_tensor(e2[:, :], e2[:, :], ltm[:, :], ALU.mult))
            edge(v, vseq, v.tensor_tensor(m1[:, :], m1[:, :], e2[:, :], ALU.add))
            edge(v, vseq, v.reduce_sum(rankc[:, :], m1[:, :], mybir.AxisListType.X))
            edge(v, vseq, v.tensor_scalar(fbc[:, :], rankc[:, :], 16.0, None, ALU.is_lt))
            edge(v, vseq, v.tensor_scalar(gatedc[:, :], lgc[:, :], 0.0, None, ALU.max))
            edge(v, vseq, v.tensor_scalar(maskc[:, :], lgc[:, :], 0.0, None, ALU.is_gt))
            v.sem_inc(s_g7.h, 1)
            s_g7.n += 1

        @ph.block.tensor
        def _(t):
            t.wait_ge(s_g7.h, 1)
            s_g8.inc(t.matmul(PS[0:1, 2080:2081], ones_f[0:32, :], maskc[:, :],
                              start=True, stop=True))

        @ph.block.scalar
        def _(s):
            s.wait_ge(s_g8.h, 1)
            edge(s, sseq, s.copy(small2[0:1, 34:35], PS[0:1, 2080:2081]))
            s.sem_inc(s_g8.h, 1)
            s_g8.n += 1

        @ph.block.vector
        def _(v):
            v.wait_ge(s_g8.h, 2)
            edge(v, vseq, v.tensor_scalar(small2[0:1, 35:36], small2[0:1, 34:35],
                                          0.0, None, ALU.is_equal))
            v.sem_inc(s_g8.h, 1)
            s_g8.n += 1

        @ph.block.tensor
        def _(t):
            t.wait_ge(s_g8.h, 3)
            s_g8.inc(t.matmul(PS[0:32, 2081:2082], onesr_f[0:1, 0:32],
                              small2[0:1, 35:36], start=True, stop=True))

        @ph.block.scalar
        def _(s):
            s.wait_ge(s_g8.h, 4)
            s_g8.inc(s.copy(infb[:, :], PS[0:32, 2081:2082]))

        @ph.block.vector
        def _(v):
            v.wait_ge(s_g8.h, 5)
            edge(v, vseq, v.tensor_tensor(zc[:, :], infb[:, :], fbc[:, :], ALU.mult))
            edge(v, vseq, v.tensor_tensor(zc[:, :], maskc[:, :], zc[:, :], ALU.add))
            edge(v, vseq, v.tensor_scalar(zc[:, :], zc[:, :], 1.0, None, ALU.min))
            edge(v, vseq, v.tensor_tensor(ezc[:, :], gatedc[:, :], zc[:, :], ALU.mult))
            edge(v, vseq, v.tensor_scalar(zc[:, :], zc[:, :], 3.4e38, -3.4e38,
                                          ALU.mult, ALU.add))
            edge(v, vseq, v.tensor_tensor(zc[:, :], ezc[:, :], zc[:, :], ALU.add))
            v.sem_inc(s_g8.h, 1)
            s_g8.n += 1

        @ph.block.scalar
        def _(s):
            s.wait_ge(s_g8.h, 6)
            s_g8.inc(s.activation(ezc[:, :], zc[:, :], AF.Exp))

        @ph.block.tensor
        def _(t):
            t.wait_ge(s_g8.h, 7)
            s_g8.inc(t.matmul(PS[0:1, 2082:2083], ones_f[0:32, :], ezc[:, :],
                              start=True, stop=True))

        @ph.block.scalar
        def _(s):
            s.wait_ge(s_g8.h, 8)
            s_g8.inc(s.copy(small2[0:1, 36:37], PS[0:1, 2082:2083]))

        @ph.block.vector
        def _(v):
            v.wait_ge(s_g8.h, 9)
            edge(v, vseq, v.reciprocal(small2[0:1, 37:38], small2[0:1, 36:37]))
            v.sem_inc(s_g8.h, 1)
            s_g8.n += 1

        @ph.block.tensor
        def _(t):
            t.wait_ge(s_g8.h, 10)
            s_g8.inc(t.matmul(PS[0:32, 2083:2084], onesr_f[0:1, 0:32],
                              small2[0:1, 37:38], start=True, stop=True))

        @ph.block.scalar
        def _(s):
            s.wait_ge(s_g8.h, 11)
            s_g8.inc(s.copy(rzb[:, :], PS[0:32, 2083:2084]))

        @ph.block.vector
        def _(v):
            v.wait_ge(s_g8.h, 12)
            edge(v, vseq, v.tensor_tensor(routc[:, :], ezc[:, :], rzb[:, :], ALU.mult))
            v.sem_inc(s_g8.h, 1)
            s_g8.n += 1

        @ph.block.tensor
        def _(t):
            t.wait_ge(s_g8.h, 13)
            ii = None
            for ci in range(16):
                ii = t.matmul(PS[:, 2084 + ci:2084 + ci + 1], selb[:, ci * 128:(ci + 1) * 128],
                              routc[:, :], start=True, stop=True)
            s_rex.inc(ii)

        @ph.block.scalar
        def _(s):
            s.wait_ge(s_rex.h, 1)
            s_rex.inc(s.copy(rexp[:, :], PS[:, 2084:2100]))

        @ph.block.vector
        def _(v):
            v.wait_ge(s_rex.h, 2)
            ii = None
            for ci in range(16):
                ii = v.tensor_scalar(pat[:, ci * 64:(ci + 1) * 64], ir2[:, :],
                                     rexp[:, ci:ci + 1], None, ALU.mult)
            s_pat.inc(ii)

        st0.close()
        owb = ph.sbuf("owbD", [128, 2 * 2048], BF16)
        wod = ph.sbuf("wod", [64, 2048], BF16)
        comb = ph.sbuf("comb", [64, 2048], BF16)
        fbuf = ph.sbuf("fbufD", [128, 2 * 2048], F32)
        lnf = ph.sbuf("lnfD", [128, 2048], F32)
        xo = ph.sbuf("xoD", [128, 2 * 2048], F32)
        obuf = ph.sbuf("obufD", [128, 2 * 2048], F32)
        fb6 = ph.sbuf("fb6D", [128, 24], F32)
        fmv = ph.sbuf("fmvD", [128, 2], F32)
        frs = ph.sbuf("frsD", [128, 1], F32)
        fnm = ph.sbuf("fnmD", [128, 1], F32)

        s_owl = [ph.sem("owl0"), ph.sem("owl1")]
        s_wodm = ph.sem("wodm")
        s_wode = ph.sem("wode")
        s_cbm = ph.sem("cbm")
        s_cbe = ph.sem("cbe")
        s_fmm = ph.sem("fmmD")
        s_fev = ph.sem("fevD")
        s_fbn = ph.sem("fbnD")
        s_fmv = ph.sem("fmvsD")
        s_frs = ph.sem("frssD")
        s_fnm = ph.sem("fnmsD")
        s_fap = ph.sem("fapD")
        s_res = ph.sem("resD")
        s_ost = [ph.sem("ostD0"), ph.sem("ostD1")]
        s_xoD = [ph.sem("xoD0"), ph.sem("xoD1")]

        @ph.block.gpsimd
        def _(g):
            g.wait_ge(s_vmm.h, 16)
            for ci in range(16):
                sl = ci % 2
                if ci >= 2:
                    g.wait_ge(s_wodm.h, ci - 1)
                s_owl[sl].inc(g.dma_start(
                    out=owb[:, sl * 2048:(sl + 1) * 2048],
                    in_=io["ow"][ci * 128:(ci + 1) * 128, :]), 16)

        @ph.block.tensor
        def _(t):
            t.wait_ge(s_pat.h, 1)
            t.wait_ge(s_vev.h, 16)
            for ci in range(16):
                sl = ci % 2
                t.wait_ge(s_owl[sl].h, (ci // 2 + 1) * 16)
                ii = None
                for mo in range(4):
                    ii = t.matmul(PS[0:64, mo * 512:(mo + 1) * 512],
                                  pat[:, ci * 64:(ci + 1) * 64],
                                  owb[:, sl * 2048 + mo * 512: sl * 2048 + (mo + 1) * 512],
                                  start=(ci == 0), stop=(ci == 15))
                s_wodm.inc(ii)
            t.wait_ge(s_vev.h, 16)
            t.wait_ge(s_wode.h, 1)
            ii = None
            for ci in range(16):
                for mo in range(4):
                    ii = t.matmul(PS[0:64, mo * 512:(mo + 1) * 512],
                                  pat[:, ci * 64:(ci + 1) * 64],
                                  vat[:, ci * 2048 + mo * 512: ci * 2048 + (mo + 1) * 512],
                                  start=(ci == 0), stop=(ci == 15))
            s_cbm.inc(ii)
            t.wait_ge(s_cbe.h, 1)
            for tt in range(16):
                if tt >= 1:
                    t.wait_ge(s_fev.h, tt)
                ii = None
                for dc in range(4):
                    ii = t.matmul(PS[:, dc * 512:(dc + 1) * 512],
                                  comb[0:64, tt * 128:(tt + 1) * 128],
                                  wod[0:64, dc * 512:(dc + 1) * 512],
                                  start=True, stop=True)
                s_fmm.inc(ii)

        @ph.block.vector
        def _(v):
            v.wait_ge(s_wodm.h, 16)
            s_wode.inc(v.tensor_copy(wod[:, :], PS[0:64, 0:2048]))
            v.wait_ge(s_cbm.h, 1)
            s_cbe.inc(v.tensor_copy(comb[:, :], PS[0:64, 0:2048]))
            for tt in range(16):
                slt = tt % 2
                v.wait_ge(s_fev.h, tt + 1)
                ii = None
                for j in range(4):
                    ii = v.bn_stats(fb6[:, j * 6:(j + 1) * 6],
                                    fbuf[:, slt * 2048 + j * 512: slt * 2048 + (j + 1) * 512])
                s_fbn.inc(ii)
                v.wait_ge(s_fbn.h, tt + 1)
                s_fmv.inc(v.bn_aggr(fmv[:, :], fb6[:, :]))
                v.wait_ge(s_fmv.h, tt + 1)
                v.wait_ge(s_frs.h, tt + 1)
                s_fnm.inc(v.tensor_scalar(fnm[:, :], fmv[:, 0:1], frs[:, :], -1.0,
                                          ALU.mult, ALU.mult))
                v.wait_ge(s_fap.h, tt + 1)
                v.wait_ge(s_xoD[slt].h, (tt // 2 + 1) * 16)
                if tt >= 2:
                    v.wait_ge(s_ost[slt].h, (tt // 2) * 16)
                s_res.inc(v.tensor_tensor(obuf[:, slt * 2048:(slt + 1) * 2048],
                                          lnf[:, :], xo[:, slt * 2048:(slt + 1) * 2048],
                                          ALU.add))

        @ph.block.scalar
        def _(s):
            for tt in range(16):
                slt = tt % 2
                s.wait_ge(s_fmm.h, tt + 1)
                if tt >= 2:
                    s.wait_ge(s_fbn.h, tt - 1)
                    s.wait_ge(s_fap.h, tt - 1)
                s_fev.inc(s.copy(fbuf[:, slt * 2048:(slt + 1) * 2048], PS[:, 0:2048]))
                s.wait_ge(s_fmv.h, tt + 1)
                edge(s, sseq, s.activation(frs[:, :], fmv[:, 1:2], AF.Ln, bias=eps[:, :]))
                s_frs.inc(s.activation(frs[:, :], frs[:, :], AF.Exp, scale=-0.5))
                s.wait_ge(s_frs.h, tt + 1)
                s.wait_ge(s_fnm.h, tt + 1)
                if tt >= 1:
                    s.wait_ge(s_res.h, tt)
                s_fap.inc(s.activation(lnf[:, :], fbuf[:, slt * 2048:(slt + 1) * 2048],
                                       AF.Identity, scale=frs[:, :], bias=fnm[:, :]))

        @ph.block.sync
        def _(sync):
            sync.wait_ge(s_vev.h, 16)
            for tt in range(16):
                slt = tt % 2
                if tt >= 2:
                    sync.wait_ge(s_res.h, tt - 1)
                s_xoD[slt].inc(sync.dma_start(
                    out=xo[:, slt * 2048:(slt + 1) * 2048],
                    in_=io["x3"][tt * 128:(tt + 1) * 128, :]), 16)
                sync.wait_ge(s_res.h, tt + 1)
                s_ost[slt].inc(sync.dma_start(
                    out=io["out"][tt * 128:(tt + 1) * 128, :],
                    in_=obuf[:, slt * 2048:(slt + 1) * 2048]), 16)
            sync.wait_ge(s_ost[0].h, 8 * 16)
            sync.wait_ge(s_ost[1].h, 8 * 16)
    return nc



def emit_sem_clear_block(nc):
    with ExitStack() as es:
        handles = []
        for i in range(128):
            try:
                handles.append(es.enter_context(nc.semaphore(f"clr{i}")))
            except Exception:
                break
        with nc.Block() as blk:
            @blk.gpsimd
            def _(g):
                g.dma_reset()
                for h in handles:
                    if h.num in USED_SEM_NUMS:
                        g.sem_clear(h)
    USED_SEM_NUMS.clear()


def build_mega(use_cc=True):
    nc = bass.Bass("TRN2", target_bir_lowering=False)
    F, B = F32, BF16
    d = {}

    def inp(name, shape, dt):
        d[name] = nc.dram_tensor(name, shape, dt, kind="ExternalInput").ap()

    def scr(name, shape, dt):
        t = nc.dram_tensor(name, shape, dt)
        d[name] = t.ap()
        d[name + "_t"] = t

    inp("xin", [3072, 2048], F)
    inp("l0wq", [2048, 2048], F); inp("l0wk", [2048, 512], F)
    inp("l0wv", [2048, 512], F); inp("l0wo", [2048, 2048], F)
    inp("cost", [128, 3072], F); inp("sint", [128, 3072], F)
    inp("l0mask", [4, 12, 128, 512], B)
    inp("ident", [128, 128], B)
    inp("w1", [5632, 2048], F); inp("w2", [5632, 2048], F); inp("w3", [2048, 5632], F)
    for n in ["q1w", "k1w", "q2w", "k2w", "cvw", "cow"]:
        inp(n, [2048, 2048], F)
    inp("lam_b", [128, 16], F)
    inp("dwv", [2048, 2048], F); inp("dow", [2048, 2048], F)
    inp("sim", [2048, 32], F); inp("gates", [1, 32], F)
    inp("selm", [32, 2048], F); inp("ir2", [128, 64], B); inp("ltm", [32, 32], F)
    d["out"] = nc.dram_tensor("out", [2048, 2048], F, kind="ExternalOutput").ap()

    scr("x1", [2048, 2048], F); scr("x2", [2048, 2048], F); scr("x3", [2048, 2048], F)
    scr("qsc", [2048, 2048], B); scr("vsc", [3072, 512], B)
    scr("at3", [16, 16, 128, 128], B)
    scr("h3", [44, 16, 128, 128], B); scr("fsc", [2048, 2048], F)
    for n in ["kvc1", "kvc2", "kvcv", "qsc1", "qsc2"]:
        scr(n, [2048, 2048], B)
    for n in ["kvg1", "kvg2", "kvgv"]:
        scr(n, [4096, 2048], B)
    scr("at3c", [16, 16, 128, 128], B)
    scr("srp", [128, 16], F); scr("srg", [128, 16], F); scr("lgd", [1, 32], F)

    ioA = {"xin": d["xin"], "wq": d["l0wq"], "wk": d["l0wk"], "wv": d["l0wv"],
           "wo": d["l0wo"], "cost": d["cost"], "sint": d["sint"],
           "l0mask": d["l0mask"], "ident": d["ident"], "x1": d["x1"],
           "qsc": d["qsc"], "vsc": d["vsc"], "at3": d["at3"]}
    build_phase_a(nc, ioA)
    emit_sem_clear_block(nc)

    ioB = {"x1": d["x1"], "w1": d["w1"], "w2": d["w2"], "w3": d["w3"],
           "ident": d["ident"], "x2": d["x2"], "h3": d["h3"], "fsc": d["fsc"]}
    build_phase_b(nc, ioB)
    emit_sem_clear_block(nc)

    ioC = {"x2": d["x2"], "q1w": d["q1w"], "k1w": d["k1w"], "q2w": d["q2w"],
           "k2w": d["k2w"], "vw": d["cvw"], "ow": d["cow"], "lam_b": d["lam_b"],
           "ident": d["ident"], "x3": d["x3"], "at3c": d["at3c"]}
    for n in ["kvc1", "kvc2", "kvcv", "kvg1", "kvg2", "kvgv", "qsc1", "qsc2"]:
        ioC[n] = d[n]
        ioC[n + "_t"] = d[n + "_t"]
    build_phase_c_pre(nc, ioC)
    emit_sem_clear_block(nc)
    build_phase_c_post(nc, ioC)
    emit_sem_clear_block(nc)

    ioD = {"x3": d["x3"], "wv": d["dwv"], "ow": d["dow"], "sim": d["sim"],
           "gates": d["gates"], "selm": d["selm"], "ir2": d["ir2"],
           "ltm": d["ltm"], "ident": d["ident"], "out": d["out"],
           "srp": d["srp"], "srp_t": d["srp_t"], "srg": d["srg"],
           "srg_t": d["srg_t"], "lgd": d["lgd"]}
    build_phase_d(nc, ioD, use_cc=use_cc)
    return nc



_CONSTS = None


def host_consts():
    global _CONSTS
    import ml_dtypes
    if _CONSTS is None:
        selm = np.zeros((32, 2048), dtype=np.float32)
        for ci in range(16):
            for p in range(128):
                selm[2 * ci + p // 64, ci * 128 + p] = 1.0
        ir2 = np.concatenate([np.eye(64), np.eye(64)], axis=0).astype(ml_dtypes.bfloat16)
        ltm = (np.arange(32)[None, :] < np.arange(32)[:, None]).astype(np.float32)
        ident = np.eye(128, dtype=ml_dtypes.bfloat16)
        _CONSTS = (selm, ir2, ltm, ident)
    return _CONSTS


def host_core_inputs(inputs, core):
    selm, ir2, ltm, ident = host_consts()
    b, half = core // 2, core % 2
    x = np.asarray(inputs["L0_attn_hidden_states"])
    xin = np.zeros((3072, 2048), dtype=np.float32)
    if half == 0:
        xin[1024:] = x[b, 0:2048]
        pos = np.arange(-1024, 2048)
    else:
        xin[:] = x[b, 1024:4096]
        pos = np.arange(1024, 4096)
    cost, sint = rope_tables(np.maximum(pos, 0))
    f32 = np.float32
    return {
        "xin": xin,
        "l0wq": np.ascontiguousarray(inputs["L0_attn_w_q"], dtype=f32),
        "l0wk": np.ascontiguousarray(inputs["L0_attn_w_k"], dtype=f32),
        "l0wv": np.ascontiguousarray(inputs["L0_attn_w_v"], dtype=f32),
        "l0wo": np.ascontiguousarray(inputs["L0_attn_w_o"], dtype=f32),
        "cost": cost, "sint": sint,
        "l0mask": np.asarray(l0_masks(half)),
        "ident": ident,
        "w1": np.ascontiguousarray(inputs["L0_ffn_w1"], dtype=f32),
        "w2": np.ascontiguousarray(inputs["L0_ffn_w2"], dtype=f32),
        "w3": np.ascontiguousarray(inputs["L0_ffn_w3"], dtype=f32),
        "q1w": np.ascontiguousarray(inputs["L1_attn_q1_w"], dtype=f32),
        "k1w": np.ascontiguousarray(inputs["L1_attn_k1_w"], dtype=f32),
        "q2w": np.ascontiguousarray(inputs["L1_attn_q2_w"], dtype=f32),
        "k2w": np.ascontiguousarray(inputs["L1_attn_k2_w"], dtype=f32),
        "cvw": np.ascontiguousarray(inputs["L1_attn_v_w"], dtype=f32),
        "cow": np.ascontiguousarray(inputs["L1_attn_o_w"], dtype=f32),
        "lam_b": np.ascontiguousarray(np.broadcast_to(
            np.asarray(inputs["L1_attn_lambda_param"], dtype=f32)[None, :], (128, 16))),
        "dwv": np.ascontiguousarray(
            np.asarray(inputs["L1_ffn_w_v"], dtype=f32).reshape(2048, 2048)),
        "dow": np.ascontiguousarray(
            np.asarray(inputs["L1_ffn_o_weights"], dtype=f32).reshape(2048, 2048)),
        "sim": np.ascontiguousarray(inputs["L1_ffn_sim_matrix"], dtype=f32),
        "gates": np.ascontiguousarray(
            np.asarray(inputs["L1_ffn_gates"], dtype=f32).reshape(1, 32)),
        "selm": selm, "ir2": ir2, "ltm": ltm,
    }


def kernel(**inputs):
    from concourse.bass_utils import run_bass_kernel_spmd
    assert inputs["L1_ffn_w_q"].shape == (32, 64, 2048)
    assert inputs["L1_ffn_w_k"].shape == (32, 64, 2048)
    nc = build_mega(use_cc=True)
    in_maps = [host_core_inputs(inputs, c) for c in range(8)]
    res = run_bass_kernel_spmd(nc, in_maps, core_ids=list(range(8)))
    out = np.zeros((4, 4096, 2048), dtype=np.float32)
    for c in range(8):
        b, half = c // 2, c % 2
        out[b, half * 2048:(half + 1) * 2048] = res.results[c]["out"]
    return out
